# version 18
# baseline (speedup 1.0000x reference)
import contextlib
import numpy as np

import concourse.bass as bass
import concourse.mybir as mybir
from concourse.bass_utils import run_bass_kernel_spmd

F32 = mybir.dt.float32
AF = mybir.ActivationFunctionType
ALU = mybir.AluOpType
AX = mybir.AxisListType

STAGE = "full"
NCYC = 3

NTOK, D, H, DH, M = 4096, 1024, 16, 64, 256
NST, TST, NSUB = 8, 512, 4
NRM = float(64.0 ** -0.25)
EPSK = 1e-4
LNEPS = 1e-5
LNEPSK = float(np.log(EPSK))
SIGN = [1, 1, -1, -1, -1, -1, 1, 1]
SIGMA = [4, 5, 6, 7, 0, 1, 2, 3]


class Builder:

    def __init__(self):
        self.ops = []
        self.cnt = dict(T=0, V=0, S=0, G=0, DY=0, DG=0, CC=0)
        self.res = {}
        self.gap_emit = None
        self.s_spacer = None
        self.prev_v_writes = set()
        self._in_gap = False

    def op(self, eng, emit, reads=(), writes=(), ndma=0, cc=False, dup=False):
        if dup:
            self.op(eng, emit, reads=reads, writes=writes, ndma=ndma, cc=cc)
            if eng == "s" and self.s_spacer is not None:
                self._raw_op("s", self.s_spacer, (), ("SSPC",), 0, False)
                self._raw_op("s", self.s_spacer, (), ("SSPC",), 0, False)
            elif eng == "v" and self.gap_emit is not None:
                self._raw_op("v", self.gap_emit, (), ("VDres",), 0, False)
                self._raw_op("v", self.gap_emit, (), ("VDres",), 0, False)
        if eng == "v" and not self._in_gap:
            touched = set(reads) | set(writes)
            if touched & self.prev_v_writes and self.gap_emit is not None:
                self._in_gap = True
                self.op("v", self.gap_emit, reads=(), writes=("VDres",))
                self._in_gap = False
            self.prev_v_writes = set(writes)
        elif eng == "v":
            self.prev_v_writes = set(writes)
        self._raw_op(eng, emit, reads, writes, ndma, cc)

    def _raw_op(self, eng, emit, reads=(), writes=(), ndma=0, cc=False):
        if ndma:
            key, amt = ("DY" if eng == "y" else "DG"), 16 * ndma
        elif cc:
            key, amt = "CC", 1
        else:
            key, amt = dict(t="T", v="V", s="S", g="G", y="DY")[eng], 1
        waits = {}

        def addw(m):
            k, v = m
            if v > waits.get(k, 0):
                waits[k] = v

        for r in reads:
            e = self.res.get(r)
            if e and e["w"] is not None:
                addw(e["w"])
        for r in writes:
            e = self.res.get(r)
            if e:
                if e["w"] is not None:
                    addw(e["w"])
                for m in e["r"]:
                    addw(m)
        self.cnt[key] += amt
        marker = (key, self.cnt[key])
        for r in writes:
            self.res[r] = {"w": marker, "r": []}
        for r in reads:
            self.res.setdefault(r, {"w": None, "r": []})["r"].append(marker)
        self.ops.append((eng, waits, emit, key, amt, tuple(reads), tuple(writes)))


def build_nc(stage):
    nc = bass.Bass("TRN2", target_bir_lowering=False)

    x_h = nc.dram_tensor("x", [NTOK, D], F32, kind="ExternalInput").ap()
    wq_h = nc.dram_tensor("wq", [D, D], F32, kind="ExternalInput").ap()
    wk_h = nc.dram_tensor("wk", [D, D], F32, kind="ExternalInput").ap()
    wv_h = nc.dram_tensor("wv", [D, D], F32, kind="ExternalInput").ap()
    wo_h = nc.dram_tensor("wo", [D, D], F32, kind="ExternalInput").ap()
    bo_h = nc.dram_tensor("bo", [1, D], F32, kind="ExternalInput").ap()
    pj_h = nc.dram_tensor("proj", [M, DH], F32, kind="ExternalInput").ap()
    w1_h = nc.dram_tensor("w1", [D, D], F32, kind="ExternalInput").ap()
    b1_h = nc.dram_tensor("b1", [D], F32, kind="ExternalInput").ap()
    w2_h = nc.dram_tensor("w2", [D, 512], F32, kind="ExternalInput").ap()
    b2_h = nc.dram_tensor("b2", [512], F32, kind="ExternalInput").ap()
    w3_h = nc.dram_tensor("w3", [512, 256], F32, kind="ExternalInput").ap()
    b3_h = nc.dram_tensor("b3", [256], F32, kind="ExternalInput").ap()
    w4_h = nc.dram_tensor("w4", [256, 1], F32, kind="ExternalInput").ap()
    b4_h = nc.dram_tensor("b4", [1, 1], F32, kind="ExternalInput").ap()
    g1_h = nc.dram_tensor("gw1", [D, 512], F32, kind="ExternalInput").ap()
    gb1_h = nc.dram_tensor("gb1", [512], F32, kind="ExternalInput").ap()
    g2_h = nc.dram_tensor("gw2", [512, 256], F32, kind="ExternalInput").ap()
    gb2_h = nc.dram_tensor("gb2", [256], F32, kind="ExternalInput").ap()
    g3_h = nc.dram_tensor("gw3", [256, 1], F32, kind="ExternalInput").ap()
    gb3_h = nc.dram_tensor("gb3", [1, 1], F32, kind="ExternalInput").ap()
    ar_h = nc.dram_tensor("ar", [1, 1], F32, kind="ExternalInput").ap()
    y_h = nc.dram_tensor("y", [NTOK, D], F32, kind="ExternalOutput").ap()
    ydbg_h = nc.dram_tensor("ydbg", [1, 4096], F32, kind="ExternalOutput").ap()
    ydbg2_h = nc.dram_tensor("ydbg2", [128, 16], F32, kind="ExternalOutput").ap()
    x2s_h = nc.dram_tensor("x2scr", [NTOK, D], F32).ap()
    ccin_h = nc.dram_tensor("cc_in", [1, 8], F32).ap()
    ccout_h = nc.dram_tensor("cc_out", [1, 8], F32).ap()

    es = contextlib.ExitStack()
    sb = lambda n, s: es.enter_context(nc.sbuf_tensor(n, s, F32))
    ps = lambda n: es.enter_context(nc.psum_tensor(n, [128, 512], F32))

    BA = sb("BA", [128, 4, 1024])
    BB = sb("BB", [128, 8, 512])
    BC = sb("BC", [128, 4096])
    BD = sb("BD", [128, 8, 512])
    BE = sb("BE", [128, 4096])
    BF = sb("BF", [128, 4096])
    BH = sb("BH", [128, 1536])
    WA = sb("WA", [128, 8, 1024])
    WB = sb("WB", [128, 8, 1024])
    W3S = sb("W3S", [128, 4, 256])
    W4S = sb("W4S", [128, 2, 1])
    GW3 = sb("GW3", [128, 2, 1])
    B1S = sb("B1S", [128, 8])
    B2S = sb("B2S", [128, 4])
    B3S = sb("B3S", [128, 2])
    GB1 = sb("GB1", [128, 4])
    GB2 = sb("GB2", [128, 2])
    IDENT = sb("IDENT", [128, 128])
    PRJCAT = sb("PRJCAT", [128, 512])
    BLK = sb("BLK", [128, 8, 16])
    CTX = sb("CTX", [128, 32 * 66])
    ARENA = sb("ARENA", [1, 1024])
    ROWS = sb("ROWS", [1, 512])
    LN1M = sb("LN1M", [128, 32])
    LN1I = sb("LN1I", [128, 32])
    LN2M = sb("LN2M", [128, 32])
    LN2I = sb("LN2I", [128, 32])
    UMAT = sb("UMAT", [128, 256])
    NSTAB = sb("NSTABm", [128, 16])
    BIASQ = sb("BIASQ", [128, 16])
    NDS = sb("NDS", [16, 512])
    NDT = sb("NDT", [128, 4, 16])
    SS4 = sb("SS4", [128, 4])
    TMP4 = sb("TMP4", [128, 4])
    TMP4B = sb("TMP4B", [128, 4])
    TMP32 = sb("TMP32", [128, 32])
    DEN = sb("DEN", [128, 16])
    DINV = sb("DINV", [128, 16])
    RBD = sb("RBD", [128, 16])
    BCAST = sb("BCAST", [128, 2])
    SVREP = sb("SVREP", [128, 1024])
    BOREP = sb("BOREP", [128, 1024])
    ONESC = sb("ONESC", [128, 8])
    ONES1 = sb("ONES1", [1, 128])
    CSFM = sb("CSFM", [128, 8])
    GH1 = sb("GH1", [128, 4])
    GH2 = sb("GH2", [128, 2])
    PLOAD = sb("PLOAD", [128, 2, 64])
    CONSTS = sb("CONSTS", [128, 4])
    VD = sb("VD", [128, 4])
    SD = sb("SD", [128, 8])
    W4R = sb("W4R", [128, 2, 128])
    GW3R = sb("GW3R", [128, 2, 128])
    ARCOL = sb("ARCOL", [128, 1])
    B4HC = sb("B4HC", [128, 1])
    B3HC = sb("B3HC", [128, 1])
    A2C = sb("A2C", [128, 1])
    CB2C = sb("CB2C", [128, 1])
    T1C = sb("T1C", [128, 1])
    TGC = sb("TGC", [128, 1])
    NEG32 = sb("NEG32", [32, 512])
    P = [ps(f"P{i}") for i in range(8)]
    nc.const_aps.register(F32, LNEPS, CONSTS.ap()[:, 0:1])
    nc.const_aps.register(F32, EPSK, CONSTS.ap()[:, 1:2])
    nc.const_aps.register(F32, LNEPSK, CONSTS.ap()[:, 2:3])
    nc.const_aps.register(F32, 0.1, CONSTS.ap()[:, 3:4])

    d_y = es.enter_context(nc.semaphore("d_y"))
    d_g = es.enter_context(nc.semaphore("d_g"))
    s_t = es.enter_context(nc.semaphore("s_t"))
    s_v = es.enter_context(nc.semaphore("s_v"))
    s_s = es.enter_context(nc.semaphore("s_s"))
    s_g = es.enter_context(nc.semaphore("s_g"))
    s_cc = es.enter_context(nc.semaphore("s_cc"))
    SEM = dict(DY=d_y, DG=d_g, T=s_t, V=s_v, S=s_s, G=s_g, CC=s_cc)

    A_CS, A_GM = 0, 0
    A_TROW, A_CROW = 0, 512
    R_B4, R_GB3, R_B4H, R_B3H = 0, 1, 2, 3
    R_SLOC, R_S2, R_GZ = 4, 8, 9
    R_S8, R_SG, R_BETA, R_BN, R_B2 = 16, 24, 32, 40, 48
    R_AROW, R_A2, R_CB2, R_TG, R_T1 = 56, 64, 72, 80, 88
    R_UR = 128

    bld = Builder()
    OP = bld.op

    bld.gap_emit = lambda v: v.tensor_copy(VD.ap()[:, 0:2], VD.ap()[:, 2:4])
    bld.s_spacer = lambda s: s.copy(SD.ap()[:, 0:2], SD.ap()[:, 2:4])

    def vgap(res="VDres"):
        OP("v", bld.gap_emit, reads=[], writes=[res])

    def sflush():
        OP("s", lambda s: s.activation(SD.ap()[:, 0:4], CONSTS.ap()[:, 0:4],
                                       AF.Copy, accum_out=SD.ap()[:, 4:5]),
           reads=["CONSTS"], writes=["SDres"])

    def ld(name, dst_ap, src_ap, reads=(), extra_writes=(), slow=False):
        def e(eng, _aps=(dst_ap, src_ap), _slow=slow):
            return eng.dma_start(out=_aps[0], in_=_aps[1],
                                 allow_slow_non_contiguous=_slow).then_inc(d_y, 16)
        OP("y", e, reads=list(reads), writes=[name] + list(extra_writes), ndma=1)

    ld("BA", BA.ap()[:, :, :],
       x_h[0:TST, :].rearrange("(j p) d -> p j d", p=128))
    ld("WA", WA.ap()[:, :, :], wk_h.rearrange("(a p) d -> p a d", p=128))
    ld("WB", WB.ap()[:, :, :], wv_h.rearrange("(a p) d -> p a d", p=128))
    ld("PLOAD", PLOAD.ap()[:, :, :], pj_h.rearrange("(a p) d -> p a d", p=128))
    ld("B1S", B1S.ap()[:, :], b1_h.rearrange("(a p) -> p a", p=128), slow=True)
    ld("B2S", B2S.ap()[:, :], b2_h.rearrange("(a p) -> p a", p=128), slow=True)
    ld("B3S", B3S.ap()[:, :], b3_h.rearrange("(a p) -> p a", p=128), slow=True)
    ld("GB1", GB1.ap()[:, :], gb1_h.rearrange("(a p) -> p a", p=128), slow=True)
    ld("GB2", GB2.ap()[:, :], gb2_h.rearrange("(a p) -> p a", p=128), slow=True)
    ld("W4S", W4S.ap()[:, :, :], w4_h.rearrange("(a p) o -> p a o", p=128), slow=True)
    ld("GW3", GW3.ap()[:, :, :], g3_h.rearrange("(a p) o -> p a o", p=128), slow=True)
    ld("GW2", W3S.ap()[:, :, :], g2_h.rearrange("(a p) o -> p a o", p=128))
    ld("BOREP", BOREP.ap()[:, :], bo_h.broadcast_to([128, D]))
    ld("ROWS", ROWS.ap()[0:1, R_B4:R_B4 + 1], b4_h[0:1, 0:1])
    ld("ROWS", ROWS.ap()[0:1, R_GB3:R_GB3 + 1], gb3_h[0:1, 0:1])
    ld("B4C", B4HC.ap()[:, :], b4_h.broadcast_to([128, 1]))
    ld("B3C", B3HC.ap()[:, :], gb3_h.broadcast_to([128, 1]))
    ld("rar", ARCOL.ap()[:, :], ar_h.broadcast_to([128, 1]))

    def init_consts(g):
        g.memset(IDENT[:, :], 0.0)
        g.affine_select(out=IDENT[:, :], in_=IDENT[:, :],
                        compare_op=ALU.not_equal, fill=1.0, base=0,
                        pattern=[[-1, 128]], channel_multiplier=1)
        g.memset(ONESC[:, :], 1.0)
        g.memset(ONES1[:, :], 1.0)
        g.memset(BLK.ap()[:, :, :], 0.0)
        for c in range(8):
            g.memset(BLK.ap()[0:64, c, 2 * c:2 * c + 1], 0.5 * NRM * NRM)
            g.memset(BLK.ap()[64:128, c, 2 * c + 1:2 * c + 2], 0.5 * NRM * NRM)
        g.memset(CTX[:, :], 0.0)
        ctx3i = CTX.ap().rearrange("p (b w) -> p b w", w=66)
        g.memset(ctx3i[:, :, 65:66], 1.0)
        g.memset(ARENA[:, :], 0.0)
        g.memset(CONSTS.ap()[:, 0:1], LNEPS)
        g.memset(CONSTS.ap()[:, 1:2], EPSK)
        g.memset(CONSTS.ap()[:, 2:3], LNEPSK)
        g.memset(CONSTS.ap()[:, 3:4], 0.1)
        g.memset(NEG32[:, :], -1.0)
        g.memset(CSFM[:, :], 0.0)
        return g.memset(PRJCAT[:, :], 0.0)
    OP("g", init_consts, reads=["ROWS", "rar"],
       writes=["IDENT", "ONES", "BLK", "CTX", "CSUM1", "PRJ", "CONSTS", "NEG32"])

    def e(s):
        s.activation(B4HC.ap()[:, :], B4HC.ap()[:, :], AF.Copy, scale=0.5)
        return s.activation(B3HC.ap()[:, :], B3HC.ap()[:, :], AF.Copy, scale=0.5)
    OP("s", e, reads=["B4C", "B3C"], writes=["halves", "B4C", "B3C"])

    def e(t):
        for q in range(2):
            ins = t.matmul(P[0].ap()[0:64, 128 * q:128 * (q + 1)],
                           PLOAD.ap()[:, q, :], IDENT[:, :],
                           is_transpose=True, start=(q == 0), stop=(q == 1))
        return ins
    OP("t", e, reads=["PLOAD", "IDENT"], writes=["P0"], dup=True)
    OP("s", lambda s: s.activation(PRJCAT[0:64, 0:256], P[0].ap()[0:64, 0:256],
                                   AF.Copy, scale=NRM),
       reads=["P0", "PRJ"], writes=["PRJLO_w"], dup=True)
    def e(g):
        return g.dma_start(out=PRJCAT[64:128, 256:512],
                           in_=PRJCAT[0:64, 0:256]).then_inc(d_g, 16)
    OP("g", e, reads=["PRJLO_w", "PRJ"], writes=["PRJHI_w"], ndma=1)
    PRJ_R = ["PRJLO_w", "PRJHI_w"]
    def e(s):
        return s.add_instruction(mybir.InstLoadActFuncSet(
            name=nc.get_next_instruction_name(), act_func_set_id=6, ins=[], outs=[]))
    OP("s", e)

    ST4 = lambda st: st * 4
    bc4 = BC.ap().rearrange("p (j d) -> p j d", j=4)
    ctx3 = CTX.ap().rearrange("p (b w) -> p b w", w=66)
    e32 = BE.ap().rearrange("p (c t) -> p c t", c=32)

    def emit_loadA(st):
        ld("BA", BA.ap()[:, :, :],
           x_h[st * TST:(st + 1) * TST, :].rearrange("(j p) d -> p j d", p=128))

    def emit_lnA(st):
        for j in range(NSUB):
            OP("v", lambda v, j=j, st=st: v.reduce_sum(
                LN1M.ap()[:, ST4(st) + j:ST4(st) + j + 1], BA.ap()[:, j, :], axis=AX.X),
               reads=["BA"], writes=["LN1M"])
        OP("s", lambda s, st=st: s.activation(
            LN1M.ap()[:, ST4(st):ST4(st) + 4], LN1M.ap()[:, ST4(st):ST4(st) + 4],
            AF.Copy, scale=1.0 / D), writes=["LN1M"])
        for j in range(NSUB):
            OP("v", lambda v, j=j, st=st: v.tensor_scalar(
                BA.ap()[:, j, :], BA.ap()[:, j, :],
                LN1M.ap()[:, ST4(st) + j:ST4(st) + j + 1], None, op0=ALU.subtract),
               reads=["LN1M"], writes=["BA"])
            OP("s", lambda s, j=j: s.activation(
                BH.ap()[:, 0:1024], BA.ap()[:, j, :], AF.Square),
               reads=["BA"], writes=["ksq0", "ksq1"])
            OP("v", lambda v, j=j: v.reduce_sum(
                SS4.ap()[:, j:j + 1], BH.ap()[:, 0:1024], axis=AX.X),
               reads=["ksq0"], writes=["SS4"])
        OP("s", lambda s: s.activation(TMP4[:, :], SS4[:, :], AF.Ln,
                                       bias=CONSTS.ap()[:, 0:1], scale=1.0 / D),
           reads=["SS4", "CONSTS"], writes=["TMP4"])
        OP("s", lambda s, st=st: s.activation(
            LN1I.ap()[:, ST4(st):ST4(st) + 4], TMP4[:, :], AF.Exp, scale=-0.5),
           reads=["TMP4"], writes=["LN1I"])
        for j in range(NSUB):
            OP("v", lambda v, j=j, st=st: v.tensor_scalar(
                BA.ap()[:, j, :], BA.ap()[:, j, :],
                LN1I.ap()[:, ST4(st) + j:ST4(st) + j + 1], None, op0=ALU.mult),
               reads=["LN1I"], writes=["BA"])

    emit_loadA(0)
    emit_lnA(0)
    for st in range(NST):
        for c in range(8):
            b = 6 + (c % 2)
            def e(t, c=c, b=b):
                for j in range(NSUB):
                    ins = t.matmul(P[b].ap()[:, 128 * j:128 * (j + 1)],
                                   BA.ap()[:, j, 128 * c:128 * (c + 1)], IDENT[:, :],
                                   is_transpose=True, start=(j == 0), stop=(j == NSUB - 1))
                return ins
            OP("t", e, reads=["BA", "IDENT"], writes=[f"P{b}"])
            OP("v", lambda v, c=c, b=b: v.reduce_sum(
                TMP32.ap()[:, c:c + 1], P[b].ap()[:, :], axis=AX.X),
               reads=[], writes=[f"P{b}", "SXT"])
            OP("s", lambda s, c=c, b=b: s.copy(BB.ap()[:, c, :], P[b].ap()[:, :]),
               reads=[f"P{b}"], writes=["BB"])
        vgap()
        OP("v", lambda v: v.tensor_tensor(CSFM[:, :], CSFM[:, :],
                                          TMP32.ap()[:, 0:8], op=ALU.add),
           reads=["SXT"], writes=["CSFMacc"])
        if st + 1 < NST:
            emit_loadA(st + 1)
            emit_lnA(st + 1)
        for j in range(NSUB):
            for h in range(2):
                b = 6 + h
                def e(t, j=j, h=h, b=b):
                    for ki in range(8):
                        ins = t.matmul(P[b].ap()[:, :],
                                       BB.ap()[:, ki, 128 * j:128 * (j + 1)],
                                       WB.ap()[:, ki, 512 * h:512 * (h + 1)],
                                       start=(ki == 0), stop=(ki == 7))
                    return ins
                OP("t", e, reads=["BB", "WB"], writes=[f"P{b}"])
                OP("s", lambda s, j=j, h=h, b=b: s.copy(
                    bc4[:, j, 512 * h:512 * (h + 1)], P[b].ap()[:, :]),
                   reads=[f"P{b}"], writes=["BC"])
        for o in range(8):
            b = 6 + (o % 2)
            def e(t, o=o, b=b):
                for ki in range(8):
                    ins = t.matmul(P[b].ap()[:, :],
                                   WA.ap()[:, ki, 128 * o:128 * (o + 1)],
                                   BB.ap()[:, ki, :],
                                   start=(ki == 0), stop=(ki == 7))
                return ins
            OP("t", e, reads=["BB", "WA"], writes=[f"P{b}"])
            OP("s", lambda s, o=o, b=b: s.copy(BD.ap()[:, o, :], P[b].ap()[:, :]),
               reads=[f"P{b}"], writes=["BD"])
        for c in range(8):
            kq = f"ksq{c % 2}"
            OP("v", lambda v, c=c: v.tensor_tensor(
                BH.ap()[:, 512 * (c % 2):512 * (c % 2 + 1)],
                BD.ap()[:, c, :], BD.ap()[:, c, :], op=ALU.mult),
               reads=["BD"], writes=[kq])
            OP("t", lambda t, c=c: t.matmul(
                P[5].ap()[0:16, :], BLK.ap()[:, c, :],
                BH.ap()[:, 512 * (c % 2):512 * (c % 2 + 1)],
                start=(c == 0), stop=(c == 7)),
               reads=[kq, "BLK"], writes=["P5"])
        OP("s", lambda s: s.activation(NDS[0:16, :], P[5].ap()[0:16, :],
                                       AF.Copy, scale=-1.0),
           reads=["P5"], writes=["NDS"], dup=True)
        def e(t):
            for j in range(NSUB):
                ins = t.matmul(P[5].ap()[:, 16 * j:16 * (j + 1)],
                               NDS.ap()[0:16, 128 * j:128 * (j + 1)],
                               IDENT[0:16, 0:16],
                               is_transpose=True, start=(j == 0), stop=(j == NSUB - 1))
            return ins
        OP("t", e, reads=["NDS", "IDENT"], writes=["P5"], dup=True)
        OP("s", lambda s: s.copy(NDT.ap()[:, :, :].rearrange("p a b -> p (a b)"),
                                 P[5].ap()[:, 0:64]),
           reads=["P5"], writes=["NDT"], dup=True)
        for j in range(NSUB):
            kbuf = BE if j % 2 == 0 else BF
            kres = f"kp{j % 2}"
            for p in range(8):
                b = 6 + (p % 2)
                def e(t, j=j, p=p, b=b):
                    return t.matmul(P[b].ap()[:, :],
                                    BD.ap()[:, p, 128 * j:128 * (j + 1)],
                                    PRJCAT.ap()[:, :], start=True, stop=True)
                OP("t", e, reads=["BD"] + PRJ_R, writes=[f"P{b}"])
                col = (ST4(st) + j) * 8 + p
                OP("v", lambda v, b=b, col=col: v.reduce_max(
                    UMAT.ap()[:, col:col + 1], P[b].ap()[:, :], axis=AX.X),
                   reads=[f"P{b}"], writes=["UMAT"])
                def e(s, j=j, p=p, b=b, kbuf=kbuf):
                    for i in range(2):
                        ins = s.activation(
                            kbuf.ap()[:, 512 * p + 256 * i:512 * p + 256 * (i + 1)],
                            P[b].ap()[:, 256 * i:256 * (i + 1)], AF.Exp,
                            bias=NDT.ap()[:, j, 2 * p + i:2 * p + i + 1])
                    return ins
                OP("s", e, reads=["NDT"], writes=[f"{kres}_{p}", f"P{b}"])
            for b in range(4):
                def e(t, j=j, b=b, kbuf=kbuf):
                    for cc in range(8):
                        c = 8 * b + cc
                        hd = c // 2
                        t.matmul(P[b].ap()[:, 64 * cc:64 * (cc + 1)],
                                 kbuf.ap()[:, 128 * c:128 * (c + 1)],
                                 bc4[:, j, 64 * hd:64 * (hd + 1)],
                                 start=(j == 0 and cc == 0),
                                 stop=(j == NSUB - 1 and cc == 7))
                        ins = t.matmul(P[4].ap()[:, c:c + 1],
                                       kbuf.ap()[:, 128 * c:128 * (c + 1)],
                                       ONESC.ap()[:, 0:1],
                                       start=(j == 0 and c == 0),
                                       stop=(j == NSUB - 1 and c == 31))
                    return ins
                OP("t", e,
                   reads=[f"{kres}_{2 * b}", f"{kres}_{2 * b + 1}", "BC", "ONES"],
                   writes=[f"P{b}", "P4"])
        for b in range(4):
            OP("v", lambda v, b=b: v.tensor_tensor(
                ctx3[:, 8 * b:8 * (b + 1), 0:64],
                ctx3[:, 8 * b:8 * (b + 1), 0:64],
                P[b].ap().rearrange("p (a d) -> p a d", a=8), op=ALU.add),
               reads=[f"P{b}"], writes=["CTX"])
        OP("v", lambda v: v.tensor_tensor(
            CTX.ap().rearrange("p (b w) -> p b w", w=66)[:, :, 64],
            CTX.ap().rearrange("p (b w) -> p b w", w=66)[:, :, 64],
            P[4].ap()[:, 0:32], op=ALU.add),
           reads=["P4"], writes=["CTX"])

    if stage == "A":
        def fin(y_eng):
            y_eng.wait_ge(d_g, bld.cnt["DG"])
            y_eng.wait_ge(d_y, bld.cnt["DY"])
        return nc, bld, SEM, es, fin

    OP("v", lambda v: v.reduce_max(TMP4.ap()[:, 0:1], UMAT[:, :], axis=AX.X),
       reads=["UMAT"], writes=["TMP4"])
    vgap()
    vgap()
    vgap("BSLK")
    OP("t", lambda t: t.matmul(P[0].ap()[0:1, 0:128], TMP4.ap()[:, 0:1],
                               IDENT[:, :], is_transpose=True, start=True, stop=True),
       reads=["TMP4", "IDENT", "BSLK"], writes=["P0"], dup=True)
    OP("s", lambda s: s.copy(ROWS.ap()[0:1, R_UR:R_UR + 128], P[0].ap()[0:1, 0:128]),
       reads=["P0"], writes=["UROW"], dup=True)
    OP("v", lambda v: v.reduce_max(ROWS.ap()[0:1, R_SLOC:R_SLOC + 1],
                                   ROWS.ap()[0:1, R_UR:R_UR + 128], axis=AX.X),
       reads=["UROW"], writes=["SLOC"], dup=True)
    vgap()
    vgap()
    vgap("BSLK2")
    OP("t", lambda t: t.matmul(P[1].ap()[0:1, 0:8], ROWS.ap()[0:1, R_SLOC:R_SLOC + 1],
                               ONESC.ap()[0:1, 0:8], start=True, stop=True),
       reads=["SLOC", "ONES", "BSLK2"], writes=["P1"], dup=True)
    OP("s", lambda s: s.copy(ROWS.ap()[0:1, R_S8:R_S8 + 8], P[1].ap()[0:1, 0:8]),
       reads=["P1"], writes=["S8"], dup=True)
    def e(g):
        return g.dma_start(out=ccin_h[0:1, 0:8],
                           in_=ROWS.ap()[0:1, R_S8:R_S8 + 8]).then_inc(d_g, 16)
    OP("g", e, reads=["S8"], writes=["ccin"], ndma=1)
    def e(g):
        return g.collective_compute(
            "AllReduce", ALU.max, replica_groups=[list(range(8))],
            ins=[ccin_h[0:1, 0:8]], outs=[ccout_h[0:1, 0:8]])
    OP("g", e, reads=["ccin"], writes=["ccout"], cc=True)
    def e(g):
        return g.dma_start(out=ROWS.ap()[0:1, R_SG:R_SG + 8],
                           in_=ccout_h[0:1, 0:8]).then_inc(d_g, 16)
    OP("g", e, reads=["ccout"], writes=["SG"], ndma=1)
    OP("s", lambda s: s.activation(ROWS.ap()[0:1, R_BETA:R_BETA + 8],
                                   ROWS.ap()[0:1, R_SG:R_SG + 8], AF.Exp,
                                   bias=CONSTS.ap()[0:1, 2:3]),
       reads=["SG", "CONSTS"], writes=["BETA"], dup=True)
    OP("s", lambda s: s.activation(ROWS.ap()[0:1, R_BN:R_BN + 8],
                                    ROWS.ap()[0:1, R_BETA:R_BETA + 8],
                                    AF.Copy, scale=float(NTOK)),
       reads=["BETA"], writes=["BN"], dup=True)
    OP("s", lambda s: s.copy(ROWS.ap()[0:1, R_B2:R_B2 + 1],
                             ROWS.ap()[0:1, R_BETA:R_BETA + 1]),
       reads=["BETA"], writes=["B2"], dup=True)
    OP("s", lambda s: s.copy(ROWS.ap()[0:1, R_B2 + 1:R_B2 + 2],
                             ROWS.ap()[0:1, R_BN:R_BN + 1]),
       reads=["BN"], writes=["B2"], dup=True)
    OP("t", lambda t: t.matmul(P[2].ap()[:, 0:2], ONES1.ap()[0:1, :],
                               ROWS.ap()[0:1, R_B2:R_B2 + 2], start=True, stop=True),
       reads=["B2", "ONES"], writes=["P2"], dup=True)
    OP("s", lambda s: s.copy(BCAST[:, :], P[2].ap()[:, 0:2]),
       reads=["P2"], writes=["BCASTw"], dup=True)
    vgap()
    vgap("ASLK")
    for h in range(2):
        def e(t, h=h):
            for ki in range(8):
                ins = t.matmul(P[4 + h].ap()[0:1, :], CSFM.ap()[:, ki:ki + 1],
                               WB.ap()[:, ki, 512 * h:512 * (h + 1)],
                               start=(ki == 0), stop=(ki == 7))
            return ins
        OP("t", e, reads=["CSFMacc", "WB", "ASLK"], writes=[f"P{4 + h}"], dup=True)
        OP("s", lambda s, h=h: s.copy(
            ARENA.ap()[0:1, A_CS + 512 * h:A_CS + 512 * (h + 1)],
            P[4 + h].ap()[0:1, :]),
           reads=[f"P{4 + h}"], writes=["SVROW", "CSUM1"], dup=True)
    for h in range(2):
        OP("t", lambda t, h=h: t.matmul(
            P[6 + h].ap()[:, :], ONES1.ap()[0:1, :],
            ARENA.ap()[0:1, A_CS + 512 * h:A_CS + 512 * (h + 1)],
            start=True, stop=True),
           reads=["SVROW", "ONES"], writes=[f"P{6 + h}"], dup=True)
        OP("s", lambda s, h=h: s.copy(SVREP.ap()[:, 512 * h:512 * (h + 1)],
                                      P[6 + h].ap()[:, :]),
           reads=[f"P{6 + h}"], writes=["SVREPw"], dup=True)

    if stage in ("AB", "dumpA"):
        if stage == "dumpA":
            dumps = [
                (y_h[0:128, :], CTX.ap()[:, 0:1024], ["CTX"]),
                (y_h[128:256, :], CTX.ap()[:, 1024:2048], ["CTX"]),
                (y_h[256:384, 0:64], CTX.ap()[:, 2048:2112], ["CTX"]),
                (y_h[384:512, 0:32], LN1M.ap()[:, :], ["LN1M"]),
                (y_h[384:512, 32:64], LN1I.ap()[:, :], ["LN1I"]),
                (y_h[512:640, :], SVREP.ap()[:, :], ["SVREPw"]),
                (y_h[640:768, 0:2], BCAST.ap()[:, :], ["BCASTw"]),
                (y_h[640:768, 4:8], SS4.ap()[:, :], ["SS4"]),
                (y_h[640:768, 8:12], TMP4.ap()[:, :], ["TMP4"]),
                (y_h[768:769, 0:512], ROWS.ap()[0:1, :], ["BETA", "BN", "SG", "S8"]),
                (y_h[769:770, 0:1024], ARENA.ap()[0:1, :], ["SVROW"]),
                (y_h[896:1024, 0:256], UMAT.ap()[:, :], ["UMAT"]),
            ]
            for dst, srcap, rds in dumps:
                def e(g, _a=(dst, srcap)):
                    return g.dma_start(out=_a[0], in_=_a[1]).then_inc(d_g, 16)
                OP("g", e, reads=rds, writes=["ydump"], ndma=1)
        def fin(y_eng):
            y_eng.wait_ge(d_g, bld.cnt["DG"])
            y_eng.wait_ge(d_y, bld.cnt["DY"])
        return nc, bld, SEM, es, fin

    ld("WA", WA.ap()[:, :, :], wq_h.rearrange("(a p) d -> p a d", p=128))
    ld("WB", WB.ap()[:, :, :], wo_h.rearrange("(a p) d -> p a d", p=128))

    x2v = BF.ap().rearrange("p (a d) -> p a d", a=4)
    atm = BF.ap()[:, 2048:3072]
    afm = BF.ap()[:, 3072:4096].rearrange("p (a c) -> p a c", a=8)

    for st in range(NST):
        for j in range(NSUB):
            ld(f"BAc{j}", BA.ap()[:, j, :],
               x_h[st * TST + 128 * j:st * TST + 128 * (j + 1), :],
               extra_writes=(["BA"] if st == 0 else []))
        for j in range(NSUB):
            OP("v", lambda v, j=j, st=st: v.tensor_scalar(
                bc4[:, j, :], BA.ap()[:, j, :],
                LN1M.ap()[:, ST4(st) + j:ST4(st) + j + 1],
                LN1I.ap()[:, ST4(st) + j:ST4(st) + j + 1],
                op0=ALU.subtract, op1=ALU.mult),
               reads=[f"BAc{j}", "LN1M", "LN1I"], writes=["BC"])
        for c in range(8):
            b = 6 + (c % 2)
            def e(t, c=c, b=b):
                for j in range(NSUB):
                    ins = t.matmul(P[b].ap()[:, 128 * j:128 * (j + 1)],
                                   bc4[:, j, 128 * c:128 * (c + 1)], IDENT[:, :],
                                   is_transpose=True, start=(j == 0), stop=(j == NSUB - 1))
                return ins
            OP("t", e, reads=["BC", "IDENT"], writes=[f"P{b}"])
            OP("s", lambda s, c=c, b=b: s.copy(BB.ap()[:, c, :], P[b].ap()[:, :]),
               reads=[f"P{b}"], writes=["BB"])
        for o in range(8):
            b = 6 + (o % 2)
            def e(t, o=o, b=b):
                for ki in range(8):
                    ins = t.matmul(P[b].ap()[:, :],
                                   WA.ap()[:, ki, 128 * o:128 * (o + 1)],
                                   BB.ap()[:, ki, :],
                                   start=(ki == 0), stop=(ki == 7))
                return ins
            OP("t", e, reads=["BB", "WA"], writes=[f"P{b}"])
            OP("s", lambda s, o=o, b=b: s.copy(BD.ap()[:, o, :], P[b].ap()[:, :]),
               reads=[f"P{b}"], writes=["BD"])
        for c in range(8):
            kq = f"qsq{c % 2}"
            OP("v", lambda v, c=c: v.tensor_tensor(
                BH.ap()[:, 512 * (c % 2):512 * (c % 2 + 1)],
                BD.ap()[:, c, :], BD.ap()[:, c, :], op=ALU.mult),
               reads=["BD"], writes=[kq, f"ksq{c % 2}"])
            OP("t", lambda t, c=c: t.matmul(
                P[0].ap()[0:16, :], BLK.ap()[:, c, :],
                BH.ap()[:, 512 * (c % 2):512 * (c % 2 + 1)],
                start=(c == 0), stop=(c == 7)),
               reads=[kq, "BLK"], writes=["P0"])
        OP("s", lambda s: s.activation(NDS[0:16, :], P[0].ap()[0:16, :],
                                       AF.Copy, scale=-1.0),
           reads=["P0"], writes=["NDS"], dup=True)
        def e(t):
            for j in range(NSUB):
                ins = t.matmul(P[0].ap()[:, 16 * j:16 * (j + 1)],
                               NDS.ap()[0:16, 128 * j:128 * (j + 1)],
                               IDENT[0:16, 0:16],
                               is_transpose=True, start=(j == 0), stop=(j == NSUB - 1))
            return ins
        OP("t", e, reads=["NDS", "IDENT"], writes=["P0"], dup=True)
        OP("s", lambda s: s.copy(NDT.ap()[:, :, :].rearrange("p a b -> p (a b)"),
                                 P[0].ap()[:, 0:64]),
           reads=["P0"], writes=["NDT"], dup=True)
        for j in range(NSUB):
            for p in range(8):
                b = 6 + (p % 2)
                def e(t, j=j, p=p, b=b):
                    return t.matmul(P[b].ap()[:, :],
                                    BD.ap()[:, p, 128 * j:128 * (j + 1)],
                                    PRJCAT.ap()[:, :], start=True, stop=True)
                OP("t", e, reads=["BD"] + PRJ_R, writes=[f"P{b}"])
                OP("v", lambda v, p=p, b=b: v.reduce_max(
                    NSTAB.ap()[:, 2 * p:2 * p + 2],
                    P[b].ap().rearrange("p (a m) -> p a m", a=2),
                    axis=AX.X, negate=True),
                   reads=[f"P{b}"], writes=["NSTABr"])
                vgap()
                OP("v", lambda v, j=j, p=p: v.tensor_tensor(
                    BIASQ.ap()[:, 2 * p:2 * p + 2], NSTAB.ap()[:, 2 * p:2 * p + 2],
                    NDT.ap()[:, j, 2 * p:2 * p + 2], op=ALU.add),
                   reads=["NSTABr", "NDT"], writes=["BIASQw"])
                def e(s, j=j, p=p, b=b):
                    for i in range(2):
                        ins = s.activation(
                            BC.ap()[:, 512 * p + 256 * i:512 * p + 256 * (i + 1)],
                            P[b].ap()[:, 256 * i:256 * (i + 1)], AF.Exp,
                            bias=BIASQ.ap()[:, 2 * p + i:2 * p + i + 1])
                    return ins
                OP("s", e, reads=["BIASQw"], writes=["BC", f"P{b}"])
            for g8 in range(8):
                b = g8 % 2
                def e(t, g8=g8, b=b):
                    for q in range(4):
                        mc = 4 * g8 + q
                        ins = t.matmul(P[b].ap()[:, 128 * q:128 * (q + 1)],
                                       BC.ap()[:, 128 * mc:128 * (mc + 1)], IDENT[:, :],
                                       is_transpose=True, start=(q == 0), stop=(q == 3))
                    return ins
                OP("t", e, reads=["BC", "IDENT"], writes=[f"P{b}"])
                ew = ["BE"]
                if st == 0 and j == 0 and g8 == 0:
                    ew += [f"kp0_{p}" for p in range(8)]
                OP("s", lambda s, g8=g8, b=b: s.activation(
                    BE.ap()[:, 512 * g8:512 * (g8 + 1)], P[b].ap()[:, :],
                    AF.Copy, bias=EPSK),
                   reads=[f"P{b}", "CONSTS"], writes=ew)
            def e(t):
                for hh in range(16):
                    for q in range(2):
                        kc = 2 * hh + q
                        ins = t.matmul(P[1].ap()[:, 2 * hh:2 * hh + 2],
                                       e32[:, kc, :], ctx3[:, kc, 64:66],
                                       start=(hh == 0 and q == 0),
                                       stop=(hh == 15 and q == 1))
                return ins
            OP("t", e, reads=["BE", "CTX"], writes=["P1"])
            for b2 in (2, 3):
                def e(t, b2=b2):
                    for hh in range(8):
                        h = 8 * (b2 - 2) + hh
                        for q in range(2):
                            kc = 2 * h + q
                            ins = t.matmul(P[b2].ap()[:, 64 * hh:64 * (hh + 1)],
                                           e32[:, kc, :], ctx3[:, kc, 0:64],
                                           start=(hh == 0 and q == 0),
                                           stop=(hh == 7 and q == 1))
                    return ins
                OP("t", e, reads=["BE", "CTX"], writes=[f"P{b2}"])
            OP("s", lambda s: s.copy(TMP32[:, :], P[1].ap()[:, 0:32]),
               reads=["P1"], writes=["P1c"])
            p1s = TMP32.ap().rearrange("p (a b) -> p a b", a=16)
            for b2 in (2, 3):
                hb = 8 * (b2 - 2)
                OP("v", lambda v, hb=hb: v.scalar_tensor_tensor(
                    DEN.ap()[:, hb:hb + 8],
                    p1s[:, hb:hb + 8, 1], BCAST.ap()[:, 1:2],
                    p1s[:, hb:hb + 8, 0], op0=ALU.mult, op1=ALU.add),
                   reads=["P1c", "BCASTw"], writes=["DENw"])
                OP("v", lambda v, hb=hb: v.tensor_scalar(
                    RBD.ap()[:, hb:hb + 8], p1s[:, hb:hb + 8, 1],
                    BCAST.ap()[:, 0:1], None, op0=ALU.mult),
                   reads=["P1c", "BCASTw"], writes=["RBDw"])
            for b2 in (2, 3):
                hb = 8 * (b2 - 2)
                OP("v", lambda v, hb=hb: v.reciprocal(
                    DINV.ap()[:, hb:hb + 8], DEN.ap()[:, hb:hb + 8]),
                   reads=["DENw"], writes=["DINVw"])
            for b2 in (2, 3):
                hb = 8 * (b2 - 2)
                OP("v", lambda v, hb=hb: v.tensor_tensor(
                    RBD.ap()[:, hb:hb + 8], RBD.ap()[:, hb:hb + 8],
                    DINV.ap()[:, hb:hb + 8], op=ALU.mult),
                   reads=["DINVw"], writes=["RBDw"])
            for b2 in (2, 3):
                hb = 8 * (b2 - 2)
                ew = ["BF"]
                if st == 0 and j == 0 and b2 == 2:
                    ew += [f"kp1_{p}" for p in range(8)]
                OP("v", lambda v, hb=hb, b2=b2: v.tensor_tensor(
                    atm[:, 512 * (b2 - 2):512 * (b2 - 1)].rearrange(
                        "p (a d) -> p a d", a=8),
                    P[b2].ap().rearrange("p (a d) -> p a d", a=8),
                    DINV.ap()[:, hb:hb + 8].unsqueeze(2).broadcast_to([128, 8, 64]),
                    op=ALU.mult),
                   reads=[f"P{b2}", "DINVw", "BF"], writes=ew)
            for b2 in (2, 3):
                hb = 8 * (b2 - 2)
                t2off = 1024 if b2 == 2 else 512
                OP("v", lambda v, hb=hb, b2=b2, t2off=t2off: v.tensor_tensor(
                    BH.ap()[:, t2off:t2off + 512].rearrange(
                        "p (a d) -> p a d", a=8),
                    SVREP.ap()[:, 512 * (b2 - 2):512 * (b2 - 1)].rearrange(
                        "p (a d) -> p a d", a=8),
                    RBD.ap()[:, hb:hb + 8].unsqueeze(2).broadcast_to([128, 8, 64]),
                    op=ALU.mult),
                   reads=["SVREPw", "RBDw"], writes=["T2w", "qsq1"])
            for b2 in (2, 3):
                t2off = 1024 if b2 == 2 else 512
                OP("v", lambda v, b2=b2, t2off=t2off: v.tensor_tensor(
                    atm[:, 512 * (b2 - 2):512 * (b2 - 1)],
                    atm[:, 512 * (b2 - 2):512 * (b2 - 1)],
                    BH.ap()[:, t2off:t2off + 512],
                    op=ALU.add),
                   reads=["T2w", "BF"], writes=["BF"])
            vgap()
            vgap("CSLK")
            for g2 in range(2):
                b = g2
                def e(t, g2=g2, b=b):
                    for q in range(4):
                        c = 4 * g2 + q
                        ins = t.matmul(P[b].ap()[:, 128 * q:128 * (q + 1)],
                                       atm[:, 128 * c:128 * (c + 1)], IDENT[:, :],
                                       is_transpose=True, start=(q == 0), stop=(q == 3))
                    return ins
                OP("t", e, reads=["BF", "IDENT", "CSLK"], writes=[f"P{b}"])
                OP("s", lambda s, g2=g2, b=b: s.copy(
                    afm[:, 4 * g2:4 * (g2 + 1), :].rearrange("p a c -> p (a c)"),
                    P[b].ap()[:, :]),
                   reads=[f"P{b}", "BF"], writes=["BF"])
            for h in range(2):
                b = 2 + h
                def e(t, h=h, b=b):
                    for ki in range(8):
                        ins = t.matmul(P[b].ap()[:, :], afm[:, ki, :],
                                       WB.ap()[:, ki, 512 * h:512 * (h + 1)],
                                       start=(ki == 0), stop=(ki == 7))
                    return ins
                OP("t", e, reads=["BF", "WB"], writes=[f"P{b}"])
            OP("g", lambda g, j=j: g.tensor_tensor(
                BF.ap()[:, 2048:3072], BA.ap()[:, j, :], BOREP[:, :], op=ALU.add),
               reads=[f"BAc{j}", "BOREP", "BF"], writes=["BF"])
            for h in range(2):
                b = 2 + h
                OP("v", lambda v, j=j, h=h, b=b: v.tensor_tensor(
                    x2v[:, j % 2, 512 * h:512 * (h + 1)],
                    BF.ap()[:, 2048 + 512 * h:2048 + 512 * (h + 1)],
                    P[b].ap()[:, :], op=ALU.add),
                   reads=[f"P{b}", "BF"], writes=["BF"])
            vgap()
            vgap("CSLK2")
            def e(g, j=j, st=st):
                return g.dma_start(
                    out=(y_h if stage == "x2" else x2s_h)[
                        st * TST + 128 * j:st * TST + 128 * (j + 1), :],
                    in_=x2v[:, j % 2, :]).then_inc(d_g, 16)
            OP("g", e, reads=["BF"], writes=["x2st"], ndma=1)
            idx = ST4(st) + j
            vgap()
            OP("v", lambda v, j=j, idx=idx: v.reduce_sum(
                LN2M.ap()[:, idx:idx + 1], x2v[:, j % 2, :], axis=AX.X),
               reads=["BF"], writes=["LN2M"])
            OP("s", lambda s, j=j: s.activation(
                BH.ap()[:, 0:1024], x2v[:, j % 2, :], AF.Square),
               reads=["BF"], writes=["qsq0", "qsq1", "ksq0", "ksq1"])
            OP("v", lambda v, j=j: v.reduce_sum(
                SS4.ap()[:, j:j + 1], BH.ap()[:, 0:1024], axis=AX.X),
               reads=["qsq0"], writes=["SS4"])
            OP("s", lambda s, idx=idx: s.activation(
                LN2M.ap()[:, idx:idx + 1], LN2M.ap()[:, idx:idx + 1], AF.Copy,
                scale=1.0 / D),
               reads=[], writes=["LN2M"])
            OP("s", lambda s, idx=idx: s.activation(
                TMP4B.ap()[:, 0:1], LN2M.ap()[:, idx:idx + 1], AF.Square),
               reads=["LN2M"], writes=["TMP4B"])
            OP("s", lambda s: s.activation(
                TMP4B.ap()[:, 1:2], TMP4B.ap()[:, 0:1], AF.Copy,
                scale=-1.0, bias=LNEPS),
               reads=["CONSTS"], writes=["TMP4B"])
            OP("s", lambda s, j=j: s.activation(
                TMP4B.ap()[:, 2:3], SS4.ap()[:, j:j + 1], AF.Ln, scale=1.0 / D,
                bias=TMP4B.ap()[:, 1:2]),
               reads=["SS4"], writes=["TMP4B"])
            OP("s", lambda s, idx=idx: s.activation(
                LN2I.ap()[:, idx:idx + 1], TMP4B.ap()[:, 2:3], AF.Exp, scale=-0.5),
               reads=["TMP4B"], writes=["LN2I"])
            for h in range(2):
                OP("t", lambda t, st=st, j=j, h=h, idx=idx: t.matmul(
                    P[4 + h].ap()[0:1, :], LN2I.ap()[:, idx:idx + 1],
                    x2v[:, j % 2, 512 * h:512 * (h + 1)],
                    start=(st == 0 and j == 0), stop=False),
                   reads=["LN2I", "BF", "CSLK2"], writes=[f"P{4 + h}"])

    if stage == "x2":
        def fin(y_eng):
            y_eng.wait_ge(d_g, bld.cnt["DG"])
            y_eng.wait_ge(d_y, bld.cnt["DY"])
        return nc, bld, SEM, es, fin

    def e(v):
        for ki in range(2):
            v.tensor_copy(W4R.ap()[:, ki, :],
                          W4S.ap()[:, ki, 0:1].broadcast_to([128, 128]))
            ins = v.tensor_copy(GW3R.ap()[:, ki, :],
                                GW3.ap()[:, ki, 0:1].broadcast_to([128, 128]))
        return ins
    OP("v", e, reads=["W4S", "GW3"], writes=["W4R", "GW3R"], dup=True)

    def e(t):
        for j in range(32):
            ins = t.matmul(P[0].ap()[0:1, 0:1], LN2M.ap()[:, j:j + 1],
                           LN2I.ap()[:, j:j + 1], start=(j == 0), stop=(j == 31))
        return ins
    OP("t", e, reads=["LN2M", "LN2I"], writes=["P0"], dup=True)
    OP("s", lambda s: s.copy(ROWS.ap()[0:1, R_S2:R_S2 + 1], P[0].ap()[0:1, 0:1]),
       reads=["P0"], writes=["S2col"], dup=True)
    for h in range(2):
        OP("t", lambda t, h=h: t.matmul(
            P[4 + h].ap()[0:1, :], ROWS.ap()[0:1, R_S2:R_S2 + 1],
            NEG32.ap()[0:1, :], start=False, stop=True),
           reads=["S2col", "NEG32"], writes=[f"P{4 + h}"])
    for h in range(2):
        OP("s", lambda s, h=h: s.activation(
            ARENA.ap()[0:1, 512 * h:512 * (h + 1)], P[4 + h].ap()[0:1, :],
            AF.Copy, scale=1.0 / NTOK),
           reads=[f"P{4 + h}"], writes=["GM", "SVROW", "CSUM1"], dup=True)
    def e(t):
        for q in range(8):
            ins = t.matmul(P[1].ap()[:, q:q + 1],
                           ARENA.ap()[0:1, 128 * q:128 * (q + 1)],
                           IDENT[0:1, 0:1], is_transpose=True,
                           start=(q == 0), stop=(q == 7))
        return ins
    OP("t", e, reads=["GM", "IDENT"], writes=["P1"], dup=True)
    OP("s", lambda s: s.copy(CSFM[:, :], P[1].ap()[:, 0:8]),
       reads=["P1"], writes=["CSFMw", "CSFMacc"], dup=True)

    ld("WA", WA.ap()[:, :, :], w1_h.rearrange("(a p) d -> p a d", p=128))
    ld("WB2", WB.ap()[:, :, 0:512], w2_h.rearrange("(a p) d -> p a d", p=128),
       extra_writes=["WB"])
    ld("WBg", WB.ap()[:, :, 512:1024], g1_h.rearrange("(a p) d -> p a d", p=128),
       extra_writes=["WB"])

    def e(s):
        return s.add_instruction(mybir.InstLoadActFuncSet(
            name=nc.get_next_instruction_name(), act_func_set_id=10, ins=[], outs=[]))
    OP("s", e)

    def e(t):
        for o in range(4):
            for ki in range(8):
                ins = t.matmul(P[2].ap()[:, o:o + 1],
                               WB.ap()[:, ki, 512 + 128 * o:512 + 128 * (o + 1)],
                               CSFM.ap()[:, ki:ki + 1],
                               start=(o == 0 and ki == 0), stop=(o == 3 and ki == 7))
        return ins
    OP("t", e, reads=["CSFMw", "WBg"], writes=["P2"], dup=True)
    for o in range(4):
        OP("s", lambda s, o=o: s.activation(GH1.ap()[:, o:o + 1], P[2].ap()[:, o:o + 1],
                                            AF.Gelu, bias=GB1.ap()[:, o:o + 1]),
           reads=["P2", "GB1"], writes=["GH1w"], dup=True)
    def e(t):
        for o in range(2):
            for ki in range(4):
                ins = t.matmul(P[3].ap()[:, o:o + 1],
                               W3S.ap()[:, ki, 128 * o:128 * (o + 1)],
                               GH1.ap()[:, ki:ki + 1],
                               start=(o == 0 and ki == 0), stop=(o == 1 and ki == 3))
        return ins
    OP("t", e, reads=["GH1w", "GW2"], writes=["P3"], dup=True)
    for o in range(2):
        OP("s", lambda s, o=o: s.activation(GH2.ap()[:, o:o + 1], P[3].ap()[:, o:o + 1],
                                            AF.Gelu, bias=GB2.ap()[:, o:o + 1]),
           reads=["P3", "GB2"], writes=["GH2w"], dup=True)
    def e(t):
        for ki in range(2):
            ins = t.matmul(P[4].ap()[:, 0:1], GW3R.ap()[:, ki, :],
                           GH2.ap()[:, ki:ki + 1], start=(ki == 0), stop=(ki == 1))
        return ins
    OP("t", e, reads=["GH2w", "GW3R"], writes=["P4"], dup=True)
    OP("s", lambda s: s.activation(TGC.ap()[:, :], P[4].ap()[:, 0:1], AF.Tanh,
                                   scale=0.5, bias=B3HC.ap()[:, 0:1]),
       reads=["P4", "B3C"], writes=["TGc"], dup=True)
    OP("t", lambda t: t.matmul(P[5].ap()[0:1, 0:128], ARCOL.ap()[0:1, 0:1],
                               ONES1.ap()[0:1, :], start=True, stop=True),
       reads=["rar", "ONES"], writes=["P5"], dup=True)
    OP("s", lambda s: s.copy(ROWS.ap()[0:1, R_UR:R_UR + 128],
                             P[5].ap()[0:1, 0:128]),
       reads=["P5"], writes=["UROW"], dup=True)
    OP("t", lambda t: t.matmul(P[5].ap()[:, 0:1], ROWS.ap()[0:1, R_UR:R_UR + 128],
                               TGC.ap()[0:1, 0:1], start=True, stop=True),
       reads=["UROW", "TGc"], writes=["P5"], dup=True)
    OP("s", lambda s: s.activation(CB2C.ap()[:, :], P[5].ap()[:, 0:1], AF.Copy,
                                   scale=0.7, bias=0.1),
       reads=["P5", "CONSTS"], writes=["CB2c"], dup=True)
    OP("s", lambda s: s.activation(A2C.ap()[:, :], ARCOL.ap()[:, :], AF.Copy,
                                   scale=0.3),
       reads=["rar"], writes=["A2c"], dup=True)

    ld("W3S", W3S.ap()[:, :, :], w3_h.rearrange("(a p) o -> p a o", p=128),
       extra_writes=["GW2"])

    be4 = BE.ap().rearrange("p (j d) -> p j d", j=4)
    h2v = BE.ap()[:, 0:2048].rearrange("p (a d) -> p a d", a=4)
    h3v = BE.ap()[:, 2048:3072].rearrange("p (a d) -> p a d", a=2)
    crrep = BE.ap()[:, 3072:3584]
    bd2 = BD.ap().rearrange("p a d -> p (a d)")

    for st in range(NST):
        xres = "BA" if st % 2 == 0 else "BF"
        xv4 = (BA.ap()[:, :, :] if st % 2 == 0
               else BF.ap().rearrange("p (j d) -> p j d", j=4))
        ld(xres, xv4[:, :, :],
           x2s_h[st * TST:(st + 1) * TST, :].rearrange("(j p) d -> p j d", p=128),
           reads=["x2st"],
           extra_writes=(["BAc0", "BAc1", "BAc2", "BAc3"] if st == 0 else []))
        for j in range(NSUB):
            OP("v", lambda v, j=j, st=st, xv4=xv4: v.tensor_scalar(
                be4[:, j, :], xv4[:, j, :],
                LN2M.ap()[:, ST4(st) + j:ST4(st) + j + 1],
                LN2I.ap()[:, ST4(st) + j:ST4(st) + j + 1],
                op0=ALU.subtract, op1=ALU.mult),
               reads=[xres, "LN2M", "LN2I"], writes=["BE"])
        vgap()
        vgap("VSLK")
        for c in range(8):
            b = c % 2
            def e(t, c=c, b=b):
                for j in range(NSUB):
                    ins = t.matmul(P[b].ap()[:, 128 * j:128 * (j + 1)],
                                   be4[:, j, 128 * c:128 * (c + 1)], IDENT[:, :],
                                   is_transpose=True, start=(j == 0), stop=(j == NSUB - 1))
                return ins
            OP("t", e, reads=["BE", "IDENT", "VSLK"], writes=[f"P{b}"])
            OP("s", lambda s, c=c, b=b: s.copy(BB.ap()[:, c, :], P[b].ap()[:, :]),
               reads=[f"P{b}"], writes=["BB"])
        cur, curres = BB, "BB"
        for cyc in range(NCYC):
            nxt = BC if cur is BB else BB
            nxtres = "BC" if curres == "BB" else "BB"
            curv = cur.ap() if cur is BB else cur.ap().rearrange("p (a d) -> p a d", a=8)
            nxtv = nxt.ap() if nxt is BB else nxt.ap().rearrange("p (a d) -> p a d", a=8)
            for o in range(8):
                b = 2 + (o % 2)
                def e(t, o=o, b=b, curv=curv):
                    for ki in range(8):
                        ins = t.matmul(P[b].ap()[:, :],
                                       WA.ap()[:, ki, 128 * o:128 * (o + 1)],
                                       curv[:, ki, :], start=(ki == 0), stop=(ki == 7))
                    return ins
                OP("t", e, reads=[curres, "WA", "VSLK"], writes=[f"P{b}"])
                OP("s", lambda s, o=o, b=b: s.activation(
                    BD.ap()[:, o, :], P[b].ap()[:, :], AF.Gelu,
                    bias=B1S.ap()[:, o:o + 1]),
                   reads=[f"P{b}", "B1S"], writes=["BD"])
            for o in range(4):
                b = 4 + (o % 2)
                def e(t, o=o, b=b):
                    for ki in range(8):
                        ins = t.matmul(P[b].ap()[:, :],
                                       WB.ap()[:, ki, 128 * o:128 * (o + 1)],
                                       BD.ap()[:, ki, :], start=(ki == 0), stop=(ki == 7))
                    return ins
                OP("t", e, reads=["BD", "WB2"], writes=[f"P{b}"])
                OP("s", lambda s, o=o, b=b: s.activation(
                    h2v[:, o, :], P[b].ap()[:, :], AF.Gelu,
                    bias=B2S.ap()[:, o:o + 1]),
                   reads=[f"P{b}", "B2S"], writes=["BE"])
            for o in range(2):
                def e(t, o=o):
                    for ki in range(4):
                        ins = t.matmul(P[6].ap()[:, :],
                                       W3S.ap()[:, ki, 128 * o:128 * (o + 1)],
                                       h2v[:, ki, :], start=(ki == 0), stop=(ki == 3))
                    return ins
                OP("t", e, reads=["BE", "W3S"], writes=["P6"])
                OP("s", lambda s, o=o: s.activation(
                    h3v[:, o, :], P[6].ap()[:, :], AF.Gelu,
                    bias=B3S.ap()[:, o:o + 1]),
                   reads=["P6", "B3S"], writes=["BE"])
            def e(t):
                for ki in range(2):
                    ins = t.matmul(P[7].ap()[:, :], W4R.ap()[:, ki, :],
                                   h3v[:, ki, :], start=(ki == 0), stop=(ki == 1))
                return ins
            OP("t", e, reads=["BE", "W4R"], writes=["P7"])
            OP("s", lambda s: s.activation(crrep, P[7].ap()[:, :], AF.Tanh,
                                           scale=0.5, bias=B4HC.ap()[:, 0:1]),
               reads=["P7", "B4C"], writes=["BE"])
            OP("v", lambda v: v.tensor_scalar(crrep, crrep, A2C.ap()[:, 0:1],
                                              CB2C.ap()[:, 0:1],
                                              op0=ALU.mult, op1=ALU.add),
               reads=["BE", "A2c", "CB2c"], writes=["BE"])
            vgap()
            vgap()
            for c in range(8):
                OP("v", lambda v, c=c, curv=curv, nxtv=nxtv: v.tensor_tensor(
                    nxtv[:, c, :], crrep, curv[:, SIGMA[c], :], op=ALU.mult),
                   reads=[curres, "BE"], writes=[nxtres])
            for c in range(8):
                OP("v", lambda v, c=c, curv=curv, nxtv=nxtv: v.tensor_tensor(
                    nxtv[:, c, :], curv[:, c, :], nxtv[:, c, :],
                    op=(ALU.add if SIGN[c] > 0 else ALU.subtract)),
                   reads=[curres], writes=[nxtres])
            vgap()
            vgap()
            vgap("VSLK")
            cur, curres = nxt, nxtres
        curv = cur.ap() if cur is BB else cur.ap().rearrange("p (a d) -> p a d", a=8)
        for j in range(NSUB):
            for g2 in range(2):
                b = g2
                def e(t, j=j, g2=g2, b=b, curv=curv):
                    for q in range(4):
                        c = 4 * g2 + q
                        ins = t.matmul(P[b].ap()[:, 128 * q:128 * (q + 1)],
                                       curv[:, c, 128 * j:128 * (j + 1)], IDENT[:, :],
                                       is_transpose=True, start=(q == 0), stop=(q == 3))
                    return ins
                OP("t", e, reads=[curres, "IDENT", "VSLK"], writes=[f"P{b}"])
                OP("v", lambda v, j=j, g2=g2, b=b, xv4=xv4: v.tensor_tensor(
                    bd2[:, 1024 * (j % 2) + 512 * g2:1024 * (j % 2) + 512 * (g2 + 1)],
                    xv4[:, j, 512 * g2:512 * (g2 + 1)],
                    P[b].ap()[:, :], op=ALU.add),
                   reads=[f"P{b}", xres], writes=["BD"])
            def e(g, j=j, st=st):
                return g.dma_start(
                    out=y_h[st * TST + 128 * j:st * TST + 128 * (j + 1), :],
                    in_=bd2[:, 1024 * (j % 2):1024 * (j % 2 + 1)]).then_inc(d_g, 16)
            OP("g", e, reads=["BD"], writes=["yst"], ndma=1)

    def e(g):
        g.dma_start(out=ydbg_h[0:1, 0:1024], in_=ARENA.ap()[0:1, :]).then_inc(d_g, 16)
        g.dma_start(out=ydbg_h[0:1, 1024:1536], in_=ROWS.ap()[0:1, :]).then_inc(d_g, 16)
        g.dma_start(out=ydbg_h[0:1, 1536:2048],
                    in_=BE.ap()[0:1, 3072:3584]).then_inc(d_g, 16)
        g.dma_start(out=ydbg_h[0:1, 2048:2056],
                    in_=CSFM.ap()[0:1, 0:8]).then_inc(d_g, 16)
        g.dma_start(out=ydbg_h[0:1, 2056:2057],
                    in_=TGC.ap()[0:1, 0:1]).then_inc(d_g, 16)
        g.dma_start(out=ydbg_h[0:1, 2057:2058],
                    in_=A2C.ap()[0:1, 0:1]).then_inc(d_g, 16)
        g.dma_start(out=ydbg_h[0:1, 2058:2059],
                    in_=CB2C.ap()[0:1, 0:1]).then_inc(d_g, 16)
        g.dma_start(out=ydbg_h[0:1, 2059:2063],
                    in_=GH1.ap()[0:1, 0:4]).then_inc(d_g, 16)
        g.dma_start(out=ydbg_h[0:1, 2063:2065],
                    in_=GH2.ap()[0:1, 0:2]).then_inc(d_g, 16)
        g.dma_start(out=ydbg2_h[:, 0:4], in_=GH1.ap()[:, :],
                    allow_slow_non_contiguous=True).then_inc(d_g, 16)
        g.dma_start(out=ydbg2_h[:, 4:6], in_=GH2.ap()[:, :],
                    allow_slow_non_contiguous=True).then_inc(d_g, 16)
        g.dma_start(out=ydbg2_h[:, 6:7], in_=TGC.ap()[:, :],
                    allow_slow_non_contiguous=True).then_inc(d_g, 16)
        g.dma_start(out=ydbg2_h[:, 7:8], in_=CB2C.ap()[:, :],
                    allow_slow_non_contiguous=True).then_inc(d_g, 16)
        g.dma_start(out=ydbg2_h[:, 8:9], in_=GW3R.ap()[:, 0, 0:1],
                    allow_slow_non_contiguous=True).then_inc(d_g, 16)
        g.dma_start(out=ydbg2_h[:, 9:10], in_=GW3R.ap()[:, 1, 0:1],
                    allow_slow_non_contiguous=True).then_inc(d_g, 16)
        g.dma_start(out=ydbg2_h[:, 10:11], in_=GW3R.ap()[:, 0, 127:128],
                    allow_slow_non_contiguous=True).then_inc(d_g, 16)
        g.dma_start(out=ydbg2_h[:, 11:12], in_=W4R.ap()[:, 0, 0:1],
                    allow_slow_non_contiguous=True).then_inc(d_g, 16)
        g.dma_start(out=ydbg2_h[:, 12:13], in_=W4R.ap()[:, 1, 0:1],
                    allow_slow_non_contiguous=True).then_inc(d_g, 16)
        g.dma_start(out=ydbg2_h[:, 13:14], in_=GW3.ap()[:, 0, 0:1],
                    allow_slow_non_contiguous=True).then_inc(d_g, 16)
        g.dma_start(out=ydbg2_h[:, 14:15], in_=GW3.ap()[:, 1, 0:1],
                    allow_slow_non_contiguous=True).then_inc(d_g, 16)
        return g.dma_start(out=ydbg2_h[:, 15:16], in_=W4S.ap()[:, 0, 0:1],
                           allow_slow_non_contiguous=True).then_inc(d_g, 16)
    OP("g", e, reads=["GM", "TGc", "CB2c", "A2c", "BE", "CSFMw", "GH1w", "GH2w"],
       writes=["ydbg"], ndma=21)

    def fin(y_eng):
        y_eng.wait_ge(d_g, bld.cnt["DG"])
        y_eng.wait_ge(d_y, bld.cnt["DY"])
    return nc, bld, SEM, es, fin


def emit(nc, bld, SEM, es, fin, nops=None):
    own = dict(t="T", v="V", s="S", g="G")
    ops = bld.ops if nops is None else bld.ops[:nops]
    if nops is not None:
        pc = dict(DY=0, DG=0)
        for (_e, _w2, _f, key, amt, _r, _w3) in ops:
            if key in pc:
                pc[key] += amt
        def fin(y_eng, _pc=pc):
            y_eng.wait_ge(SEM["DG"], _pc["DG"])
            y_eng.wait_ge(SEM["DY"], _pc["DY"])
    with es:
        with nc.Block() as block:
            def run(eng_name):
                def body(eng):
                    hw = {}
                    for (e, waits, emit_fn, key, amt, _r, _w) in ops:
                        if e != eng_name:
                            continue
                        for k, v in sorted(waits.items()):
                            if k == own.get(eng_name):
                                continue
                            if hw.get(k, 0) >= v:
                                continue
                            eng.wait_ge(SEM[k], v)
                            hw[k] = v
                        last = emit_fn(eng)
                        if key not in ("DY", "DG"):
                            last.then_inc(SEM[key], amt)
                    if eng_name == "y":
                        fin(eng)
                return body
            block.sync(run("y"))
            block.tensor(run("t"))
            block.vector(run("v"))
            block.scalar(run("s"))
            block.gpsimd(run("g"))
    return nc


NOPS = None
LAST_RESULTS = None

def kernel(**inputs):
    ncores = 8
    nc, bld, SEM, es, fin = build_nc(STAGE)
    emit(nc, bld, SEM, es, fin, nops=NOPS)

    f32 = np.float32
    ca = lambda a: np.ascontiguousarray(np.asarray(a, dtype=f32))
    x = ca(inputs["L0_attn_x"])
    shared = {
        "wq": ca(inputs["L0_attn_wq"]), "wk": ca(inputs["L0_attn_wk"]),
        "wv": ca(inputs["L0_attn_wv"]), "wo": ca(inputs["L0_attn_wo"]),
        "bo": ca(inputs["L0_attn_bo"]).reshape(1, D),
        "proj": ca(inputs["L0_attn_proj"]),
        "w1": ca(inputs["L0_ffn_cn_w1"]), "b1": ca(inputs["L0_ffn_cn_b1"]),
        "w2": ca(inputs["L0_ffn_cn_w2"]), "b2": ca(inputs["L0_ffn_cn_b2"]),
        "w3": ca(inputs["L0_ffn_cn_w3"]), "b3": ca(inputs["L0_ffn_cn_b3"]),
        "w4": ca(inputs["L0_ffn_cn_w4"]),
        "b4": ca(inputs["L0_ffn_cn_b4"]).reshape(1, 1),
        "gw1": ca(inputs["L0_ffn_gc_w1"]), "gb1": ca(inputs["L0_ffn_gc_b1"]),
        "gw2": ca(inputs["L0_ffn_gc_w2"]), "gb2": ca(inputs["L0_ffn_gc_b2"]),
        "gw3": ca(inputs["L0_ffn_gc_w3"]),
        "gb3": ca(inputs["L0_ffn_gc_b3"]).reshape(1, 1),
        "ar": ca(inputs["L0_ffn_adaptive_range"]).reshape(1, 1),
    }
    in_maps = [dict(shared, x=np.ascontiguousarray(x[i])) for i in range(ncores)]
    res = run_bass_kernel_spmd(nc, in_maps, core_ids=list(range(ncores)))
    global LAST_RESULTS
    LAST_RESULTS = res.results
    out = np.stack([np.asarray(res.results[i]["y"]) for i in range(ncores)], axis=0)
    return out.astype(np.float32)


# version 19
# speedup vs baseline: 1.0010x; 1.0010x over previous
import contextlib
import numpy as np

import concourse.bass as bass
import concourse.mybir as mybir
from concourse.bass_utils import run_bass_kernel_spmd

F32 = mybir.dt.float32
AF = mybir.ActivationFunctionType
ALU = mybir.AluOpType
AX = mybir.AxisListType

STAGE = "full"
DBG = False
NCYC = 3

NTOK, D, H, DH, M = 4096, 1024, 16, 64, 256
NST, TST, NSUB = 8, 512, 4
NRM = float(64.0 ** -0.25)
EPSK = 1e-4
LNEPS = 1e-5
LNEPSK = float(np.log(EPSK))
SIGN = [1, 1, -1, -1, -1, -1, 1, 1]
SIGMA = [4, 5, 6, 7, 0, 1, 2, 3]


class Builder:

    def __init__(self):
        self.ops = []
        self.cnt = dict(T=0, V=0, S=0, G=0, DY=0, DG=0, CC=0)
        self.res = {}
        self.gap_emit = None
        self.s_spacer = None
        self.prev_v_writes = set()
        self._in_gap = False

    def op(self, eng, emit, reads=(), writes=(), ndma=0, cc=False, dup=False):
        if dup:
            self.op(eng, emit, reads=reads, writes=writes, ndma=ndma, cc=cc)
            if eng == "s" and self.s_spacer is not None:
                self._raw_op("s", self.s_spacer, (), ("SSPC",), 0, False)
                self._raw_op("s", self.s_spacer, (), ("SSPC",), 0, False)
            elif eng == "v" and self.gap_emit is not None:
                self._raw_op("v", self.gap_emit, (), ("VDres",), 0, False)
                self._raw_op("v", self.gap_emit, (), ("VDres",), 0, False)
        if eng == "v" and not self._in_gap:
            touched = set(reads) | set(writes)
            if touched & self.prev_v_writes and self.gap_emit is not None:
                self._in_gap = True
                self.op("v", self.gap_emit, reads=(), writes=("VDres",))
                self._in_gap = False
            self.prev_v_writes = set(writes)
        elif eng == "v":
            self.prev_v_writes = set(writes)
        self._raw_op(eng, emit, reads, writes, ndma, cc)

    def _raw_op(self, eng, emit, reads=(), writes=(), ndma=0, cc=False):
        if ndma:
            key, amt = ("DY" if eng == "y" else "DG"), 16 * ndma
        elif cc:
            key, amt = "CC", 1
        else:
            key, amt = dict(t="T", v="V", s="S", g="G", y="DY")[eng], 1
        waits = {}

        def addw(m):
            k, v = m
            if v > waits.get(k, 0):
                waits[k] = v

        for r in reads:
            e = self.res.get(r)
            if e and e["w"] is not None:
                addw(e["w"])
        for r in writes:
            e = self.res.get(r)
            if e:
                if e["w"] is not None:
                    addw(e["w"])
                for m in e["r"]:
                    addw(m)
        self.cnt[key] += amt
        marker = (key, self.cnt[key])
        for r in writes:
            self.res[r] = {"w": marker, "r": []}
        for r in reads:
            self.res.setdefault(r, {"w": None, "r": []})["r"].append(marker)
        self.ops.append((eng, waits, emit, key, amt, tuple(reads), tuple(writes)))


def build_nc(stage):
    nc = bass.Bass("TRN2", target_bir_lowering=False)

    x_h = nc.dram_tensor("x", [NTOK, D], F32, kind="ExternalInput").ap()
    wq_h = nc.dram_tensor("wq", [D, D], F32, kind="ExternalInput").ap()
    wk_h = nc.dram_tensor("wk", [D, D], F32, kind="ExternalInput").ap()
    wv_h = nc.dram_tensor("wv", [D, D], F32, kind="ExternalInput").ap()
    wo_h = nc.dram_tensor("wo", [D, D], F32, kind="ExternalInput").ap()
    bo_h = nc.dram_tensor("bo", [1, D], F32, kind="ExternalInput").ap()
    pj_h = nc.dram_tensor("proj", [M, DH], F32, kind="ExternalInput").ap()
    w1_h = nc.dram_tensor("w1", [D, D], F32, kind="ExternalInput").ap()
    b1_h = nc.dram_tensor("b1", [D], F32, kind="ExternalInput").ap()
    w2_h = nc.dram_tensor("w2", [D, 512], F32, kind="ExternalInput").ap()
    b2_h = nc.dram_tensor("b2", [512], F32, kind="ExternalInput").ap()
    w3_h = nc.dram_tensor("w3", [512, 256], F32, kind="ExternalInput").ap()
    b3_h = nc.dram_tensor("b3", [256], F32, kind="ExternalInput").ap()
    w4_h = nc.dram_tensor("w4", [256, 1], F32, kind="ExternalInput").ap()
    b4_h = nc.dram_tensor("b4", [1, 1], F32, kind="ExternalInput").ap()
    g1_h = nc.dram_tensor("gw1", [D, 512], F32, kind="ExternalInput").ap()
    gb1_h = nc.dram_tensor("gb1", [512], F32, kind="ExternalInput").ap()
    g2_h = nc.dram_tensor("gw2", [512, 256], F32, kind="ExternalInput").ap()
    gb2_h = nc.dram_tensor("gb2", [256], F32, kind="ExternalInput").ap()
    g3_h = nc.dram_tensor("gw3", [256, 1], F32, kind="ExternalInput").ap()
    gb3_h = nc.dram_tensor("gb3", [1, 1], F32, kind="ExternalInput").ap()
    ar_h = nc.dram_tensor("ar", [1, 1], F32, kind="ExternalInput").ap()
    y_h = nc.dram_tensor("y", [NTOK, D], F32, kind="ExternalOutput").ap()
    ydbg_h = nc.dram_tensor("ydbg", [1, 4096], F32, kind="ExternalOutput").ap()
    ydbg2_h = nc.dram_tensor("ydbg2", [128, 16], F32, kind="ExternalOutput").ap()
    x2s_h = nc.dram_tensor("x2scr", [NTOK, D], F32).ap()
    ccin_h = nc.dram_tensor("cc_in", [1, 8], F32).ap()
    ccout_h = nc.dram_tensor("cc_out", [1, 8], F32).ap()

    es = contextlib.ExitStack()
    sb = lambda n, s: es.enter_context(nc.sbuf_tensor(n, s, F32))
    ps = lambda n: es.enter_context(nc.psum_tensor(n, [128, 512], F32))

    BA = sb("BA", [128, 4, 1024])
    BB = sb("BB", [128, 8, 512])
    BC = sb("BC", [128, 4096])
    BD = sb("BD", [128, 8, 512])
    BE = sb("BE", [128, 4096])
    BF = sb("BF", [128, 4096])
    BH = sb("BH", [128, 1536])
    WA = sb("WA", [128, 8, 1024])
    WB = sb("WB", [128, 8, 1024])
    W3S = sb("W3S", [128, 4, 256])
    W4S = sb("W4S", [128, 2, 1])
    GW3 = sb("GW3", [128, 2, 1])
    B1S = sb("B1S", [128, 8])
    B2S = sb("B2S", [128, 4])
    B3S = sb("B3S", [128, 2])
    GB1 = sb("GB1", [128, 4])
    GB2 = sb("GB2", [128, 2])
    IDENT = sb("IDENT", [128, 128])
    PRJCAT = sb("PRJCAT", [128, 512])
    BLK = sb("BLK", [128, 8, 16])
    CTX = sb("CTX", [128, 32 * 66])
    ARENA = sb("ARENA", [1, 1024])
    ROWS = sb("ROWS", [1, 512])
    LN1M = sb("LN1M", [128, 32])
    LN1I = sb("LN1I", [128, 32])
    LN2M = sb("LN2M", [128, 32])
    LN2I = sb("LN2I", [128, 32])
    UMAT = sb("UMAT", [128, 256])
    NSTAB = sb("NSTABm", [128, 16])
    BIASQ = sb("BIASQ", [128, 16])
    NDS = sb("NDS", [16, 512])
    NDT = sb("NDT", [128, 4, 16])
    SS4 = sb("SS4", [128, 4])
    TMP4 = sb("TMP4", [128, 4])
    TMP4B = sb("TMP4B", [128, 4])
    TMP32 = sb("TMP32", [128, 32])
    DEN = sb("DEN", [128, 16])
    DINV = sb("DINV", [128, 16])
    RBD = sb("RBD", [128, 16])
    BCAST = sb("BCAST", [128, 2])
    SVREP = sb("SVREP", [128, 1024])
    BOREP = sb("BOREP", [128, 1024])
    ONESC = sb("ONESC", [128, 8])
    ONES1 = sb("ONES1", [1, 128])
    CSFM = sb("CSFM", [128, 8])
    GH1 = sb("GH1", [128, 4])
    GH2 = sb("GH2", [128, 2])
    PLOAD = sb("PLOAD", [128, 2, 64])
    CONSTS = sb("CONSTS", [128, 4])
    VD = sb("VD", [128, 4])
    SD = sb("SD", [128, 8])
    W4R = sb("W4R", [128, 2, 128])
    GW3R = sb("GW3R", [128, 2, 128])
    ARCOL = sb("ARCOL", [128, 1])
    B4HC = sb("B4HC", [128, 1])
    B3HC = sb("B3HC", [128, 1])
    A2C = sb("A2C", [128, 1])
    CB2C = sb("CB2C", [128, 1])
    T1C = sb("T1C", [128, 1])
    TGC = sb("TGC", [128, 1])
    NEG32 = sb("NEG32", [32, 512])
    P = [ps(f"P{i}") for i in range(8)]
    nc.const_aps.register(F32, LNEPS, CONSTS.ap()[:, 0:1])
    nc.const_aps.register(F32, EPSK, CONSTS.ap()[:, 1:2])
    nc.const_aps.register(F32, LNEPSK, CONSTS.ap()[:, 2:3])
    nc.const_aps.register(F32, 0.1, CONSTS.ap()[:, 3:4])

    d_y = es.enter_context(nc.semaphore("d_y"))
    d_g = es.enter_context(nc.semaphore("d_g"))
    s_t = es.enter_context(nc.semaphore("s_t"))
    s_v = es.enter_context(nc.semaphore("s_v"))
    s_s = es.enter_context(nc.semaphore("s_s"))
    s_g = es.enter_context(nc.semaphore("s_g"))
    s_cc = es.enter_context(nc.semaphore("s_cc"))
    SEM = dict(DY=d_y, DG=d_g, T=s_t, V=s_v, S=s_s, G=s_g, CC=s_cc)

    A_CS, A_GM = 0, 0
    A_TROW, A_CROW = 0, 512
    R_B4, R_GB3, R_B4H, R_B3H = 0, 1, 2, 3
    R_SLOC, R_S2, R_GZ = 4, 8, 9
    R_S8, R_SG, R_BETA, R_BN, R_B2 = 16, 24, 32, 40, 48
    R_AROW, R_A2, R_CB2, R_TG, R_T1 = 56, 64, 72, 80, 88
    R_UR = 128

    bld = Builder()
    OP = bld.op

    bld.gap_emit = lambda v: v.tensor_copy(VD.ap()[:, 0:2], VD.ap()[:, 2:4])
    bld.s_spacer = lambda s: s.copy(SD.ap()[:, 0:2], SD.ap()[:, 2:4])

    def vgap(res="VDres"):
        OP("v", bld.gap_emit, reads=[], writes=[res])

    def sflush():
        OP("s", lambda s: s.activation(SD.ap()[:, 0:4], CONSTS.ap()[:, 0:4],
                                       AF.Copy, accum_out=SD.ap()[:, 4:5]),
           reads=["CONSTS"], writes=["SDres"])

    def ld(name, dst_ap, src_ap, reads=(), extra_writes=(), slow=False):
        def e(eng, _aps=(dst_ap, src_ap), _slow=slow):
            return eng.dma_start(out=_aps[0], in_=_aps[1],
                                 allow_slow_non_contiguous=_slow).then_inc(d_y, 16)
        OP("y", e, reads=list(reads), writes=[name] + list(extra_writes), ndma=1)

    ld("BA", BA.ap()[:, :, :],
       x_h[0:TST, :].rearrange("(j p) d -> p j d", p=128))
    ld("WA", WA.ap()[:, :, :], wk_h.rearrange("(a p) d -> p a d", p=128))
    ld("WB", WB.ap()[:, :, :], wv_h.rearrange("(a p) d -> p a d", p=128))
    ld("PLOAD", PLOAD.ap()[:, :, :], pj_h.rearrange("(a p) d -> p a d", p=128))
    ld("B1S", B1S.ap()[:, :], b1_h.rearrange("(a p) -> p a", p=128), slow=True)
    ld("B2S", B2S.ap()[:, :], b2_h.rearrange("(a p) -> p a", p=128), slow=True)
    ld("B3S", B3S.ap()[:, :], b3_h.rearrange("(a p) -> p a", p=128), slow=True)
    ld("GB1", GB1.ap()[:, :], gb1_h.rearrange("(a p) -> p a", p=128), slow=True)
    ld("GB2", GB2.ap()[:, :], gb2_h.rearrange("(a p) -> p a", p=128), slow=True)
    ld("W4S", W4S.ap()[:, :, :], w4_h.rearrange("(a p) o -> p a o", p=128), slow=True)
    ld("GW3", GW3.ap()[:, :, :], g3_h.rearrange("(a p) o -> p a o", p=128), slow=True)
    ld("GW2", W3S.ap()[:, :, :], g2_h.rearrange("(a p) o -> p a o", p=128))
    ld("BOREP", BOREP.ap()[:, :], bo_h.broadcast_to([128, D]))
    ld("ROWS", ROWS.ap()[0:1, R_B4:R_B4 + 1], b4_h[0:1, 0:1])
    ld("ROWS", ROWS.ap()[0:1, R_GB3:R_GB3 + 1], gb3_h[0:1, 0:1])
    ld("B4C", B4HC.ap()[:, :], b4_h.broadcast_to([128, 1]))
    ld("B3C", B3HC.ap()[:, :], gb3_h.broadcast_to([128, 1]))
    ld("rar", ARCOL.ap()[:, :], ar_h.broadcast_to([128, 1]))

    def init_consts(g):
        g.memset(IDENT[:, :], 0.0)
        g.affine_select(out=IDENT[:, :], in_=IDENT[:, :],
                        compare_op=ALU.not_equal, fill=1.0, base=0,
                        pattern=[[-1, 128]], channel_multiplier=1)
        g.memset(ONESC[:, :], 1.0)
        g.memset(ONES1[:, :], 1.0)
        g.memset(BLK.ap()[:, :, :], 0.0)
        for c in range(8):
            g.memset(BLK.ap()[0:64, c, 2 * c:2 * c + 1], 0.5 * NRM * NRM)
            g.memset(BLK.ap()[64:128, c, 2 * c + 1:2 * c + 2], 0.5 * NRM * NRM)
        g.memset(CTX[:, :], 0.0)
        ctx3i = CTX.ap().rearrange("p (b w) -> p b w", w=66)
        g.memset(ctx3i[:, :, 65:66], 1.0)
        g.memset(ARENA[:, :], 0.0)
        g.memset(CONSTS.ap()[:, 0:1], LNEPS)
        g.memset(CONSTS.ap()[:, 1:2], EPSK)
        g.memset(CONSTS.ap()[:, 2:3], LNEPSK)
        g.memset(CONSTS.ap()[:, 3:4], 0.1)
        g.memset(NEG32[:, :], -1.0)
        g.memset(CSFM[:, :], 0.0)
        return g.memset(PRJCAT[:, :], 0.0)
    OP("g", init_consts, reads=["ROWS", "rar"],
       writes=["IDENT", "ONES", "BLK", "CTX", "CSUM1", "PRJ", "CONSTS", "NEG32"])

    def e(s):
        s.activation(B4HC.ap()[:, :], B4HC.ap()[:, :], AF.Copy, scale=0.5)
        return s.activation(B3HC.ap()[:, :], B3HC.ap()[:, :], AF.Copy, scale=0.5)
    OP("s", e, reads=["B4C", "B3C"], writes=["halves", "B4C", "B3C"])

    def e(t):
        for q in range(2):
            ins = t.matmul(P[0].ap()[0:64, 128 * q:128 * (q + 1)],
                           PLOAD.ap()[:, q, :], IDENT[:, :],
                           is_transpose=True, start=(q == 0), stop=(q == 1))
        return ins
    OP("t", e, reads=["PLOAD", "IDENT"], writes=["P0"], dup=True)
    OP("s", lambda s: s.activation(PRJCAT[0:64, 0:256], P[0].ap()[0:64, 0:256],
                                   AF.Copy, scale=NRM),
       reads=["P0", "PRJ"], writes=["PRJLO_w"], dup=True)
    def e(g):
        return g.dma_start(out=PRJCAT[64:128, 256:512],
                           in_=PRJCAT[0:64, 0:256]).then_inc(d_g, 16)
    OP("g", e, reads=["PRJLO_w", "PRJ"], writes=["PRJHI_w"], ndma=1)
    PRJ_R = ["PRJLO_w", "PRJHI_w"]
    def e(s):
        return s.add_instruction(mybir.InstLoadActFuncSet(
            name=nc.get_next_instruction_name(), act_func_set_id=6, ins=[], outs=[]))
    OP("s", e)

    ST4 = lambda st: st * 4
    bc4 = BC.ap().rearrange("p (j d) -> p j d", j=4)
    ctx3 = CTX.ap().rearrange("p (b w) -> p b w", w=66)
    e32 = BE.ap().rearrange("p (c t) -> p c t", c=32)

    def emit_loadA(st):
        ld("BA", BA.ap()[:, :, :],
           x_h[st * TST:(st + 1) * TST, :].rearrange("(j p) d -> p j d", p=128))

    def emit_lnA(st):
        for j in range(NSUB):
            OP("v", lambda v, j=j, st=st: v.reduce_sum(
                LN1M.ap()[:, ST4(st) + j:ST4(st) + j + 1], BA.ap()[:, j, :], axis=AX.X),
               reads=["BA"], writes=["LN1M"])
        OP("s", lambda s, st=st: s.activation(
            LN1M.ap()[:, ST4(st):ST4(st) + 4], LN1M.ap()[:, ST4(st):ST4(st) + 4],
            AF.Copy, scale=1.0 / D), writes=["LN1M"])
        for j in range(NSUB):
            OP("v", lambda v, j=j, st=st: v.tensor_scalar(
                BA.ap()[:, j, :], BA.ap()[:, j, :],
                LN1M.ap()[:, ST4(st) + j:ST4(st) + j + 1], None, op0=ALU.subtract),
               reads=["LN1M"], writes=["BA"])
            OP("s", lambda s, j=j: s.activation(
                BH.ap()[:, 0:1024], BA.ap()[:, j, :], AF.Square),
               reads=["BA"], writes=["ksq0", "ksq1"])
            OP("v", lambda v, j=j: v.reduce_sum(
                SS4.ap()[:, j:j + 1], BH.ap()[:, 0:1024], axis=AX.X),
               reads=["ksq0"], writes=["SS4"])
        OP("s", lambda s: s.activation(TMP4[:, :], SS4[:, :], AF.Ln,
                                       bias=CONSTS.ap()[:, 0:1], scale=1.0 / D),
           reads=["SS4", "CONSTS"], writes=["TMP4"])
        OP("s", lambda s, st=st: s.activation(
            LN1I.ap()[:, ST4(st):ST4(st) + 4], TMP4[:, :], AF.Exp, scale=-0.5),
           reads=["TMP4"], writes=["LN1I"])
        for j in range(NSUB):
            OP("v", lambda v, j=j, st=st: v.tensor_scalar(
                BA.ap()[:, j, :], BA.ap()[:, j, :],
                LN1I.ap()[:, ST4(st) + j:ST4(st) + j + 1], None, op0=ALU.mult),
               reads=["LN1I"], writes=["BA"])

    emit_loadA(0)
    emit_lnA(0)
    for st in range(NST):
        for c in range(8):
            b = 6 + (c % 2)
            def e(t, c=c, b=b):
                for j in range(NSUB):
                    ins = t.matmul(P[b].ap()[:, 128 * j:128 * (j + 1)],
                                   BA.ap()[:, j, 128 * c:128 * (c + 1)], IDENT[:, :],
                                   is_transpose=True, start=(j == 0), stop=(j == NSUB - 1))
                return ins
            OP("t", e, reads=["BA", "IDENT"], writes=[f"P{b}"])
            OP("v", lambda v, c=c, b=b: v.reduce_sum(
                TMP32.ap()[:, c:c + 1], P[b].ap()[:, :], axis=AX.X),
               reads=[], writes=[f"P{b}", "SXT"])
            OP("s", lambda s, c=c, b=b: s.copy(BB.ap()[:, c, :], P[b].ap()[:, :]),
               reads=[f"P{b}"], writes=["BB"])
        vgap()
        OP("v", lambda v: v.tensor_tensor(CSFM[:, :], CSFM[:, :],
                                          TMP32.ap()[:, 0:8], op=ALU.add),
           reads=["SXT"], writes=["CSFMacc"])
        if st + 1 < NST:
            emit_loadA(st + 1)
            emit_lnA(st + 1)
        for j in range(NSUB):
            for h in range(2):
                b = 6 + h
                def e(t, j=j, h=h, b=b):
                    for ki in range(8):
                        ins = t.matmul(P[b].ap()[:, :],
                                       BB.ap()[:, ki, 128 * j:128 * (j + 1)],
                                       WB.ap()[:, ki, 512 * h:512 * (h + 1)],
                                       start=(ki == 0), stop=(ki == 7))
                    return ins
                OP("t", e, reads=["BB", "WB"], writes=[f"P{b}"])
                OP("s", lambda s, j=j, h=h, b=b: s.copy(
                    bc4[:, j, 512 * h:512 * (h + 1)], P[b].ap()[:, :]),
                   reads=[f"P{b}"], writes=["BC"])
        for o in range(8):
            b = 6 + (o % 2)
            def e(t, o=o, b=b):
                for ki in range(8):
                    ins = t.matmul(P[b].ap()[:, :],
                                   WA.ap()[:, ki, 128 * o:128 * (o + 1)],
                                   BB.ap()[:, ki, :],
                                   start=(ki == 0), stop=(ki == 7))
                return ins
            OP("t", e, reads=["BB", "WA"], writes=[f"P{b}"])
            OP("s", lambda s, o=o, b=b: s.copy(BD.ap()[:, o, :], P[b].ap()[:, :]),
               reads=[f"P{b}"], writes=["BD"])
        for c in range(8):
            kq = f"ksq{c % 2}"
            OP("v", lambda v, c=c: v.tensor_tensor(
                BH.ap()[:, 512 * (c % 2):512 * (c % 2 + 1)],
                BD.ap()[:, c, :], BD.ap()[:, c, :], op=ALU.mult),
               reads=["BD"], writes=[kq])
            OP("t", lambda t, c=c: t.matmul(
                P[5].ap()[0:16, :], BLK.ap()[:, c, :],
                BH.ap()[:, 512 * (c % 2):512 * (c % 2 + 1)],
                start=(c == 0), stop=(c == 7)),
               reads=[kq, "BLK"], writes=["P5"])
        OP("s", lambda s: s.activation(NDS[0:16, :], P[5].ap()[0:16, :],
                                       AF.Copy, scale=-1.0),
           reads=["P5"], writes=["NDS"], dup=True)
        def e(t):
            for j in range(NSUB):
                ins = t.matmul(P[5].ap()[:, 16 * j:16 * (j + 1)],
                               NDS.ap()[0:16, 128 * j:128 * (j + 1)],
                               IDENT[0:16, 0:16],
                               is_transpose=True, start=(j == 0), stop=(j == NSUB - 1))
            return ins
        OP("t", e, reads=["NDS", "IDENT"], writes=["P5"], dup=True)
        OP("s", lambda s: s.copy(NDT.ap()[:, :, :].rearrange("p a b -> p (a b)"),
                                 P[5].ap()[:, 0:64]),
           reads=["P5"], writes=["NDT"], dup=True)
        for j in range(NSUB):
            kbuf = BE if j % 2 == 0 else BF
            kres = f"kp{j % 2}"
            for p in range(8):
                b = 6 + (p % 2)
                def e(t, j=j, p=p, b=b):
                    return t.matmul(P[b].ap()[:, :],
                                    BD.ap()[:, p, 128 * j:128 * (j + 1)],
                                    PRJCAT.ap()[:, :], start=True, stop=True)
                OP("t", e, reads=["BD"] + PRJ_R, writes=[f"P{b}"])
                col = (ST4(st) + j) * 8 + p
                OP("v", lambda v, b=b, col=col: v.reduce_max(
                    UMAT.ap()[:, col:col + 1], P[b].ap()[:, :], axis=AX.X),
                   reads=[f"P{b}"], writes=["UMAT"])
                def e(s, j=j, p=p, b=b, kbuf=kbuf):
                    for i in range(2):
                        ins = s.activation(
                            kbuf.ap()[:, 512 * p + 256 * i:512 * p + 256 * (i + 1)],
                            P[b].ap()[:, 256 * i:256 * (i + 1)], AF.Exp,
                            bias=NDT.ap()[:, j, 2 * p + i:2 * p + i + 1])
                    return ins
                OP("s", e, reads=["NDT"], writes=[f"{kres}_{p}", f"P{b}"])
            for b in range(4):
                def e(t, j=j, b=b, kbuf=kbuf):
                    for cc in range(8):
                        c = 8 * b + cc
                        hd = c // 2
                        t.matmul(P[b].ap()[:, 64 * cc:64 * (cc + 1)],
                                 kbuf.ap()[:, 128 * c:128 * (c + 1)],
                                 bc4[:, j, 64 * hd:64 * (hd + 1)],
                                 start=(j == 0 and cc == 0),
                                 stop=(j == NSUB - 1 and cc == 7))
                        ins = t.matmul(P[4].ap()[:, c:c + 1],
                                       kbuf.ap()[:, 128 * c:128 * (c + 1)],
                                       ONESC.ap()[:, 0:1],
                                       start=(j == 0 and c == 0),
                                       stop=(j == NSUB - 1 and c == 31))
                    return ins
                OP("t", e,
                   reads=[f"{kres}_{2 * b}", f"{kres}_{2 * b + 1}", "BC", "ONES"],
                   writes=[f"P{b}", "P4"])
        for b in range(4):
            OP("v", lambda v, b=b: v.tensor_tensor(
                ctx3[:, 8 * b:8 * (b + 1), 0:64],
                ctx3[:, 8 * b:8 * (b + 1), 0:64],
                P[b].ap().rearrange("p (a d) -> p a d", a=8), op=ALU.add),
               reads=[f"P{b}"], writes=["CTX"])
        OP("v", lambda v: v.tensor_tensor(
            CTX.ap().rearrange("p (b w) -> p b w", w=66)[:, :, 64],
            CTX.ap().rearrange("p (b w) -> p b w", w=66)[:, :, 64],
            P[4].ap()[:, 0:32], op=ALU.add),
           reads=["P4"], writes=["CTX"])

    if stage == "A":
        def fin(y_eng):
            y_eng.wait_ge(d_g, bld.cnt["DG"])
            y_eng.wait_ge(d_y, bld.cnt["DY"])
        return nc, bld, SEM, es, fin

    OP("v", lambda v: v.reduce_max(TMP4.ap()[:, 0:1], UMAT[:, :], axis=AX.X),
       reads=["UMAT"], writes=["TMP4"])
    vgap()
    vgap()
    vgap("BSLK")
    OP("t", lambda t: t.matmul(P[0].ap()[0:1, 0:128], TMP4.ap()[:, 0:1],
                               IDENT[:, :], is_transpose=True, start=True, stop=True),
       reads=["TMP4", "IDENT", "BSLK"], writes=["P0"], dup=True)
    OP("s", lambda s: s.copy(ROWS.ap()[0:1, R_UR:R_UR + 128], P[0].ap()[0:1, 0:128]),
       reads=["P0"], writes=["UROW"], dup=True)
    OP("v", lambda v: v.reduce_max(ROWS.ap()[0:1, R_SLOC:R_SLOC + 1],
                                   ROWS.ap()[0:1, R_UR:R_UR + 128], axis=AX.X),
       reads=["UROW"], writes=["SLOC"], dup=True)
    vgap()
    vgap()
    vgap("BSLK2")
    OP("t", lambda t: t.matmul(P[1].ap()[0:1, 0:8], ROWS.ap()[0:1, R_SLOC:R_SLOC + 1],
                               ONESC.ap()[0:1, 0:8], start=True, stop=True),
       reads=["SLOC", "ONES", "BSLK2"], writes=["P1"], dup=True)
    OP("s", lambda s: s.copy(ROWS.ap()[0:1, R_S8:R_S8 + 8], P[1].ap()[0:1, 0:8]),
       reads=["P1"], writes=["S8"], dup=True)
    def e(g):
        return g.dma_start(out=ccin_h[0:1, 0:8],
                           in_=ROWS.ap()[0:1, R_S8:R_S8 + 8]).then_inc(d_g, 16)
    OP("g", e, reads=["S8"], writes=["ccin"], ndma=1)
    def e(g):
        return g.collective_compute(
            "AllReduce", ALU.max, replica_groups=[list(range(8))],
            ins=[ccin_h[0:1, 0:8]], outs=[ccout_h[0:1, 0:8]])
    OP("g", e, reads=["ccin"], writes=["ccout"], cc=True)
    def e(g):
        return g.dma_start(out=ROWS.ap()[0:1, R_SG:R_SG + 8],
                           in_=ccout_h[0:1, 0:8]).then_inc(d_g, 16)
    OP("g", e, reads=["ccout"], writes=["SG"], ndma=1)
    OP("s", lambda s: s.activation(ROWS.ap()[0:1, R_BETA:R_BETA + 8],
                                   ROWS.ap()[0:1, R_SG:R_SG + 8], AF.Exp,
                                   bias=CONSTS.ap()[0:1, 2:3]),
       reads=["SG", "CONSTS"], writes=["BETA"], dup=True)
    OP("s", lambda s: s.activation(ROWS.ap()[0:1, R_BN:R_BN + 8],
                                    ROWS.ap()[0:1, R_BETA:R_BETA + 8],
                                    AF.Copy, scale=float(NTOK)),
       reads=["BETA"], writes=["BN"], dup=True)
    OP("s", lambda s: s.copy(ROWS.ap()[0:1, R_B2:R_B2 + 1],
                             ROWS.ap()[0:1, R_BETA:R_BETA + 1]),
       reads=["BETA"], writes=["B2"], dup=True)
    OP("s", lambda s: s.copy(ROWS.ap()[0:1, R_B2 + 1:R_B2 + 2],
                             ROWS.ap()[0:1, R_BN:R_BN + 1]),
       reads=["BN"], writes=["B2"], dup=True)
    OP("t", lambda t: t.matmul(P[2].ap()[:, 0:2], ONES1.ap()[0:1, :],
                               ROWS.ap()[0:1, R_B2:R_B2 + 2], start=True, stop=True),
       reads=["B2", "ONES"], writes=["P2"], dup=True)
    OP("s", lambda s: s.copy(BCAST[:, :], P[2].ap()[:, 0:2]),
       reads=["P2"], writes=["BCASTw"], dup=True)
    vgap()
    vgap("ASLK")
    for h in range(2):
        def e(t, h=h):
            for ki in range(8):
                ins = t.matmul(P[4 + h].ap()[0:1, :], CSFM.ap()[:, ki:ki + 1],
                               WB.ap()[:, ki, 512 * h:512 * (h + 1)],
                               start=(ki == 0), stop=(ki == 7))
            return ins
        OP("t", e, reads=["CSFMacc", "WB", "ASLK"], writes=[f"P{4 + h}"], dup=True)
        OP("s", lambda s, h=h: s.copy(
            ARENA.ap()[0:1, A_CS + 512 * h:A_CS + 512 * (h + 1)],
            P[4 + h].ap()[0:1, :]),
           reads=[f"P{4 + h}"], writes=["SVROW", "CSUM1"], dup=True)
    for h in range(2):
        OP("t", lambda t, h=h: t.matmul(
            P[6 + h].ap()[:, :], ONES1.ap()[0:1, :],
            ARENA.ap()[0:1, A_CS + 512 * h:A_CS + 512 * (h + 1)],
            start=True, stop=True),
           reads=["SVROW", "ONES"], writes=[f"P{6 + h}"], dup=True)
        OP("s", lambda s, h=h: s.copy(SVREP.ap()[:, 512 * h:512 * (h + 1)],
                                      P[6 + h].ap()[:, :]),
           reads=[f"P{6 + h}"], writes=["SVREPw"], dup=True)

    if stage in ("AB", "dumpA"):
        if stage == "dumpA":
            dumps = [
                (y_h[0:128, :], CTX.ap()[:, 0:1024], ["CTX"]),
                (y_h[128:256, :], CTX.ap()[:, 1024:2048], ["CTX"]),
                (y_h[256:384, 0:64], CTX.ap()[:, 2048:2112], ["CTX"]),
                (y_h[384:512, 0:32], LN1M.ap()[:, :], ["LN1M"]),
                (y_h[384:512, 32:64], LN1I.ap()[:, :], ["LN1I"]),
                (y_h[512:640, :], SVREP.ap()[:, :], ["SVREPw"]),
                (y_h[640:768, 0:2], BCAST.ap()[:, :], ["BCASTw"]),
                (y_h[640:768, 4:8], SS4.ap()[:, :], ["SS4"]),
                (y_h[640:768, 8:12], TMP4.ap()[:, :], ["TMP4"]),
                (y_h[768:769, 0:512], ROWS.ap()[0:1, :], ["BETA", "BN", "SG", "S8"]),
                (y_h[769:770, 0:1024], ARENA.ap()[0:1, :], ["SVROW"]),
                (y_h[896:1024, 0:256], UMAT.ap()[:, :], ["UMAT"]),
            ]
            for dst, srcap, rds in dumps:
                def e(g, _a=(dst, srcap)):
                    return g.dma_start(out=_a[0], in_=_a[1]).then_inc(d_g, 16)
                OP("g", e, reads=rds, writes=["ydump"], ndma=1)
        def fin(y_eng):
            y_eng.wait_ge(d_g, bld.cnt["DG"])
            y_eng.wait_ge(d_y, bld.cnt["DY"])
        return nc, bld, SEM, es, fin

    ld("WA", WA.ap()[:, :, :], wq_h.rearrange("(a p) d -> p a d", p=128))
    ld("WB", WB.ap()[:, :, :], wo_h.rearrange("(a p) d -> p a d", p=128))

    x2v = BF.ap().rearrange("p (a d) -> p a d", a=4)
    atm = BF.ap()[:, 2048:3072]
    afm = BF.ap()[:, 3072:4096].rearrange("p (a c) -> p a c", a=8)

    for st in range(NST):
        for j in range(NSUB):
            ld(f"BAc{j}", BA.ap()[:, j, :],
               x_h[st * TST + 128 * j:st * TST + 128 * (j + 1), :],
               extra_writes=(["BA"] if st == 0 else []))
        for j in range(NSUB):
            OP("v", lambda v, j=j, st=st: v.tensor_scalar(
                bc4[:, j, :], BA.ap()[:, j, :],
                LN1M.ap()[:, ST4(st) + j:ST4(st) + j + 1],
                LN1I.ap()[:, ST4(st) + j:ST4(st) + j + 1],
                op0=ALU.subtract, op1=ALU.mult),
               reads=[f"BAc{j}", "LN1M", "LN1I"], writes=["BC"])
        for c in range(8):
            b = 6 + (c % 2)
            def e(t, c=c, b=b):
                for j in range(NSUB):
                    ins = t.matmul(P[b].ap()[:, 128 * j:128 * (j + 1)],
                                   bc4[:, j, 128 * c:128 * (c + 1)], IDENT[:, :],
                                   is_transpose=True, start=(j == 0), stop=(j == NSUB - 1))
                return ins
            OP("t", e, reads=["BC", "IDENT"], writes=[f"P{b}"])
            OP("s", lambda s, c=c, b=b: s.copy(BB.ap()[:, c, :], P[b].ap()[:, :]),
               reads=[f"P{b}"], writes=["BB"])
        for o in range(8):
            b = 6 + (o % 2)
            def e(t, o=o, b=b):
                for ki in range(8):
                    ins = t.matmul(P[b].ap()[:, :],
                                   WA.ap()[:, ki, 128 * o:128 * (o + 1)],
                                   BB.ap()[:, ki, :],
                                   start=(ki == 0), stop=(ki == 7))
                return ins
            OP("t", e, reads=["BB", "WA"], writes=[f"P{b}"])
            OP("s", lambda s, o=o, b=b: s.copy(BD.ap()[:, o, :], P[b].ap()[:, :]),
               reads=[f"P{b}"], writes=["BD"])
        for c in range(8):
            kq = f"qsq{c % 2}"
            OP("v", lambda v, c=c: v.tensor_tensor(
                BH.ap()[:, 512 * (c % 2):512 * (c % 2 + 1)],
                BD.ap()[:, c, :], BD.ap()[:, c, :], op=ALU.mult),
               reads=["BD"], writes=[kq, f"ksq{c % 2}"])
            OP("t", lambda t, c=c: t.matmul(
                P[0].ap()[0:16, :], BLK.ap()[:, c, :],
                BH.ap()[:, 512 * (c % 2):512 * (c % 2 + 1)],
                start=(c == 0), stop=(c == 7)),
               reads=[kq, "BLK"], writes=["P0"])
        OP("s", lambda s: s.activation(NDS[0:16, :], P[0].ap()[0:16, :],
                                       AF.Copy, scale=-1.0),
           reads=["P0"], writes=["NDS"], dup=True)
        def e(t):
            for j in range(NSUB):
                ins = t.matmul(P[0].ap()[:, 16 * j:16 * (j + 1)],
                               NDS.ap()[0:16, 128 * j:128 * (j + 1)],
                               IDENT[0:16, 0:16],
                               is_transpose=True, start=(j == 0), stop=(j == NSUB - 1))
            return ins
        OP("t", e, reads=["NDS", "IDENT"], writes=["P0"], dup=True)
        OP("s", lambda s: s.copy(NDT.ap()[:, :, :].rearrange("p a b -> p (a b)"),
                                 P[0].ap()[:, 0:64]),
           reads=["P0"], writes=["NDT"], dup=True)
        for j in range(NSUB):
            for p in range(8):
                b = 6 + (p % 2)
                def e(t, j=j, p=p, b=b):
                    return t.matmul(P[b].ap()[:, :],
                                    BD.ap()[:, p, 128 * j:128 * (j + 1)],
                                    PRJCAT.ap()[:, :], start=True, stop=True)
                OP("t", e, reads=["BD"] + PRJ_R, writes=[f"P{b}"])
                OP("v", lambda v, p=p, b=b: v.reduce_max(
                    NSTAB.ap()[:, 2 * p:2 * p + 2],
                    P[b].ap().rearrange("p (a m) -> p a m", a=2),
                    axis=AX.X, negate=True),
                   reads=[f"P{b}"], writes=["NSTABr"])
                vgap()
                OP("v", lambda v, j=j, p=p: v.tensor_tensor(
                    BIASQ.ap()[:, 2 * p:2 * p + 2], NSTAB.ap()[:, 2 * p:2 * p + 2],
                    NDT.ap()[:, j, 2 * p:2 * p + 2], op=ALU.add),
                   reads=["NSTABr", "NDT"], writes=["BIASQw"])
                def e(s, j=j, p=p, b=b):
                    for i in range(2):
                        ins = s.activation(
                            BC.ap()[:, 512 * p + 256 * i:512 * p + 256 * (i + 1)],
                            P[b].ap()[:, 256 * i:256 * (i + 1)], AF.Exp,
                            bias=BIASQ.ap()[:, 2 * p + i:2 * p + i + 1])
                    return ins
                OP("s", e, reads=["BIASQw"], writes=["BC", f"P{b}"])
            for g8 in range(8):
                b = g8 % 2
                def e(t, g8=g8, b=b):
                    for q in range(4):
                        mc = 4 * g8 + q
                        ins = t.matmul(P[b].ap()[:, 128 * q:128 * (q + 1)],
                                       BC.ap()[:, 128 * mc:128 * (mc + 1)], IDENT[:, :],
                                       is_transpose=True, start=(q == 0), stop=(q == 3))
                    return ins
                OP("t", e, reads=["BC", "IDENT"], writes=[f"P{b}"])
                ew = ["BE"]
                if st == 0 and j == 0 and g8 == 0:
                    ew += [f"kp0_{p}" for p in range(8)]
                OP("s", lambda s, g8=g8, b=b: s.activation(
                    BE.ap()[:, 512 * g8:512 * (g8 + 1)], P[b].ap()[:, :],
                    AF.Copy, bias=EPSK),
                   reads=[f"P{b}", "CONSTS"], writes=ew)
            def e(t):
                for hh in range(16):
                    for q in range(2):
                        kc = 2 * hh + q
                        ins = t.matmul(P[1].ap()[:, 2 * hh:2 * hh + 2],
                                       e32[:, kc, :], ctx3[:, kc, 64:66],
                                       start=(hh == 0 and q == 0),
                                       stop=(hh == 15 and q == 1))
                return ins
            OP("t", e, reads=["BE", "CTX"], writes=["P1"])
            for b2 in (2, 3):
                def e(t, b2=b2):
                    for hh in range(8):
                        h = 8 * (b2 - 2) + hh
                        for q in range(2):
                            kc = 2 * h + q
                            ins = t.matmul(P[b2].ap()[:, 64 * hh:64 * (hh + 1)],
                                           e32[:, kc, :], ctx3[:, kc, 0:64],
                                           start=(hh == 0 and q == 0),
                                           stop=(hh == 7 and q == 1))
                    return ins
                OP("t", e, reads=["BE", "CTX"], writes=[f"P{b2}"])
            OP("s", lambda s: s.copy(TMP32[:, :], P[1].ap()[:, 0:32]),
               reads=["P1"], writes=["P1c"])
            p1s = TMP32.ap().rearrange("p (a b) -> p a b", a=16)
            for b2 in (2, 3):
                hb = 8 * (b2 - 2)
                OP("v", lambda v, hb=hb: v.scalar_tensor_tensor(
                    DEN.ap()[:, hb:hb + 8],
                    p1s[:, hb:hb + 8, 1], BCAST.ap()[:, 1:2],
                    p1s[:, hb:hb + 8, 0], op0=ALU.mult, op1=ALU.add),
                   reads=["P1c", "BCASTw"], writes=["DENw"])
                OP("v", lambda v, hb=hb: v.tensor_scalar(
                    RBD.ap()[:, hb:hb + 8], p1s[:, hb:hb + 8, 1],
                    BCAST.ap()[:, 0:1], None, op0=ALU.mult),
                   reads=["P1c", "BCASTw"], writes=["RBDw"])
            for b2 in (2, 3):
                hb = 8 * (b2 - 2)
                OP("v", lambda v, hb=hb: v.reciprocal(
                    DINV.ap()[:, hb:hb + 8], DEN.ap()[:, hb:hb + 8]),
                   reads=["DENw"], writes=["DINVw"])
            for b2 in (2, 3):
                hb = 8 * (b2 - 2)
                OP("v", lambda v, hb=hb: v.tensor_tensor(
                    RBD.ap()[:, hb:hb + 8], RBD.ap()[:, hb:hb + 8],
                    DINV.ap()[:, hb:hb + 8], op=ALU.mult),
                   reads=["DINVw"], writes=["RBDw"])
            for b2 in (2, 3):
                hb = 8 * (b2 - 2)
                ew = ["BF"]
                if st == 0 and j == 0 and b2 == 2:
                    ew += [f"kp1_{p}" for p in range(8)]
                OP("v", lambda v, hb=hb, b2=b2: v.tensor_tensor(
                    atm[:, 512 * (b2 - 2):512 * (b2 - 1)].rearrange(
                        "p (a d) -> p a d", a=8),
                    P[b2].ap().rearrange("p (a d) -> p a d", a=8),
                    DINV.ap()[:, hb:hb + 8].unsqueeze(2).broadcast_to([128, 8, 64]),
                    op=ALU.mult),
                   reads=[f"P{b2}", "DINVw", "BF"], writes=ew)
            for b2 in (2, 3):
                hb = 8 * (b2 - 2)
                t2off = 1024 if b2 == 2 else 512
                OP("v", lambda v, hb=hb, b2=b2, t2off=t2off: v.tensor_tensor(
                    BH.ap()[:, t2off:t2off + 512].rearrange(
                        "p (a d) -> p a d", a=8),
                    SVREP.ap()[:, 512 * (b2 - 2):512 * (b2 - 1)].rearrange(
                        "p (a d) -> p a d", a=8),
                    RBD.ap()[:, hb:hb + 8].unsqueeze(2).broadcast_to([128, 8, 64]),
                    op=ALU.mult),
                   reads=["SVREPw", "RBDw"], writes=["T2w", "qsq1"])
            for b2 in (2, 3):
                t2off = 1024 if b2 == 2 else 512
                OP("v", lambda v, b2=b2, t2off=t2off: v.tensor_tensor(
                    atm[:, 512 * (b2 - 2):512 * (b2 - 1)],
                    atm[:, 512 * (b2 - 2):512 * (b2 - 1)],
                    BH.ap()[:, t2off:t2off + 512],
                    op=ALU.add),
                   reads=["T2w", "BF"], writes=["BF"])
            vgap()
            vgap("CSLK")
            for g2 in range(2):
                b = g2
                def e(t, g2=g2, b=b):
                    for q in range(4):
                        c = 4 * g2 + q
                        ins = t.matmul(P[b].ap()[:, 128 * q:128 * (q + 1)],
                                       atm[:, 128 * c:128 * (c + 1)], IDENT[:, :],
                                       is_transpose=True, start=(q == 0), stop=(q == 3))
                    return ins
                OP("t", e, reads=["BF", "IDENT", "CSLK"], writes=[f"P{b}"])
                OP("s", lambda s, g2=g2, b=b: s.copy(
                    afm[:, 4 * g2:4 * (g2 + 1), :].rearrange("p a c -> p (a c)"),
                    P[b].ap()[:, :]),
                   reads=[f"P{b}", "BF"], writes=["BF"])
            for h in range(2):
                b = 2 + h
                def e(t, h=h, b=b):
                    for ki in range(8):
                        ins = t.matmul(P[b].ap()[:, :], afm[:, ki, :],
                                       WB.ap()[:, ki, 512 * h:512 * (h + 1)],
                                       start=(ki == 0), stop=(ki == 7))
                    return ins
                OP("t", e, reads=["BF", "WB"], writes=[f"P{b}"])
            OP("g", lambda g, j=j: g.tensor_tensor(
                BF.ap()[:, 2048:3072], BA.ap()[:, j, :], BOREP[:, :], op=ALU.add),
               reads=[f"BAc{j}", "BOREP", "BF"], writes=["BF"])
            for h in range(2):
                b = 2 + h
                OP("v", lambda v, j=j, h=h, b=b: v.tensor_tensor(
                    x2v[:, j % 2, 512 * h:512 * (h + 1)],
                    BF.ap()[:, 2048 + 512 * h:2048 + 512 * (h + 1)],
                    P[b].ap()[:, :], op=ALU.add),
                   reads=[f"P{b}", "BF"], writes=["BF"])
            vgap()
            vgap("CSLK2")
            def e(g, j=j, st=st):
                return g.dma_start(
                    out=(y_h if stage == "x2" else x2s_h)[
                        st * TST + 128 * j:st * TST + 128 * (j + 1), :],
                    in_=x2v[:, j % 2, :]).then_inc(d_g, 16)
            OP("g", e, reads=["BF"], writes=["x2st"], ndma=1)
            idx = ST4(st) + j
            vgap()
            OP("v", lambda v, j=j, idx=idx: v.reduce_sum(
                LN2M.ap()[:, idx:idx + 1], x2v[:, j % 2, :], axis=AX.X),
               reads=["BF"], writes=["LN2M"])
            OP("s", lambda s, j=j: s.activation(
                BH.ap()[:, 0:1024], x2v[:, j % 2, :], AF.Square),
               reads=["BF"], writes=["qsq0", "qsq1", "ksq0", "ksq1"])
            OP("v", lambda v, j=j: v.reduce_sum(
                SS4.ap()[:, j:j + 1], BH.ap()[:, 0:1024], axis=AX.X),
               reads=["qsq0"], writes=["SS4"])
            OP("s", lambda s, idx=idx: s.activation(
                LN2M.ap()[:, idx:idx + 1], LN2M.ap()[:, idx:idx + 1], AF.Copy,
                scale=1.0 / D),
               reads=[], writes=["LN2M"])
            OP("s", lambda s, idx=idx: s.activation(
                TMP4B.ap()[:, 0:1], LN2M.ap()[:, idx:idx + 1], AF.Square),
               reads=["LN2M"], writes=["TMP4B"])
            OP("s", lambda s: s.activation(
                TMP4B.ap()[:, 1:2], TMP4B.ap()[:, 0:1], AF.Copy,
                scale=-1.0, bias=LNEPS),
               reads=["CONSTS"], writes=["TMP4B"])
            OP("s", lambda s, j=j: s.activation(
                TMP4B.ap()[:, 2:3], SS4.ap()[:, j:j + 1], AF.Ln, scale=1.0 / D,
                bias=TMP4B.ap()[:, 1:2]),
               reads=["SS4"], writes=["TMP4B"])
            OP("s", lambda s, idx=idx: s.activation(
                LN2I.ap()[:, idx:idx + 1], TMP4B.ap()[:, 2:3], AF.Exp, scale=-0.5),
               reads=["TMP4B"], writes=["LN2I"])
            for h in range(2):
                OP("t", lambda t, st=st, j=j, h=h, idx=idx: t.matmul(
                    P[4 + h].ap()[0:1, :], LN2I.ap()[:, idx:idx + 1],
                    x2v[:, j % 2, 512 * h:512 * (h + 1)],
                    start=(st == 0 and j == 0), stop=False),
                   reads=["LN2I", "BF", "CSLK2"], writes=[f"P{4 + h}"])

    if stage == "x2":
        def fin(y_eng):
            y_eng.wait_ge(d_g, bld.cnt["DG"])
            y_eng.wait_ge(d_y, bld.cnt["DY"])
        return nc, bld, SEM, es, fin

    def e(v):
        for ki in range(2):
            v.tensor_copy(W4R.ap()[:, ki, :],
                          W4S.ap()[:, ki, 0:1].broadcast_to([128, 128]))
            ins = v.tensor_copy(GW3R.ap()[:, ki, :],
                                GW3.ap()[:, ki, 0:1].broadcast_to([128, 128]))
        return ins
    OP("v", e, reads=["W4S", "GW3"], writes=["W4R", "GW3R"], dup=True)

    def e(t):
        for j in range(32):
            ins = t.matmul(P[0].ap()[0:1, 0:1], LN2M.ap()[:, j:j + 1],
                           LN2I.ap()[:, j:j + 1], start=(j == 0), stop=(j == 31))
        return ins
    OP("t", e, reads=["LN2M", "LN2I"], writes=["P0"], dup=True)
    OP("s", lambda s: s.copy(ROWS.ap()[0:1, R_S2:R_S2 + 1], P[0].ap()[0:1, 0:1]),
       reads=["P0"], writes=["S2col"], dup=True)
    for h in range(2):
        OP("t", lambda t, h=h: t.matmul(
            P[4 + h].ap()[0:1, :], ROWS.ap()[0:1, R_S2:R_S2 + 1],
            NEG32.ap()[0:1, :], start=False, stop=True),
           reads=["S2col", "NEG32"], writes=[f"P{4 + h}"])
    for h in range(2):
        OP("s", lambda s, h=h: s.activation(
            ARENA.ap()[0:1, 512 * h:512 * (h + 1)], P[4 + h].ap()[0:1, :],
            AF.Copy, scale=1.0 / NTOK),
           reads=[f"P{4 + h}"], writes=["GM", "SVROW", "CSUM1"], dup=True)
    def e(t):
        for q in range(8):
            ins = t.matmul(P[1].ap()[:, q:q + 1],
                           ARENA.ap()[0:1, 128 * q:128 * (q + 1)],
                           IDENT[0:1, 0:1], is_transpose=True,
                           start=(q == 0), stop=(q == 7))
        return ins
    OP("t", e, reads=["GM", "IDENT"], writes=["P1"], dup=True)
    OP("s", lambda s: s.copy(CSFM[:, :], P[1].ap()[:, 0:8]),
       reads=["P1"], writes=["CSFMw", "CSFMacc"], dup=True)

    ld("WA", WA.ap()[:, :, :], w1_h.rearrange("(a p) d -> p a d", p=128))
    ld("WB2", WB.ap()[:, :, 0:512], w2_h.rearrange("(a p) d -> p a d", p=128),
       extra_writes=["WB"])
    ld("WBg", WB.ap()[:, :, 512:1024], g1_h.rearrange("(a p) d -> p a d", p=128),
       extra_writes=["WB"])

    def e(s):
        return s.add_instruction(mybir.InstLoadActFuncSet(
            name=nc.get_next_instruction_name(), act_func_set_id=10, ins=[], outs=[]))
    OP("s", e)

    def e(t):
        for o in range(4):
            for ki in range(8):
                ins = t.matmul(P[2].ap()[:, o:o + 1],
                               WB.ap()[:, ki, 512 + 128 * o:512 + 128 * (o + 1)],
                               CSFM.ap()[:, ki:ki + 1],
                               start=(o == 0 and ki == 0), stop=(o == 3 and ki == 7))
        return ins
    OP("t", e, reads=["CSFMw", "WBg"], writes=["P2"], dup=True)
    for o in range(4):
        OP("s", lambda s, o=o: s.activation(GH1.ap()[:, o:o + 1], P[2].ap()[:, o:o + 1],
                                            AF.Gelu, bias=GB1.ap()[:, o:o + 1]),
           reads=["P2", "GB1"], writes=["GH1w"], dup=True)
    def e(t):
        for o in range(2):
            for ki in range(4):
                ins = t.matmul(P[3].ap()[:, o:o + 1],
                               W3S.ap()[:, ki, 128 * o:128 * (o + 1)],
                               GH1.ap()[:, ki:ki + 1],
                               start=(o == 0 and ki == 0), stop=(o == 1 and ki == 3))
        return ins
    OP("t", e, reads=["GH1w", "GW2"], writes=["P3"], dup=True)
    for o in range(2):
        OP("s", lambda s, o=o: s.activation(GH2.ap()[:, o:o + 1], P[3].ap()[:, o:o + 1],
                                            AF.Gelu, bias=GB2.ap()[:, o:o + 1]),
           reads=["P3", "GB2"], writes=["GH2w"], dup=True)
    def e(t):
        for ki in range(2):
            ins = t.matmul(P[4].ap()[:, 0:1], GW3R.ap()[:, ki, :],
                           GH2.ap()[:, ki:ki + 1], start=(ki == 0), stop=(ki == 1))
        return ins
    OP("t", e, reads=["GH2w", "GW3R"], writes=["P4"], dup=True)
    OP("s", lambda s: s.activation(TGC.ap()[:, :], P[4].ap()[:, 0:1], AF.Tanh,
                                   scale=0.5, bias=B3HC.ap()[:, 0:1]),
       reads=["P4", "B3C"], writes=["TGc"], dup=True)
    OP("t", lambda t: t.matmul(P[5].ap()[0:1, 0:128], ARCOL.ap()[0:1, 0:1],
                               ONES1.ap()[0:1, :], start=True, stop=True),
       reads=["rar", "ONES"], writes=["P5"], dup=True)
    OP("s", lambda s: s.copy(ROWS.ap()[0:1, R_UR:R_UR + 128],
                             P[5].ap()[0:1, 0:128]),
       reads=["P5"], writes=["UROW"], dup=True)
    OP("t", lambda t: t.matmul(P[5].ap()[:, 0:1], ROWS.ap()[0:1, R_UR:R_UR + 128],
                               TGC.ap()[0:1, 0:1], start=True, stop=True),
       reads=["UROW", "TGc"], writes=["P5"], dup=True)
    OP("s", lambda s: s.activation(CB2C.ap()[:, :], P[5].ap()[:, 0:1], AF.Copy,
                                   scale=0.7, bias=0.1),
       reads=["P5", "CONSTS"], writes=["CB2c"], dup=True)
    OP("s", lambda s: s.activation(A2C.ap()[:, :], ARCOL.ap()[:, :], AF.Copy,
                                   scale=0.3),
       reads=["rar"], writes=["A2c"], dup=True)

    ld("W3S", W3S.ap()[:, :, :], w3_h.rearrange("(a p) o -> p a o", p=128),
       extra_writes=["GW2"])

    be4 = BE.ap().rearrange("p (j d) -> p j d", j=4)
    h2v = BE.ap()[:, 0:2048].rearrange("p (a d) -> p a d", a=4)
    h3v = BE.ap()[:, 2048:3072].rearrange("p (a d) -> p a d", a=2)
    crrep = BE.ap()[:, 3072:3584]
    bd2 = BD.ap().rearrange("p a d -> p (a d)")

    for st in range(NST):
        xres = "BA" if st % 2 == 0 else "BF"
        xv4 = (BA.ap()[:, :, :] if st % 2 == 0
               else BF.ap().rearrange("p (j d) -> p j d", j=4))
        ld(xres, xv4[:, :, :],
           x2s_h[st * TST:(st + 1) * TST, :].rearrange("(j p) d -> p j d", p=128),
           reads=["x2st"],
           extra_writes=(["BAc0", "BAc1", "BAc2", "BAc3"] if st == 0 else []))
        for j in range(NSUB):
            OP("v", lambda v, j=j, st=st, xv4=xv4: v.tensor_scalar(
                be4[:, j, :], xv4[:, j, :],
                LN2M.ap()[:, ST4(st) + j:ST4(st) + j + 1],
                LN2I.ap()[:, ST4(st) + j:ST4(st) + j + 1],
                op0=ALU.subtract, op1=ALU.mult),
               reads=[xres, "LN2M", "LN2I"], writes=["BE"])
        vgap()
        vgap("VSLK")
        for c in range(8):
            b = c % 2
            def e(t, c=c, b=b):
                for j in range(NSUB):
                    ins = t.matmul(P[b].ap()[:, 128 * j:128 * (j + 1)],
                                   be4[:, j, 128 * c:128 * (c + 1)], IDENT[:, :],
                                   is_transpose=True, start=(j == 0), stop=(j == NSUB - 1))
                return ins
            OP("t", e, reads=["BE", "IDENT", "VSLK"], writes=[f"P{b}"])
            OP("s", lambda s, c=c, b=b: s.copy(BB.ap()[:, c, :], P[b].ap()[:, :]),
               reads=[f"P{b}"], writes=["BB"])
        cur, curres = BB, "BB"
        for cyc in range(NCYC):
            nxt = BC if cur is BB else BB
            nxtres = "BC" if curres == "BB" else "BB"
            curv = cur.ap() if cur is BB else cur.ap().rearrange("p (a d) -> p a d", a=8)
            nxtv = nxt.ap() if nxt is BB else nxt.ap().rearrange("p (a d) -> p a d", a=8)
            for o in range(8):
                b = 2 + (o % 2)
                def e(t, o=o, b=b, curv=curv):
                    for ki in range(8):
                        ins = t.matmul(P[b].ap()[:, :],
                                       WA.ap()[:, ki, 128 * o:128 * (o + 1)],
                                       curv[:, ki, :], start=(ki == 0), stop=(ki == 7))
                    return ins
                OP("t", e, reads=[curres, "WA", "VSLK"], writes=[f"P{b}"])
                OP("s", lambda s, o=o, b=b: s.activation(
                    BD.ap()[:, o, :], P[b].ap()[:, :], AF.Gelu,
                    bias=B1S.ap()[:, o:o + 1]),
                   reads=[f"P{b}", "B1S"], writes=["BD"])
            for o in range(4):
                b = 4 + (o % 2)
                def e(t, o=o, b=b):
                    for ki in range(8):
                        ins = t.matmul(P[b].ap()[:, :],
                                       WB.ap()[:, ki, 128 * o:128 * (o + 1)],
                                       BD.ap()[:, ki, :], start=(ki == 0), stop=(ki == 7))
                    return ins
                OP("t", e, reads=["BD", "WB2"], writes=[f"P{b}"])
                OP("s", lambda s, o=o, b=b: s.activation(
                    h2v[:, o, :], P[b].ap()[:, :], AF.Gelu,
                    bias=B2S.ap()[:, o:o + 1]),
                   reads=[f"P{b}", "B2S"], writes=["BE"])
            for o in range(2):
                b3k = 6 if o == 0 else 0
                def e(t, o=o, b3k=b3k):
                    for ki in range(4):
                        ins = t.matmul(P[b3k].ap()[:, :],
                                       W3S.ap()[:, ki, 128 * o:128 * (o + 1)],
                                       h2v[:, ki, :], start=(ki == 0), stop=(ki == 3))
                    return ins
                OP("t", e, reads=["BE", "W3S"], writes=[f"P{b3k}"])
                OP("s", lambda s, o=o, b3k=b3k: s.activation(
                    h3v[:, o, :], P[b3k].ap()[:, :], AF.Gelu,
                    bias=B3S.ap()[:, o:o + 1]),
                   reads=[f"P{b3k}", "B3S"], writes=["BE"])
            def e(t):
                for ki in range(2):
                    ins = t.matmul(P[7].ap()[:, :], W4R.ap()[:, ki, :],
                                   h3v[:, ki, :], start=(ki == 0), stop=(ki == 1))
                return ins
            OP("t", e, reads=["BE", "W4R"], writes=["P7"])
            OP("s", lambda s: s.activation(crrep, P[7].ap()[:, :], AF.Tanh,
                                           scale=0.5, bias=B4HC.ap()[:, 0:1]),
               reads=["P7", "B4C"], writes=["BE"])
            OP("v", lambda v: v.tensor_scalar(crrep, crrep, A2C.ap()[:, 0:1],
                                              CB2C.ap()[:, 0:1],
                                              op0=ALU.mult, op1=ALU.add),
               reads=["BE", "A2c", "CB2c"], writes=["BE"])
            vgap()
            vgap()
            for c in range(8):
                OP("v", lambda v, c=c, curv=curv, nxtv=nxtv: v.tensor_tensor(
                    nxtv[:, c, :], crrep, curv[:, SIGMA[c], :], op=ALU.mult),
                   reads=[curres, "BE"], writes=[nxtres])
            for c in range(8):
                OP("v", lambda v, c=c, curv=curv, nxtv=nxtv: v.tensor_tensor(
                    nxtv[:, c, :], curv[:, c, :], nxtv[:, c, :],
                    op=(ALU.add if SIGN[c] > 0 else ALU.subtract)),
                   reads=[curres], writes=[nxtres])
            vgap()
            vgap()
            vgap("VSLK")
            cur, curres = nxt, nxtres
        curv = cur.ap() if cur is BB else cur.ap().rearrange("p (a d) -> p a d", a=8)
        for j in range(NSUB):
            for g2 in range(2):
                b = g2
                def e(t, j=j, g2=g2, b=b, curv=curv):
                    for q in range(4):
                        c = 4 * g2 + q
                        ins = t.matmul(P[b].ap()[:, 128 * q:128 * (q + 1)],
                                       curv[:, c, 128 * j:128 * (j + 1)], IDENT[:, :],
                                       is_transpose=True, start=(q == 0), stop=(q == 3))
                    return ins
                OP("t", e, reads=[curres, "IDENT", "VSLK"], writes=[f"P{b}"])
                OP("v", lambda v, j=j, g2=g2, b=b, xv4=xv4: v.tensor_tensor(
                    bd2[:, 1024 * (j % 2) + 512 * g2:1024 * (j % 2) + 512 * (g2 + 1)],
                    xv4[:, j, 512 * g2:512 * (g2 + 1)],
                    P[b].ap()[:, :], op=ALU.add),
                   reads=[f"P{b}", xres], writes=["BD"])
            def e(g, j=j, st=st):
                return g.dma_start(
                    out=y_h[st * TST + 128 * j:st * TST + 128 * (j + 1), :],
                    in_=bd2[:, 1024 * (j % 2):1024 * (j % 2 + 1)]).then_inc(d_g, 16)
            OP("g", e, reads=["BD"], writes=["yst"], ndma=1)

    def e(g):
        g.dma_start(out=ydbg_h[0:1, 0:1024], in_=ARENA.ap()[0:1, :]).then_inc(d_g, 16)
        g.dma_start(out=ydbg_h[0:1, 1024:1536], in_=ROWS.ap()[0:1, :]).then_inc(d_g, 16)
        g.dma_start(out=ydbg_h[0:1, 1536:2048],
                    in_=BE.ap()[0:1, 3072:3584]).then_inc(d_g, 16)
        g.dma_start(out=ydbg_h[0:1, 2048:2056],
                    in_=CSFM.ap()[0:1, 0:8]).then_inc(d_g, 16)
        g.dma_start(out=ydbg_h[0:1, 2056:2057],
                    in_=TGC.ap()[0:1, 0:1]).then_inc(d_g, 16)
        g.dma_start(out=ydbg_h[0:1, 2057:2058],
                    in_=A2C.ap()[0:1, 0:1]).then_inc(d_g, 16)
        g.dma_start(out=ydbg_h[0:1, 2058:2059],
                    in_=CB2C.ap()[0:1, 0:1]).then_inc(d_g, 16)
        g.dma_start(out=ydbg_h[0:1, 2059:2063],
                    in_=GH1.ap()[0:1, 0:4]).then_inc(d_g, 16)
        return g.dma_start(out=ydbg_h[0:1, 2063:2065],
                           in_=GH2.ap()[0:1, 0:2]).then_inc(d_g, 16)
    OP("g", e, reads=["GM", "TGc", "CB2c", "A2c", "BE", "CSFMw", "GH1w", "GH2w"],
       writes=["ydbg"], ndma=9)
    if DBG:
        def e(g):
            g.dma_start(out=ydbg2_h[:, 0:4], in_=GH1.ap()[:, :],
                        allow_slow_non_contiguous=True).then_inc(d_g, 16)
            g.dma_start(out=ydbg2_h[:, 4:6], in_=GH2.ap()[:, :],
                        allow_slow_non_contiguous=True).then_inc(d_g, 16)
            g.dma_start(out=ydbg2_h[:, 6:7], in_=TGC.ap()[:, :],
                        allow_slow_non_contiguous=True).then_inc(d_g, 16)
            g.dma_start(out=ydbg2_h[:, 7:8], in_=CB2C.ap()[:, :],
                        allow_slow_non_contiguous=True).then_inc(d_g, 16)
            g.dma_start(out=ydbg2_h[:, 8:9], in_=GW3R.ap()[:, 0, 0:1],
                        allow_slow_non_contiguous=True).then_inc(d_g, 16)
            g.dma_start(out=ydbg2_h[:, 9:10], in_=GW3R.ap()[:, 1, 0:1],
                        allow_slow_non_contiguous=True).then_inc(d_g, 16)
            g.dma_start(out=ydbg2_h[:, 10:11], in_=GW3R.ap()[:, 0, 127:128],
                        allow_slow_non_contiguous=True).then_inc(d_g, 16)
            g.dma_start(out=ydbg2_h[:, 11:12], in_=W4R.ap()[:, 0, 0:1],
                        allow_slow_non_contiguous=True).then_inc(d_g, 16)
            g.dma_start(out=ydbg2_h[:, 12:13], in_=W4R.ap()[:, 1, 0:1],
                        allow_slow_non_contiguous=True).then_inc(d_g, 16)
            g.dma_start(out=ydbg2_h[:, 13:14], in_=GW3.ap()[:, 0, 0:1],
                        allow_slow_non_contiguous=True).then_inc(d_g, 16)
            g.dma_start(out=ydbg2_h[:, 14:15], in_=GW3.ap()[:, 1, 0:1],
                        allow_slow_non_contiguous=True).then_inc(d_g, 16)
            return g.dma_start(out=ydbg2_h[:, 15:16], in_=W4S.ap()[:, 0, 0:1],
                               allow_slow_non_contiguous=True).then_inc(d_g, 16)
        OP("g", e, reads=["GM", "TGc", "CB2c", "A2c", "GH1w", "GH2w",
                          "W4R", "GW3R", "W4S", "GW3"],
           writes=["ydbg2"], ndma=12)

    def fin(y_eng):
        y_eng.wait_ge(d_g, bld.cnt["DG"])
        y_eng.wait_ge(d_y, bld.cnt["DY"])
    return nc, bld, SEM, es, fin


def emit(nc, bld, SEM, es, fin, nops=None):
    own = dict(t="T", v="V", s="S", g="G")
    ops = bld.ops if nops is None else bld.ops[:nops]
    if nops is not None:
        pc = dict(DY=0, DG=0)
        for (_e, _w2, _f, key, amt, _r, _w3) in ops:
            if key in pc:
                pc[key] += amt
        def fin(y_eng, _pc=pc):
            y_eng.wait_ge(SEM["DG"], _pc["DG"])
            y_eng.wait_ge(SEM["DY"], _pc["DY"])
    with es:
        with nc.Block() as block:
            def run(eng_name):
                def body(eng):
                    hw = {}
                    for (e, waits, emit_fn, key, amt, _r, _w) in ops:
                        if e != eng_name:
                            continue
                        for k, v in sorted(waits.items()):
                            if k == own.get(eng_name):
                                continue
                            if hw.get(k, 0) >= v:
                                continue
                            eng.wait_ge(SEM[k], v)
                            hw[k] = v
                        last = emit_fn(eng)
                        if key not in ("DY", "DG"):
                            last.then_inc(SEM[key], amt)
                    if eng_name == "y":
                        fin(eng)
                return body
            block.sync(run("y"))
            block.tensor(run("t"))
            block.vector(run("v"))
            block.scalar(run("s"))
            block.gpsimd(run("g"))
    return nc


NOPS = None
LAST_RESULTS = None

def kernel(**inputs):
    ncores = 8
    nc, bld, SEM, es, fin = build_nc(STAGE)
    emit(nc, bld, SEM, es, fin, nops=NOPS)

    f32 = np.float32
    ca = lambda a: np.ascontiguousarray(np.asarray(a, dtype=f32))
    x = ca(inputs["L0_attn_x"])
    shared = {
        "wq": ca(inputs["L0_attn_wq"]), "wk": ca(inputs["L0_attn_wk"]),
        "wv": ca(inputs["L0_attn_wv"]), "wo": ca(inputs["L0_attn_wo"]),
        "bo": ca(inputs["L0_attn_bo"]).reshape(1, D),
        "proj": ca(inputs["L0_attn_proj"]),
        "w1": ca(inputs["L0_ffn_cn_w1"]), "b1": ca(inputs["L0_ffn_cn_b1"]),
        "w2": ca(inputs["L0_ffn_cn_w2"]), "b2": ca(inputs["L0_ffn_cn_b2"]),
        "w3": ca(inputs["L0_ffn_cn_w3"]), "b3": ca(inputs["L0_ffn_cn_b3"]),
        "w4": ca(inputs["L0_ffn_cn_w4"]),
        "b4": ca(inputs["L0_ffn_cn_b4"]).reshape(1, 1),
        "gw1": ca(inputs["L0_ffn_gc_w1"]), "gb1": ca(inputs["L0_ffn_gc_b1"]),
        "gw2": ca(inputs["L0_ffn_gc_w2"]), "gb2": ca(inputs["L0_ffn_gc_b2"]),
        "gw3": ca(inputs["L0_ffn_gc_w3"]),
        "gb3": ca(inputs["L0_ffn_gc_b3"]).reshape(1, 1),
        "ar": ca(inputs["L0_ffn_adaptive_range"]).reshape(1, 1),
    }
    in_maps = [dict(shared, x=np.ascontiguousarray(x[i])) for i in range(ncores)]
    res = run_bass_kernel_spmd(nc, in_maps, core_ids=list(range(ncores)))
    global LAST_RESULTS
    LAST_RESULTS = res.results
    out = np.stack([np.asarray(res.results[i]["y"]) for i in range(ncores)], axis=0)
    return out.astype(np.float32)


# version 20
# speedup vs baseline: 1.0133x; 1.0122x over previous
import contextlib
import numpy as np

import concourse.bass as bass
import concourse.mybir as mybir
from concourse.bass_utils import run_bass_kernel_spmd

F32 = mybir.dt.float32
AF = mybir.ActivationFunctionType
ALU = mybir.AluOpType
AX = mybir.AxisListType

STAGE = "full"
DBG = False
NCYC = 3

NTOK, D, H, DH, M = 4096, 1024, 16, 64, 256
NST, TST, NSUB = 8, 512, 4
NRM = float(64.0 ** -0.25)
EPSK = 1e-4
LNEPS = 1e-5
LNEPSK = float(np.log(EPSK))
SIGN = [1, 1, -1, -1, -1, -1, 1, 1]
SIGMA = [4, 5, 6, 7, 0, 1, 2, 3]


class Builder:

    def __init__(self):
        self.ops = []
        self.cnt = dict(T=0, V=0, S=0, G=0, DY=0, DG=0, CC=0)
        self.res = {}
        self.gap_emit = None
        self.s_spacer = None
        self.prev_v_writes = set()
        self._in_gap = False

    def op(self, eng, emit, reads=(), writes=(), ndma=0, cc=False, dup=False):
        if dup:
            self.op(eng, emit, reads=reads, writes=writes, ndma=ndma, cc=cc)
            if eng == "s" and self.s_spacer is not None:
                self._raw_op("s", self.s_spacer, (), ("SSPC",), 0, False)
                self._raw_op("s", self.s_spacer, (), ("SSPC",), 0, False)
            elif eng == "v" and self.gap_emit is not None:
                self._raw_op("v", self.gap_emit, (), ("VDres",), 0, False)
                self._raw_op("v", self.gap_emit, (), ("VDres",), 0, False)
        if eng == "v" and not self._in_gap:
            touched = set(reads) | set(writes)
            if touched & self.prev_v_writes and self.gap_emit is not None:
                self._in_gap = True
                self.op("v", self.gap_emit, reads=(), writes=("VDres",))
                self._in_gap = False
            self.prev_v_writes = set(writes)
        elif eng == "v":
            self.prev_v_writes = set(writes)
        self._raw_op(eng, emit, reads, writes, ndma, cc)

    def _raw_op(self, eng, emit, reads=(), writes=(), ndma=0, cc=False):
        if ndma:
            key, amt = ("DY" if eng == "y" else "DG"), 16 * ndma
        elif cc:
            key, amt = "CC", 1
        else:
            key, amt = dict(t="T", v="V", s="S", g="G", y="DY")[eng], 1
        waits = {}

        def addw(m):
            k, v = m
            if v > waits.get(k, 0):
                waits[k] = v

        for r in reads:
            e = self.res.get(r)
            if e and e["w"] is not None:
                addw(e["w"])
        for r in writes:
            e = self.res.get(r)
            if e:
                if e["w"] is not None:
                    addw(e["w"])
                for m in e["r"]:
                    addw(m)
        self.cnt[key] += amt
        marker = (key, self.cnt[key])
        for r in writes:
            self.res[r] = {"w": marker, "r": []}
        for r in reads:
            self.res.setdefault(r, {"w": None, "r": []})["r"].append(marker)
        self.ops.append((eng, waits, emit, key, amt, tuple(reads), tuple(writes)))


def build_nc(stage):
    nc = bass.Bass("TRN2", target_bir_lowering=False)

    x_h = nc.dram_tensor("x", [NTOK, D], F32, kind="ExternalInput").ap()
    wq_h = nc.dram_tensor("wq", [D, D], F32, kind="ExternalInput").ap()
    wk_h = nc.dram_tensor("wk", [D, D], F32, kind="ExternalInput").ap()
    wv_h = nc.dram_tensor("wv", [D, D], F32, kind="ExternalInput").ap()
    wo_h = nc.dram_tensor("wo", [D, D], F32, kind="ExternalInput").ap()
    bo_h = nc.dram_tensor("bo", [1, D], F32, kind="ExternalInput").ap()
    pj_h = nc.dram_tensor("proj", [M, DH], F32, kind="ExternalInput").ap()
    w1_h = nc.dram_tensor("w1", [D, D], F32, kind="ExternalInput").ap()
    b1_h = nc.dram_tensor("b1", [D], F32, kind="ExternalInput").ap()
    w2_h = nc.dram_tensor("w2", [D, 512], F32, kind="ExternalInput").ap()
    b2_h = nc.dram_tensor("b2", [512], F32, kind="ExternalInput").ap()
    w3_h = nc.dram_tensor("w3", [512, 256], F32, kind="ExternalInput").ap()
    b3_h = nc.dram_tensor("b3", [256], F32, kind="ExternalInput").ap()
    w4_h = nc.dram_tensor("w4", [256, 1], F32, kind="ExternalInput").ap()
    b4_h = nc.dram_tensor("b4", [1, 1], F32, kind="ExternalInput").ap()
    g1_h = nc.dram_tensor("gw1", [D, 512], F32, kind="ExternalInput").ap()
    gb1_h = nc.dram_tensor("gb1", [512], F32, kind="ExternalInput").ap()
    g2_h = nc.dram_tensor("gw2", [512, 256], F32, kind="ExternalInput").ap()
    gb2_h = nc.dram_tensor("gb2", [256], F32, kind="ExternalInput").ap()
    g3_h = nc.dram_tensor("gw3", [256, 1], F32, kind="ExternalInput").ap()
    gb3_h = nc.dram_tensor("gb3", [1, 1], F32, kind="ExternalInput").ap()
    ar_h = nc.dram_tensor("ar", [1, 1], F32, kind="ExternalInput").ap()
    y_h = nc.dram_tensor("y", [NTOK, D], F32, kind="ExternalOutput").ap()
    ydbg_h = nc.dram_tensor("ydbg", [1, 4096], F32, kind="ExternalOutput").ap()
    ydbg2_h = nc.dram_tensor("ydbg2", [128, 16], F32, kind="ExternalOutput").ap()
    x2s_h = nc.dram_tensor("x2scr", [NTOK, D], F32).ap()
    ccin_h = nc.dram_tensor("cc_in", [1, 8], F32).ap()
    ccout_h = nc.dram_tensor("cc_out", [1, 8], F32).ap()

    es = contextlib.ExitStack()
    sb = lambda n, s: es.enter_context(nc.sbuf_tensor(n, s, F32))
    ps = lambda n: es.enter_context(nc.psum_tensor(n, [128, 512], F32))

    BA = sb("BA", [128, 4, 1024])
    BB = sb("BB", [128, 8, 512])
    BC = sb("BC", [128, 4096])
    BD = sb("BD", [128, 8, 512])
    BE = sb("BE", [128, 4096])
    BF = sb("BF", [128, 4096])
    BH = sb("BH", [128, 1536])
    WA = sb("WA", [128, 8, 1024])
    WB = sb("WB", [128, 8, 1024])
    W3S = sb("W3S", [128, 4, 256])
    W4S = sb("W4S", [128, 2, 1])
    GW3 = sb("GW3", [128, 2, 1])
    B1S = sb("B1S", [128, 8])
    B2S = sb("B2S", [128, 4])
    B3S = sb("B3S", [128, 2])
    GB1 = sb("GB1", [128, 4])
    GB2 = sb("GB2", [128, 2])
    IDENT = sb("IDENT", [128, 128])
    PRJCAT = sb("PRJCAT", [128, 512])
    BLK = sb("BLK", [128, 8, 16])
    CTX = sb("CTX", [128, 32 * 66])
    ARENA = sb("ARENA", [1, 1024])
    ROWS = sb("ROWS", [1, 512])
    LN1M = sb("LN1M", [128, 32])
    LN1I = sb("LN1I", [128, 32])
    LN2M = sb("LN2M", [128, 32])
    LN2I = sb("LN2I", [128, 32])
    UMAT = sb("UMAT", [128, 256])
    NSTAB = sb("NSTABm", [128, 16])
    BIASQ = sb("BIASQ", [128, 16])
    NDS = sb("NDS", [16, 512])
    NDT = sb("NDT", [128, 4, 16])
    SS4 = sb("SS4", [128, 4])
    TMP4 = sb("TMP4", [128, 4])
    TMP4B = sb("TMP4B", [128, 4])
    TMP32 = sb("TMP32", [128, 32])
    DEN = sb("DEN", [128, 16])
    DINV = sb("DINV", [128, 16])
    RBD = sb("RBD", [128, 16])
    BCAST = sb("BCAST", [128, 2])
    SVREP = sb("SVREP", [128, 1024])
    BOREP = sb("BOREP", [128, 1024])
    ONESC = sb("ONESC", [128, 8])
    ONES1 = sb("ONES1", [1, 128])
    CSFM = sb("CSFM", [128, 8])
    GH1 = sb("GH1", [128, 4])
    GH2 = sb("GH2", [128, 2])
    PLOAD = sb("PLOAD", [128, 2, 64])
    CONSTS = sb("CONSTS", [128, 4])
    VD = sb("VD", [128, 4])
    SD = sb("SD", [128, 8])
    W4R = sb("W4R", [128, 2, 128])
    GW3R = sb("GW3R", [128, 2, 128])
    ARCOL = sb("ARCOL", [128, 1])
    B4HC = sb("B4HC", [128, 1])
    B3HC = sb("B3HC", [128, 1])
    A2C = sb("A2C", [128, 1])
    CB2C = sb("CB2C", [128, 1])
    T1C = sb("T1C", [128, 1])
    TGC = sb("TGC", [128, 1])
    NEG32 = sb("NEG32", [32, 512])
    P = [ps(f"P{i}") for i in range(8)]
    nc.const_aps.register(F32, LNEPS, CONSTS.ap()[:, 0:1])
    nc.const_aps.register(F32, EPSK, CONSTS.ap()[:, 1:2])
    nc.const_aps.register(F32, LNEPSK, CONSTS.ap()[:, 2:3])
    nc.const_aps.register(F32, 0.1, CONSTS.ap()[:, 3:4])

    d_y = es.enter_context(nc.semaphore("d_y"))
    d_g = es.enter_context(nc.semaphore("d_g"))
    s_t = es.enter_context(nc.semaphore("s_t"))
    s_v = es.enter_context(nc.semaphore("s_v"))
    s_s = es.enter_context(nc.semaphore("s_s"))
    s_g = es.enter_context(nc.semaphore("s_g"))
    s_cc = es.enter_context(nc.semaphore("s_cc"))
    SEM = dict(DY=d_y, DG=d_g, T=s_t, V=s_v, S=s_s, G=s_g, CC=s_cc)

    A_CS, A_GM = 0, 0
    A_TROW, A_CROW = 0, 512
    R_B4, R_GB3, R_B4H, R_B3H = 0, 1, 2, 3
    R_SLOC, R_S2, R_GZ = 4, 8, 9
    R_S8, R_SG, R_BETA, R_BN, R_B2 = 16, 24, 32, 40, 48
    R_AROW, R_A2, R_CB2, R_TG, R_T1 = 56, 64, 72, 80, 88
    R_UR = 128

    bld = Builder()
    OP = bld.op

    bld.gap_emit = lambda v: v.tensor_copy(VD.ap()[:, 0:2], VD.ap()[:, 2:4])
    bld.s_spacer = lambda s: s.copy(SD.ap()[:, 0:2], SD.ap()[:, 2:4])

    def vgap(res="VDres"):
        OP("v", bld.gap_emit, reads=[], writes=[res])

    def sflush():
        OP("s", lambda s: s.activation(SD.ap()[:, 0:4], CONSTS.ap()[:, 0:4],
                                       AF.Copy, accum_out=SD.ap()[:, 4:5]),
           reads=["CONSTS"], writes=["SDres"])

    def ld(name, dst_ap, src_ap, reads=(), extra_writes=(), slow=False):
        def e(eng, _aps=(dst_ap, src_ap), _slow=slow):
            return eng.dma_start(out=_aps[0], in_=_aps[1],
                                 allow_slow_non_contiguous=_slow).then_inc(d_y, 16)
        OP("y", e, reads=list(reads), writes=[name] + list(extra_writes), ndma=1)

    ld("BA", BA.ap()[:, :, :],
       x_h[0:TST, :].rearrange("(j p) d -> p j d", p=128))
    ld("WA", WA.ap()[:, :, :], wk_h.rearrange("(a p) d -> p a d", p=128))
    ld("WB", WB.ap()[:, :, :], wv_h.rearrange("(a p) d -> p a d", p=128))
    ld("PLOAD", PLOAD.ap()[:, :, :], pj_h.rearrange("(a p) d -> p a d", p=128))
    ld("B1S", B1S.ap()[:, :], b1_h.rearrange("(a p) -> p a", p=128), slow=True)
    ld("B2S", B2S.ap()[:, :], b2_h.rearrange("(a p) -> p a", p=128), slow=True)
    ld("B3S", B3S.ap()[:, :], b3_h.rearrange("(a p) -> p a", p=128), slow=True)
    ld("GB1", GB1.ap()[:, :], gb1_h.rearrange("(a p) -> p a", p=128), slow=True)
    ld("GB2", GB2.ap()[:, :], gb2_h.rearrange("(a p) -> p a", p=128), slow=True)
    ld("W4S", W4S.ap()[:, :, :], w4_h.rearrange("(a p) o -> p a o", p=128), slow=True)
    ld("GW3", GW3.ap()[:, :, :], g3_h.rearrange("(a p) o -> p a o", p=128), slow=True)
    ld("GW2", W3S.ap()[:, :, :], g2_h.rearrange("(a p) o -> p a o", p=128))
    ld("BOREP", BOREP.ap()[:, :], bo_h.broadcast_to([128, D]))
    ld("ROWS", ROWS.ap()[0:1, R_B4:R_B4 + 1], b4_h[0:1, 0:1])
    ld("ROWS", ROWS.ap()[0:1, R_GB3:R_GB3 + 1], gb3_h[0:1, 0:1])
    ld("B4C", B4HC.ap()[:, :], b4_h.broadcast_to([128, 1]))
    ld("B3C", B3HC.ap()[:, :], gb3_h.broadcast_to([128, 1]))
    ld("rar", ARCOL.ap()[:, :], ar_h.broadcast_to([128, 1]))

    def init_consts(g):
        g.memset(IDENT[:, :], 0.0)
        g.affine_select(out=IDENT[:, :], in_=IDENT[:, :],
                        compare_op=ALU.not_equal, fill=1.0, base=0,
                        pattern=[[-1, 128]], channel_multiplier=1)
        g.memset(ONESC[:, :], 1.0)
        g.memset(ONES1[:, :], 1.0)
        g.memset(BLK.ap()[:, :, :], 0.0)
        for c in range(8):
            g.memset(BLK.ap()[0:64, c, 2 * c:2 * c + 1], 0.5 * NRM * NRM)
            g.memset(BLK.ap()[64:128, c, 2 * c + 1:2 * c + 2], 0.5 * NRM * NRM)
        g.memset(CTX[:, :], 0.0)
        ctx3i = CTX.ap().rearrange("p (b w) -> p b w", w=66)
        g.memset(ctx3i[:, :, 65:66], 1.0)
        g.memset(ARENA[:, :], 0.0)
        g.memset(CONSTS.ap()[:, 0:1], LNEPS)
        g.memset(CONSTS.ap()[:, 1:2], EPSK)
        g.memset(CONSTS.ap()[:, 2:3], LNEPSK)
        g.memset(CONSTS.ap()[:, 3:4], 0.1)
        g.memset(NEG32[:, :], -1.0)
        g.memset(CSFM[:, :], 0.0)
        return g.memset(PRJCAT[:, :], 0.0)
    OP("g", init_consts, reads=["ROWS", "rar"],
       writes=["IDENT", "ONES", "BLK", "CTX", "CSUM1", "PRJ", "CONSTS", "NEG32"])

    def e(s):
        s.activation(B4HC.ap()[:, :], B4HC.ap()[:, :], AF.Copy, scale=0.5)
        return s.activation(B3HC.ap()[:, :], B3HC.ap()[:, :], AF.Copy, scale=0.5)
    OP("s", e, reads=["B4C", "B3C"], writes=["halves", "B4C", "B3C"])

    def e(t):
        for q in range(2):
            ins = t.matmul(P[0].ap()[0:64, 128 * q:128 * (q + 1)],
                           PLOAD.ap()[:, q, :], IDENT[:, :],
                           is_transpose=True, start=(q == 0), stop=(q == 1))
        return ins
    OP("t", e, reads=["PLOAD", "IDENT"], writes=["P0"], dup=True)
    OP("s", lambda s: s.activation(PRJCAT[0:64, 0:256], P[0].ap()[0:64, 0:256],
                                   AF.Copy, scale=NRM),
       reads=["P0", "PRJ"], writes=["PRJLO_w"], dup=True)
    def e(g):
        return g.dma_start(out=PRJCAT[64:128, 256:512],
                           in_=PRJCAT[0:64, 0:256]).then_inc(d_g, 16)
    OP("g", e, reads=["PRJLO_w", "PRJ"], writes=["PRJHI_w"], ndma=1)
    PRJ_R = ["PRJLO_w", "PRJHI_w"]
    def e(s):
        return s.add_instruction(mybir.InstLoadActFuncSet(
            name=nc.get_next_instruction_name(), act_func_set_id=6, ins=[], outs=[]))
    OP("s", e)

    ST4 = lambda st: st * 4
    bc4 = BC.ap().rearrange("p (j d) -> p j d", j=4)
    ctx3 = CTX.ap().rearrange("p (b w) -> p b w", w=66)
    e32 = BE.ap().rearrange("p (c t) -> p c t", c=32)

    def emit_loadA(st):
        ld("BA", BA.ap()[:, :, :],
           x_h[st * TST:(st + 1) * TST, :].rearrange("(j p) d -> p j d", p=128))

    def emit_lnA(st):
        for j in range(NSUB):
            OP("v", lambda v, j=j, st=st: v.reduce_sum(
                LN1M.ap()[:, ST4(st) + j:ST4(st) + j + 1], BA.ap()[:, j, :], axis=AX.X),
               reads=["BA"], writes=["LN1M"])
        OP("s", lambda s, st=st: s.activation(
            LN1M.ap()[:, ST4(st):ST4(st) + 4], LN1M.ap()[:, ST4(st):ST4(st) + 4],
            AF.Copy, scale=1.0 / D), writes=["LN1M"])
        for j in range(NSUB):
            OP("v", lambda v, j=j, st=st: v.tensor_scalar(
                BA.ap()[:, j, :], BA.ap()[:, j, :],
                LN1M.ap()[:, ST4(st) + j:ST4(st) + j + 1], None, op0=ALU.subtract),
               reads=["LN1M"], writes=["BA"])
            OP("s", lambda s, j=j: s.activation(
                BH.ap()[:, 0:1024], BA.ap()[:, j, :], AF.Square),
               reads=["BA"], writes=["ksq0", "ksq1"])
            OP("v", lambda v, j=j: v.reduce_sum(
                SS4.ap()[:, j:j + 1], BH.ap()[:, 0:1024], axis=AX.X),
               reads=["ksq0"], writes=["SS4"])
        OP("s", lambda s: s.activation(TMP4[:, :], SS4[:, :], AF.Ln,
                                       bias=CONSTS.ap()[:, 0:1], scale=1.0 / D),
           reads=["SS4", "CONSTS"], writes=["TMP4"])
        OP("s", lambda s, st=st: s.activation(
            LN1I.ap()[:, ST4(st):ST4(st) + 4], TMP4[:, :], AF.Exp, scale=-0.5),
           reads=["TMP4"], writes=["LN1I"])
        for j in range(NSUB):
            OP("v", lambda v, j=j, st=st: v.tensor_scalar(
                BA.ap()[:, j, :], BA.ap()[:, j, :],
                LN1I.ap()[:, ST4(st) + j:ST4(st) + j + 1], None, op0=ALU.mult),
               reads=["LN1I"], writes=["BA"])

    emit_loadA(0)
    emit_lnA(0)
    for st in range(NST):
        for c in range(8):
            b = 6 + (c % 2)
            def e(t, c=c, b=b):
                for j in range(NSUB):
                    ins = t.matmul(P[b].ap()[:, 128 * j:128 * (j + 1)],
                                   BA.ap()[:, j, 128 * c:128 * (c + 1)], IDENT[:, :],
                                   is_transpose=True, start=(j == 0), stop=(j == NSUB - 1))
                return ins
            OP("t", e, reads=["BA", "IDENT"], writes=[f"P{b}"])
            OP("v", lambda v, c=c, b=b: v.reduce_sum(
                TMP32.ap()[:, c:c + 1], P[b].ap()[:, :], axis=AX.X),
               reads=[], writes=[f"P{b}", "SXT"])
            OP("s", lambda s, c=c, b=b: s.copy(BB.ap()[:, c, :], P[b].ap()[:, :]),
               reads=[f"P{b}"], writes=["BB"])
        vgap()
        OP("v", lambda v: v.tensor_tensor(CSFM[:, :], CSFM[:, :],
                                          TMP32.ap()[:, 0:8], op=ALU.add),
           reads=["SXT"], writes=["CSFMacc"])
        if st + 1 < NST:
            emit_loadA(st + 1)
            emit_lnA(st + 1)
        for j in range(NSUB):
            for h in range(2):
                b = 6 + h
                def e(t, j=j, h=h, b=b):
                    for ki in range(8):
                        ins = t.matmul(P[b].ap()[:, :],
                                       BB.ap()[:, ki, 128 * j:128 * (j + 1)],
                                       WB.ap()[:, ki, 512 * h:512 * (h + 1)],
                                       start=(ki == 0), stop=(ki == 7))
                    return ins
                OP("t", e, reads=["BB", "WB"], writes=[f"P{b}"])
                OP("s", lambda s, j=j, h=h, b=b: s.copy(
                    bc4[:, j, 512 * h:512 * (h + 1)], P[b].ap()[:, :]),
                   reads=[f"P{b}"], writes=["BC"])
        for o in range(8):
            b = 6 + (o % 2)
            def e(t, o=o, b=b):
                for ki in range(8):
                    ins = t.matmul(P[b].ap()[:, :],
                                   WA.ap()[:, ki, 128 * o:128 * (o + 1)],
                                   BB.ap()[:, ki, :],
                                   start=(ki == 0), stop=(ki == 7))
                return ins
            OP("t", e, reads=["BB", "WA"], writes=[f"P{b}"])
            OP("s", lambda s, o=o, b=b: s.copy(BD.ap()[:, o, :], P[b].ap()[:, :]),
               reads=[f"P{b}"], writes=["BD"])
        for c in range(8):
            kq = f"ksq{c % 2}"
            OP("v", lambda v, c=c: v.tensor_tensor(
                BH.ap()[:, 512 * (c % 2):512 * (c % 2 + 1)],
                BD.ap()[:, c, :], BD.ap()[:, c, :], op=ALU.mult),
               reads=["BD"], writes=[kq])
            OP("t", lambda t, c=c: t.matmul(
                P[5].ap()[0:16, :], BLK.ap()[:, c, :],
                BH.ap()[:, 512 * (c % 2):512 * (c % 2 + 1)],
                start=(c == 0), stop=(c == 7)),
               reads=[kq, "BLK"], writes=["P5"])
        OP("s", lambda s: s.activation(NDS[0:16, :], P[5].ap()[0:16, :],
                                       AF.Copy, scale=-1.0),
           reads=["P5"], writes=["NDS"], dup=True)
        def e(t):
            for j in range(NSUB):
                ins = t.matmul(P[5].ap()[:, 16 * j:16 * (j + 1)],
                               NDS.ap()[0:16, 128 * j:128 * (j + 1)],
                               IDENT[0:16, 0:16],
                               is_transpose=True, start=(j == 0), stop=(j == NSUB - 1))
            return ins
        OP("t", e, reads=["NDS", "IDENT"], writes=["P5"], dup=True)
        OP("s", lambda s: s.copy(NDT.ap()[:, :, :].rearrange("p a b -> p (a b)"),
                                 P[5].ap()[:, 0:64]),
           reads=["P5"], writes=["NDT"], dup=True)
        for j in range(NSUB):
            kbuf = BE if j % 2 == 0 else BF
            kres = f"kp{j % 2}"
            for p in range(8):
                b = 6 + (p % 2)
                def e(t, j=j, p=p, b=b):
                    return t.matmul(P[b].ap()[:, :],
                                    BD.ap()[:, p, 128 * j:128 * (j + 1)],
                                    PRJCAT.ap()[:, :], start=True, stop=True)
                OP("t", e, reads=["BD"] + PRJ_R, writes=[f"P{b}"])
                col = (ST4(st) + j) * 8 + p
                OP("v", lambda v, b=b, col=col: v.reduce_max(
                    UMAT.ap()[:, col:col + 1], P[b].ap()[:, :], axis=AX.X),
                   reads=[f"P{b}"], writes=["UMAT"])
                def e(s, j=j, p=p, b=b, kbuf=kbuf):
                    for i in range(2):
                        ins = s.activation(
                            kbuf.ap()[:, 512 * p + 256 * i:512 * p + 256 * (i + 1)],
                            P[b].ap()[:, 256 * i:256 * (i + 1)], AF.Exp,
                            bias=NDT.ap()[:, j, 2 * p + i:2 * p + i + 1])
                    return ins
                OP("s", e, reads=["NDT"], writes=[f"{kres}_{p}", f"P{b}"])
            for b in range(4):
                def e(t, j=j, b=b, kbuf=kbuf):
                    for cc in range(8):
                        c = 8 * b + cc
                        hd = c // 2
                        t.matmul(P[b].ap()[:, 64 * cc:64 * (cc + 1)],
                                 kbuf.ap()[:, 128 * c:128 * (c + 1)],
                                 bc4[:, j, 64 * hd:64 * (hd + 1)],
                                 start=(j == 0 and cc == 0),
                                 stop=(j == NSUB - 1 and cc == 7))
                        ins = t.matmul(P[4].ap()[:, c:c + 1],
                                       kbuf.ap()[:, 128 * c:128 * (c + 1)],
                                       ONESC.ap()[:, 0:1],
                                       start=(j == 0 and c == 0),
                                       stop=(j == NSUB - 1 and c == 31))
                    return ins
                OP("t", e,
                   reads=[f"{kres}_{2 * b}", f"{kres}_{2 * b + 1}", "BC", "ONES"],
                   writes=[f"P{b}", "P4"])
        for b in range(4):
            OP("v", lambda v, b=b: v.tensor_tensor(
                ctx3[:, 8 * b:8 * (b + 1), 0:64],
                ctx3[:, 8 * b:8 * (b + 1), 0:64],
                P[b].ap().rearrange("p (a d) -> p a d", a=8), op=ALU.add),
               reads=[f"P{b}"], writes=["CTX"])
        OP("v", lambda v: v.tensor_tensor(
            CTX.ap().rearrange("p (b w) -> p b w", w=66)[:, :, 64],
            CTX.ap().rearrange("p (b w) -> p b w", w=66)[:, :, 64],
            P[4].ap()[:, 0:32], op=ALU.add),
           reads=["P4"], writes=["CTX"])

    if stage == "A":
        def fin(y_eng):
            y_eng.wait_ge(d_g, bld.cnt["DG"])
            y_eng.wait_ge(d_y, bld.cnt["DY"])
        return nc, bld, SEM, es, fin

    OP("v", lambda v: v.reduce_max(TMP4.ap()[:, 0:1], UMAT[:, :], axis=AX.X),
       reads=["UMAT"], writes=["TMP4"])
    vgap()
    vgap()
    vgap("BSLK")
    OP("t", lambda t: t.matmul(P[0].ap()[0:1, 0:128], TMP4.ap()[:, 0:1],
                               IDENT[:, :], is_transpose=True, start=True, stop=True),
       reads=["TMP4", "IDENT", "BSLK"], writes=["P0"], dup=True)
    OP("s", lambda s: s.copy(ROWS.ap()[0:1, R_UR:R_UR + 128], P[0].ap()[0:1, 0:128]),
       reads=["P0"], writes=["UROW"], dup=True)
    OP("v", lambda v: v.reduce_max(ROWS.ap()[0:1, R_SLOC:R_SLOC + 1],
                                   ROWS.ap()[0:1, R_UR:R_UR + 128], axis=AX.X),
       reads=["UROW"], writes=["SLOC"], dup=True)
    vgap()
    vgap()
    vgap("BSLK2")
    OP("t", lambda t: t.matmul(P[1].ap()[0:1, 0:8], ROWS.ap()[0:1, R_SLOC:R_SLOC + 1],
                               ONESC.ap()[0:1, 0:8], start=True, stop=True),
       reads=["SLOC", "ONES", "BSLK2"], writes=["P1"], dup=True)
    OP("s", lambda s: s.copy(ROWS.ap()[0:1, R_S8:R_S8 + 8], P[1].ap()[0:1, 0:8]),
       reads=["P1"], writes=["S8"], dup=True)
    def e(g):
        return g.dma_start(out=ccin_h[0:1, 0:8],
                           in_=ROWS.ap()[0:1, R_S8:R_S8 + 8]).then_inc(d_g, 16)
    OP("g", e, reads=["S8"], writes=["ccin"], ndma=1)
    def e(g):
        return g.collective_compute(
            "AllReduce", ALU.max, replica_groups=[list(range(8))],
            ins=[ccin_h[0:1, 0:8]], outs=[ccout_h[0:1, 0:8]])
    OP("g", e, reads=["ccin"], writes=["ccout"], cc=True)
    def e(g):
        return g.dma_start(out=ROWS.ap()[0:1, R_SG:R_SG + 8],
                           in_=ccout_h[0:1, 0:8]).then_inc(d_g, 16)
    OP("g", e, reads=["ccout"], writes=["SG"], ndma=1)
    OP("s", lambda s: s.activation(ROWS.ap()[0:1, R_BETA:R_BETA + 8],
                                   ROWS.ap()[0:1, R_SG:R_SG + 8], AF.Exp,
                                   bias=CONSTS.ap()[0:1, 2:3]),
       reads=["SG", "CONSTS"], writes=["BETA"], dup=True)
    OP("s", lambda s: s.activation(ROWS.ap()[0:1, R_BN:R_BN + 8],
                                    ROWS.ap()[0:1, R_BETA:R_BETA + 8],
                                    AF.Copy, scale=float(NTOK)),
       reads=["BETA"], writes=["BN"], dup=True)
    OP("s", lambda s: s.copy(ROWS.ap()[0:1, R_B2:R_B2 + 1],
                             ROWS.ap()[0:1, R_BETA:R_BETA + 1]),
       reads=["BETA"], writes=["B2"], dup=True)
    OP("s", lambda s: s.copy(ROWS.ap()[0:1, R_B2 + 1:R_B2 + 2],
                             ROWS.ap()[0:1, R_BN:R_BN + 1]),
       reads=["BN"], writes=["B2"], dup=True)
    OP("t", lambda t: t.matmul(P[2].ap()[:, 0:2], ONES1.ap()[0:1, :],
                               ROWS.ap()[0:1, R_B2:R_B2 + 2], start=True, stop=True),
       reads=["B2", "ONES"], writes=["P2"], dup=True)
    OP("s", lambda s: s.copy(BCAST[:, :], P[2].ap()[:, 0:2]),
       reads=["P2"], writes=["BCASTw"], dup=True)
    vgap()
    vgap("ASLK")
    for h in range(2):
        def e(t, h=h):
            for ki in range(8):
                ins = t.matmul(P[4 + h].ap()[0:1, :], CSFM.ap()[:, ki:ki + 1],
                               WB.ap()[:, ki, 512 * h:512 * (h + 1)],
                               start=(ki == 0), stop=(ki == 7))
            return ins
        OP("t", e, reads=["CSFMacc", "WB", "ASLK"], writes=[f"P{4 + h}"], dup=True)
        OP("s", lambda s, h=h: s.copy(
            ARENA.ap()[0:1, A_CS + 512 * h:A_CS + 512 * (h + 1)],
            P[4 + h].ap()[0:1, :]),
           reads=[f"P{4 + h}"], writes=["SVROW", "CSUM1"], dup=True)
    for h in range(2):
        OP("t", lambda t, h=h: t.matmul(
            P[6 + h].ap()[:, :], ONES1.ap()[0:1, :],
            ARENA.ap()[0:1, A_CS + 512 * h:A_CS + 512 * (h + 1)],
            start=True, stop=True),
           reads=["SVROW", "ONES"], writes=[f"P{6 + h}"], dup=True)
        OP("s", lambda s, h=h: s.copy(SVREP.ap()[:, 512 * h:512 * (h + 1)],
                                      P[6 + h].ap()[:, :]),
           reads=[f"P{6 + h}"], writes=["SVREPw"], dup=True)

    if stage in ("AB", "dumpA"):
        if stage == "dumpA":
            dumps = [
                (y_h[0:128, :], CTX.ap()[:, 0:1024], ["CTX"]),
                (y_h[128:256, :], CTX.ap()[:, 1024:2048], ["CTX"]),
                (y_h[256:384, 0:64], CTX.ap()[:, 2048:2112], ["CTX"]),
                (y_h[384:512, 0:32], LN1M.ap()[:, :], ["LN1M"]),
                (y_h[384:512, 32:64], LN1I.ap()[:, :], ["LN1I"]),
                (y_h[512:640, :], SVREP.ap()[:, :], ["SVREPw"]),
                (y_h[640:768, 0:2], BCAST.ap()[:, :], ["BCASTw"]),
                (y_h[640:768, 4:8], SS4.ap()[:, :], ["SS4"]),
                (y_h[640:768, 8:12], TMP4.ap()[:, :], ["TMP4"]),
                (y_h[768:769, 0:512], ROWS.ap()[0:1, :], ["BETA", "BN", "SG", "S8"]),
                (y_h[769:770, 0:1024], ARENA.ap()[0:1, :], ["SVROW"]),
                (y_h[896:1024, 0:256], UMAT.ap()[:, :], ["UMAT"]),
            ]
            for dst, srcap, rds in dumps:
                def e(g, _a=(dst, srcap)):
                    return g.dma_start(out=_a[0], in_=_a[1]).then_inc(d_g, 16)
                OP("g", e, reads=rds, writes=["ydump"], ndma=1)
        def fin(y_eng):
            y_eng.wait_ge(d_g, bld.cnt["DG"])
            y_eng.wait_ge(d_y, bld.cnt["DY"])
        return nc, bld, SEM, es, fin

    ld("WA", WA.ap()[:, :, :], wq_h.rearrange("(a p) d -> p a d", p=128))
    ld("WB", WB.ap()[:, :, :], wo_h.rearrange("(a p) d -> p a d", p=128))

    x2v = BF.ap().rearrange("p (a d) -> p a d", a=4)
    atm = BF.ap()[:, 2048:3072]
    afm = BF.ap()[:, 3072:4096].rearrange("p (a c) -> p a c", a=8)

    for st in range(NST):
        for j in range(NSUB):
            ld(f"BAc{j}", BA.ap()[:, j, :],
               x_h[st * TST + 128 * j:st * TST + 128 * (j + 1), :],
               extra_writes=(["BA"] if st == 0 else []))
        for j in range(NSUB):
            OP("v", lambda v, j=j, st=st: v.tensor_scalar(
                bc4[:, j, :], BA.ap()[:, j, :],
                LN1M.ap()[:, ST4(st) + j:ST4(st) + j + 1],
                LN1I.ap()[:, ST4(st) + j:ST4(st) + j + 1],
                op0=ALU.subtract, op1=ALU.mult),
               reads=[f"BAc{j}", "LN1M", "LN1I"], writes=["BC"])
        for c in range(8):
            b = 6 + (c % 2)
            def e(t, c=c, b=b):
                for j in range(NSUB):
                    ins = t.matmul(P[b].ap()[:, 128 * j:128 * (j + 1)],
                                   bc4[:, j, 128 * c:128 * (c + 1)], IDENT[:, :],
                                   is_transpose=True, start=(j == 0), stop=(j == NSUB - 1))
                return ins
            OP("t", e, reads=["BC", "IDENT"], writes=[f"P{b}"])
            OP("s", lambda s, c=c, b=b: s.copy(BB.ap()[:, c, :], P[b].ap()[:, :]),
               reads=[f"P{b}"], writes=["BB"])
        for o in range(8):
            b = 6 + (o % 2)
            def e(t, o=o, b=b):
                for ki in range(8):
                    ins = t.matmul(P[b].ap()[:, :],
                                   WA.ap()[:, ki, 128 * o:128 * (o + 1)],
                                   BB.ap()[:, ki, :],
                                   start=(ki == 0), stop=(ki == 7))
                return ins
            OP("t", e, reads=["BB", "WA"], writes=[f"P{b}"])
            OP("s", lambda s, o=o, b=b: s.copy(BD.ap()[:, o, :], P[b].ap()[:, :]),
               reads=[f"P{b}"], writes=["BD"])
        for c in range(8):
            kq = f"qsq{c % 2}"
            OP("v", lambda v, c=c: v.tensor_tensor(
                BH.ap()[:, 512 * (c % 2):512 * (c % 2 + 1)],
                BD.ap()[:, c, :], BD.ap()[:, c, :], op=ALU.mult),
               reads=["BD"], writes=[kq, f"ksq{c % 2}"])
            OP("t", lambda t, c=c: t.matmul(
                P[0].ap()[0:16, :], BLK.ap()[:, c, :],
                BH.ap()[:, 512 * (c % 2):512 * (c % 2 + 1)],
                start=(c == 0), stop=(c == 7)),
               reads=[kq, "BLK"], writes=["P0"])
        OP("s", lambda s: s.activation(NDS[0:16, :], P[0].ap()[0:16, :],
                                       AF.Copy, scale=-1.0),
           reads=["P0"], writes=["NDS"], dup=True)
        def e(t):
            for j in range(NSUB):
                ins = t.matmul(P[0].ap()[:, 16 * j:16 * (j + 1)],
                               NDS.ap()[0:16, 128 * j:128 * (j + 1)],
                               IDENT[0:16, 0:16],
                               is_transpose=True, start=(j == 0), stop=(j == NSUB - 1))
            return ins
        OP("t", e, reads=["NDS", "IDENT"], writes=["P0"], dup=True)
        OP("s", lambda s: s.copy(NDT.ap()[:, :, :].rearrange("p a b -> p (a b)"),
                                 P[0].ap()[:, 0:64]),
           reads=["P0"], writes=["NDT"], dup=True)
        for j in range(NSUB):
            for p in range(8):
                b = 6 + (p % 2)
                def e(t, j=j, p=p, b=b):
                    return t.matmul(P[b].ap()[:, :],
                                    BD.ap()[:, p, 128 * j:128 * (j + 1)],
                                    PRJCAT.ap()[:, :], start=True, stop=True)
                OP("t", e, reads=["BD"] + PRJ_R, writes=[f"P{b}"])
                OP("v", lambda v, p=p, b=b: v.reduce_max(
                    NSTAB.ap()[:, 2 * p:2 * p + 2],
                    P[b].ap().rearrange("p (a m) -> p a m", a=2),
                    axis=AX.X, negate=True),
                   reads=[f"P{b}"], writes=["NSTABr"])
                vgap()
                OP("v", lambda v, j=j, p=p: v.tensor_tensor(
                    BIASQ.ap()[:, 2 * p:2 * p + 2], NSTAB.ap()[:, 2 * p:2 * p + 2],
                    NDT.ap()[:, j, 2 * p:2 * p + 2], op=ALU.add),
                   reads=["NSTABr", "NDT"], writes=["BIASQw"])
                def e(s, j=j, p=p, b=b):
                    for i in range(2):
                        ins = s.activation(
                            BC.ap()[:, 512 * p + 256 * i:512 * p + 256 * (i + 1)],
                            P[b].ap()[:, 256 * i:256 * (i + 1)], AF.Exp,
                            bias=BIASQ.ap()[:, 2 * p + i:2 * p + i + 1])
                    return ins
                OP("s", e, reads=["BIASQw"], writes=["BC", f"P{b}"])
            for g8 in range(8):
                b = g8 % 2
                def e(t, g8=g8, b=b):
                    for q in range(4):
                        mc = 4 * g8 + q
                        ins = t.matmul(P[b].ap()[:, 128 * q:128 * (q + 1)],
                                       BC.ap()[:, 128 * mc:128 * (mc + 1)], IDENT[:, :],
                                       is_transpose=True, start=(q == 0), stop=(q == 3))
                    return ins
                OP("t", e, reads=["BC", "IDENT"], writes=[f"P{b}"])
                ew = ["BE"]
                if st == 0 and j == 0 and g8 == 0:
                    ew += [f"kp0_{p}" for p in range(8)]
                OP("s", lambda s, g8=g8, b=b: s.activation(
                    BE.ap()[:, 512 * g8:512 * (g8 + 1)], P[b].ap()[:, :],
                    AF.Copy, bias=EPSK),
                   reads=[f"P{b}", "CONSTS"], writes=ew)
            def e(t):
                for hh in range(16):
                    for q in range(2):
                        kc = 2 * hh + q
                        ins = t.matmul(P[1].ap()[:, 2 * hh:2 * hh + 2],
                                       e32[:, kc, :], ctx3[:, kc, 64:66],
                                       start=(hh == 0 and q == 0),
                                       stop=(hh == 15 and q == 1))
                return ins
            OP("t", e, reads=["BE", "CTX"], writes=["P1"])
            for b2 in (2, 3):
                def e(t, b2=b2):
                    for hh in range(8):
                        h = 8 * (b2 - 2) + hh
                        for q in range(2):
                            kc = 2 * h + q
                            ins = t.matmul(P[b2].ap()[:, 64 * hh:64 * (hh + 1)],
                                           e32[:, kc, :], ctx3[:, kc, 0:64],
                                           start=(hh == 0 and q == 0),
                                           stop=(hh == 7 and q == 1))
                    return ins
                OP("t", e, reads=["BE", "CTX"], writes=[f"P{b2}"])
            OP("s", lambda s: s.copy(TMP32[:, :], P[1].ap()[:, 0:32]),
               reads=["P1"], writes=["P1c"])
            p1s = TMP32.ap().rearrange("p (a b) -> p a b", a=16)
            for b2 in (2, 3):
                hb = 8 * (b2 - 2)
                OP("v", lambda v, hb=hb: v.scalar_tensor_tensor(
                    DEN.ap()[:, hb:hb + 8],
                    p1s[:, hb:hb + 8, 1], BCAST.ap()[:, 1:2],
                    p1s[:, hb:hb + 8, 0], op0=ALU.mult, op1=ALU.add),
                   reads=["P1c", "BCASTw"], writes=["DENw"])
                OP("v", lambda v, hb=hb: v.tensor_scalar(
                    RBD.ap()[:, hb:hb + 8], p1s[:, hb:hb + 8, 1],
                    BCAST.ap()[:, 0:1], None, op0=ALU.mult),
                   reads=["P1c", "BCASTw"], writes=["RBDw"])
            for b2 in (2, 3):
                hb = 8 * (b2 - 2)
                OP("v", lambda v, hb=hb: v.reciprocal(
                    DINV.ap()[:, hb:hb + 8], DEN.ap()[:, hb:hb + 8]),
                   reads=["DENw"], writes=["DINVw"])
            for b2 in (2, 3):
                hb = 8 * (b2 - 2)
                OP("v", lambda v, hb=hb: v.tensor_tensor(
                    RBD.ap()[:, hb:hb + 8], RBD.ap()[:, hb:hb + 8],
                    DINV.ap()[:, hb:hb + 8], op=ALU.mult),
                   reads=["DINVw"], writes=["RBDw"])
            for b2 in (2, 3):
                hb = 8 * (b2 - 2)
                ew = ["BF"]
                if st == 0 and j == 0 and b2 == 2:
                    ew += [f"kp1_{p}" for p in range(8)]
                OP("v", lambda v, hb=hb, b2=b2: v.tensor_tensor(
                    atm[:, 512 * (b2 - 2):512 * (b2 - 1)].rearrange(
                        "p (a d) -> p a d", a=8),
                    P[b2].ap().rearrange("p (a d) -> p a d", a=8),
                    DINV.ap()[:, hb:hb + 8].unsqueeze(2).broadcast_to([128, 8, 64]),
                    op=ALU.mult),
                   reads=[f"P{b2}", "DINVw", "BF"], writes=ew)
            for b2 in (2, 3):
                hb = 8 * (b2 - 2)
                t2off = 1024 if b2 == 2 else 512
                OP("v", lambda v, hb=hb, b2=b2, t2off=t2off: v.tensor_tensor(
                    BH.ap()[:, t2off:t2off + 512].rearrange(
                        "p (a d) -> p a d", a=8),
                    SVREP.ap()[:, 512 * (b2 - 2):512 * (b2 - 1)].rearrange(
                        "p (a d) -> p a d", a=8),
                    RBD.ap()[:, hb:hb + 8].unsqueeze(2).broadcast_to([128, 8, 64]),
                    op=ALU.mult),
                   reads=["SVREPw", "RBDw"], writes=["T2w", "qsq1"])
            for b2 in (2, 3):
                t2off = 1024 if b2 == 2 else 512
                OP("v", lambda v, b2=b2, t2off=t2off: v.tensor_tensor(
                    atm[:, 512 * (b2 - 2):512 * (b2 - 1)],
                    atm[:, 512 * (b2 - 2):512 * (b2 - 1)],
                    BH.ap()[:, t2off:t2off + 512],
                    op=ALU.add),
                   reads=["T2w", "BF"], writes=["BF"])
            vgap()
            vgap("CSLK")
            for g2 in range(2):
                b = g2
                def e(t, g2=g2, b=b):
                    for q in range(4):
                        c = 4 * g2 + q
                        ins = t.matmul(P[b].ap()[:, 128 * q:128 * (q + 1)],
                                       atm[:, 128 * c:128 * (c + 1)], IDENT[:, :],
                                       is_transpose=True, start=(q == 0), stop=(q == 3))
                    return ins
                OP("t", e, reads=["BF", "IDENT", "CSLK"], writes=[f"P{b}"])
                OP("s", lambda s, g2=g2, b=b: s.copy(
                    afm[:, 4 * g2:4 * (g2 + 1), :].rearrange("p a c -> p (a c)"),
                    P[b].ap()[:, :]),
                   reads=[f"P{b}", "BF"], writes=["BF"])
            for h in range(2):
                b = 2 + h
                def e(t, h=h, b=b):
                    for ki in range(8):
                        ins = t.matmul(P[b].ap()[:, :], afm[:, ki, :],
                                       WB.ap()[:, ki, 512 * h:512 * (h + 1)],
                                       start=(ki == 0), stop=(ki == 7))
                    return ins
                OP("t", e, reads=["BF", "WB"], writes=[f"P{b}"])
            OP("g", lambda g, j=j: g.tensor_tensor(
                BF.ap()[:, 2048:3072], BA.ap()[:, j, :], BOREP[:, :], op=ALU.add),
               reads=[f"BAc{j}", "BOREP", "BF"], writes=["BF"])
            for h in range(2):
                b = 2 + h
                OP("v", lambda v, j=j, h=h, b=b: v.tensor_tensor(
                    x2v[:, j % 2, 512 * h:512 * (h + 1)],
                    BF.ap()[:, 2048 + 512 * h:2048 + 512 * (h + 1)],
                    P[b].ap()[:, :], op=ALU.add),
                   reads=[f"P{b}", "BF"], writes=["BF"])
            vgap()
            vgap("CSLK2")
            def e(g, j=j, st=st):
                return g.dma_start(
                    out=(y_h if stage == "x2" else x2s_h)[
                        st * TST + 128 * j:st * TST + 128 * (j + 1), :],
                    in_=x2v[:, j % 2, :]).then_inc(d_g, 16)
            OP("g", e, reads=["BF"], writes=["x2st"], ndma=1)
            idx = ST4(st) + j
            vgap()
            OP("v", lambda v, j=j, idx=idx: v.reduce_sum(
                LN2M.ap()[:, idx:idx + 1], x2v[:, j % 2, :], axis=AX.X),
               reads=["BF"], writes=["LN2M"])
            OP("s", lambda s, j=j: s.activation(
                BH.ap()[:, 0:1024], x2v[:, j % 2, :], AF.Square),
               reads=["BF"], writes=["qsq0", "qsq1", "ksq0", "ksq1"])
            OP("v", lambda v, j=j: v.reduce_sum(
                SS4.ap()[:, j:j + 1], BH.ap()[:, 0:1024], axis=AX.X),
               reads=["qsq0"], writes=["SS4"])
            OP("s", lambda s, idx=idx: s.activation(
                LN2M.ap()[:, idx:idx + 1], LN2M.ap()[:, idx:idx + 1], AF.Copy,
                scale=1.0 / D),
               reads=[], writes=["LN2M"])
            OP("s", lambda s, idx=idx: s.activation(
                TMP4B.ap()[:, 0:1], LN2M.ap()[:, idx:idx + 1], AF.Square),
               reads=["LN2M"], writes=["TMP4B"])
            OP("s", lambda s: s.activation(
                TMP4B.ap()[:, 1:2], TMP4B.ap()[:, 0:1], AF.Copy,
                scale=-1.0, bias=LNEPS),
               reads=["CONSTS"], writes=["TMP4B"])
            OP("s", lambda s, j=j: s.activation(
                TMP4B.ap()[:, 2:3], SS4.ap()[:, j:j + 1], AF.Ln, scale=1.0 / D,
                bias=TMP4B.ap()[:, 1:2]),
               reads=["SS4"], writes=["TMP4B"])
            OP("s", lambda s, idx=idx: s.activation(
                LN2I.ap()[:, idx:idx + 1], TMP4B.ap()[:, 2:3], AF.Exp, scale=-0.5),
               reads=["TMP4B"], writes=["LN2I"])
            for h in range(2):
                OP("t", lambda t, st=st, j=j, h=h, idx=idx: t.matmul(
                    P[4 + h].ap()[0:1, :], LN2I.ap()[:, idx:idx + 1],
                    x2v[:, j % 2, 512 * h:512 * (h + 1)],
                    start=(st == 0 and j == 0), stop=False),
                   reads=["LN2I", "BF", "CSLK2"], writes=[f"P{4 + h}"])

    if stage == "x2":
        def fin(y_eng):
            y_eng.wait_ge(d_g, bld.cnt["DG"])
            y_eng.wait_ge(d_y, bld.cnt["DY"])
        return nc, bld, SEM, es, fin

    def e(v):
        for ki in range(2):
            v.tensor_copy(W4R.ap()[:, ki, :],
                          W4S.ap()[:, ki, 0:1].broadcast_to([128, 128]))
            ins = v.tensor_copy(GW3R.ap()[:, ki, :],
                                GW3.ap()[:, ki, 0:1].broadcast_to([128, 128]))
        return ins
    OP("v", e, reads=["W4S", "GW3"], writes=["W4R", "GW3R"], dup=True)

    def e(t):
        for j in range(32):
            ins = t.matmul(P[0].ap()[0:1, 0:1], LN2M.ap()[:, j:j + 1],
                           LN2I.ap()[:, j:j + 1], start=(j == 0), stop=(j == 31))
        return ins
    OP("t", e, reads=["LN2M", "LN2I"], writes=["P0"], dup=True)
    OP("s", lambda s: s.copy(ROWS.ap()[0:1, R_S2:R_S2 + 1], P[0].ap()[0:1, 0:1]),
       reads=["P0"], writes=["S2col"], dup=True)
    for h in range(2):
        OP("t", lambda t, h=h: t.matmul(
            P[4 + h].ap()[0:1, :], ROWS.ap()[0:1, R_S2:R_S2 + 1],
            NEG32.ap()[0:1, :], start=False, stop=True),
           reads=["S2col", "NEG32"], writes=[f"P{4 + h}"])
    for h in range(2):
        OP("s", lambda s, h=h: s.activation(
            ARENA.ap()[0:1, 512 * h:512 * (h + 1)], P[4 + h].ap()[0:1, :],
            AF.Copy, scale=1.0 / NTOK),
           reads=[f"P{4 + h}"], writes=["GM", "SVROW", "CSUM1"], dup=True)
    def e(t):
        for q in range(8):
            ins = t.matmul(P[1].ap()[:, q:q + 1],
                           ARENA.ap()[0:1, 128 * q:128 * (q + 1)],
                           IDENT[0:1, 0:1], is_transpose=True,
                           start=(q == 0), stop=(q == 7))
        return ins
    OP("t", e, reads=["GM", "IDENT"], writes=["P1"], dup=True)
    OP("s", lambda s: s.copy(CSFM[:, :], P[1].ap()[:, 0:8]),
       reads=["P1"], writes=["CSFMw", "CSFMacc"], dup=True)

    ld("WA", WA.ap()[:, :, :], w1_h.rearrange("(a p) d -> p a d", p=128))
    ld("WB2", WB.ap()[:, :, 0:512], w2_h.rearrange("(a p) d -> p a d", p=128),
       extra_writes=["WB"])
    ld("WBg", WB.ap()[:, :, 512:1024], g1_h.rearrange("(a p) d -> p a d", p=128),
       extra_writes=["WB"])

    def e(s):
        return s.add_instruction(mybir.InstLoadActFuncSet(
            name=nc.get_next_instruction_name(), act_func_set_id=10, ins=[], outs=[]))
    OP("s", e)

    def e(t):
        for o in range(4):
            for ki in range(8):
                ins = t.matmul(P[2].ap()[:, o:o + 1],
                               WB.ap()[:, ki, 512 + 128 * o:512 + 128 * (o + 1)],
                               CSFM.ap()[:, ki:ki + 1],
                               start=(o == 0 and ki == 0), stop=(o == 3 and ki == 7))
        return ins
    OP("t", e, reads=["CSFMw", "WBg"], writes=["P2"], dup=True)
    for o in range(4):
        OP("s", lambda s, o=o: s.activation(GH1.ap()[:, o:o + 1], P[2].ap()[:, o:o + 1],
                                            AF.Gelu, bias=GB1.ap()[:, o:o + 1]),
           reads=["P2", "GB1"], writes=["GH1w"], dup=True)
    def e(t):
        for o in range(2):
            for ki in range(4):
                ins = t.matmul(P[3].ap()[:, o:o + 1],
                               W3S.ap()[:, ki, 128 * o:128 * (o + 1)],
                               GH1.ap()[:, ki:ki + 1],
                               start=(o == 0 and ki == 0), stop=(o == 1 and ki == 3))
        return ins
    OP("t", e, reads=["GH1w", "GW2"], writes=["P3"], dup=True)
    for o in range(2):
        OP("s", lambda s, o=o: s.activation(GH2.ap()[:, o:o + 1], P[3].ap()[:, o:o + 1],
                                            AF.Gelu, bias=GB2.ap()[:, o:o + 1]),
           reads=["P3", "GB2"], writes=["GH2w"], dup=True)
    def e(t):
        for ki in range(2):
            ins = t.matmul(P[4].ap()[:, 0:1], GW3R.ap()[:, ki, :],
                           GH2.ap()[:, ki:ki + 1], start=(ki == 0), stop=(ki == 1))
        return ins
    OP("t", e, reads=["GH2w", "GW3R"], writes=["P4"], dup=True)
    OP("s", lambda s: s.activation(TGC.ap()[:, :], P[4].ap()[:, 0:1], AF.Tanh,
                                   scale=0.5, bias=B3HC.ap()[:, 0:1]),
       reads=["P4", "B3C"], writes=["TGc"], dup=True)
    OP("t", lambda t: t.matmul(P[5].ap()[0:1, 0:128], ARCOL.ap()[0:1, 0:1],
                               ONES1.ap()[0:1, :], start=True, stop=True),
       reads=["rar", "ONES"], writes=["P5"], dup=True)
    OP("s", lambda s: s.copy(ROWS.ap()[0:1, R_UR:R_UR + 128],
                             P[5].ap()[0:1, 0:128]),
       reads=["P5"], writes=["UROW"], dup=True)
    OP("t", lambda t: t.matmul(P[5].ap()[:, 0:1], ROWS.ap()[0:1, R_UR:R_UR + 128],
                               TGC.ap()[0:1, 0:1], start=True, stop=True),
       reads=["UROW", "TGc"], writes=["P5"], dup=True)
    OP("s", lambda s: s.activation(CB2C.ap()[:, :], P[5].ap()[:, 0:1], AF.Copy,
                                   scale=0.7, bias=0.1),
       reads=["P5", "CONSTS"], writes=["CB2c"], dup=True)
    OP("s", lambda s: s.activation(A2C.ap()[:, :], ARCOL.ap()[:, :], AF.Copy,
                                   scale=0.3),
       reads=["rar"], writes=["A2c"], dup=True)

    ld("W3S", W3S.ap()[:, :, :], w3_h.rearrange("(a p) o -> p a o", p=128),
       extra_writes=["GW2"])

    be4 = BE.ap().rearrange("p (j d) -> p j d", j=4)
    h2v = BE.ap()[:, 0:2048].rearrange("p (a d) -> p a d", a=4)
    h3v = BE.ap()[:, 2048:3072].rearrange("p (a d) -> p a d", a=2)
    crrep = BE.ap()[:, 3072:3584]
    bd2 = BD.ap().rearrange("p a d -> p (a d)")

    for st in range(NST):
        xres = "BA" if st % 2 == 0 else "BF"
        xv4 = (BA.ap()[:, :, :] if st % 2 == 0
               else BF.ap().rearrange("p (j d) -> p j d", j=4))
        ld(xres, xv4[:, :, :],
           x2s_h[st * TST:(st + 1) * TST, :].rearrange("(j p) d -> p j d", p=128),
           reads=["x2st"],
           extra_writes=(["BAc0", "BAc1", "BAc2", "BAc3"] if st == 0 else []))
        for j in range(NSUB):
            OP("v", lambda v, j=j, st=st, xv4=xv4: v.tensor_scalar(
                be4[:, j, :], xv4[:, j, :],
                LN2M.ap()[:, ST4(st) + j:ST4(st) + j + 1],
                LN2I.ap()[:, ST4(st) + j:ST4(st) + j + 1],
                op0=ALU.subtract, op1=ALU.mult),
               reads=[xres, "LN2M", "LN2I"], writes=["BE"])
        vgap()
        vgap("VSLK")
        for c in range(8):
            b = c % 2
            def e(t, c=c, b=b):
                for j in range(NSUB):
                    ins = t.matmul(P[b].ap()[:, 128 * j:128 * (j + 1)],
                                   be4[:, j, 128 * c:128 * (c + 1)], IDENT[:, :],
                                   is_transpose=True, start=(j == 0), stop=(j == NSUB - 1))
                return ins
            OP("t", e, reads=["BE", "IDENT", "VSLK"], writes=[f"P{b}"])
            OP("s", lambda s, c=c, b=b: s.copy(BB.ap()[:, c, :], P[b].ap()[:, :]),
               reads=[f"P{b}"], writes=["BB"])
        cur, curres = BB, "BB"
        for cyc in range(NCYC):
            nxt = BC if cur is BB else BB
            nxtres = "BC" if curres == "BB" else "BB"
            curv = cur.ap() if cur is BB else cur.ap().rearrange("p (a d) -> p a d", a=8)
            nxtv = nxt.ap() if nxt is BB else nxt.ap().rearrange("p (a d) -> p a d", a=8)
            for o in range(8):
                b = 2 + (o % 2)
                if cyc == 0:
                    def e(t, o=o, b=b, curv=curv):
                        for ki in range(8):
                            ins = t.matmul(P[b].ap()[:, :],
                                           WA.ap()[:, ki, 128 * o:128 * (o + 1)],
                                           curv[:, ki, :],
                                           start=(ki == 0), stop=(ki == 7))
                        return ins
                    OP("t", e, reads=[curres, "WA", "VSLK"], writes=[f"P{b}"])
                else:
                    for ki in range(8):
                        slk = (f"{curres}a{ki + 2}" if ki <= 5 else "VSLK")
                        OP("t", lambda t, o=o, ki=ki, b=b, curv=curv: t.matmul(
                            P[b].ap()[:, :],
                            WA.ap()[:, ki, 128 * o:128 * (o + 1)],
                            curv[:, ki, :], start=(ki == 0), stop=(ki == 7)),
                           reads=[slk, "WA"], writes=[f"P{b}"])
                OP("s", lambda s, o=o, b=b: s.activation(
                    BD.ap()[:, o, :], P[b].ap()[:, :], AF.Gelu,
                    bias=B1S.ap()[:, o:o + 1]),
                   reads=[f"P{b}", "B1S"], writes=["BD"])
            for o in range(4):
                b = 4 + (o % 2)
                def e(t, o=o, b=b):
                    for ki in range(8):
                        ins = t.matmul(P[b].ap()[:, :],
                                       WB.ap()[:, ki, 128 * o:128 * (o + 1)],
                                       BD.ap()[:, ki, :], start=(ki == 0), stop=(ki == 7))
                    return ins
                OP("t", e, reads=["BD", "WB2"], writes=[f"P{b}"])
                OP("s", lambda s, o=o, b=b: s.activation(
                    h2v[:, o, :], P[b].ap()[:, :], AF.Gelu,
                    bias=B2S.ap()[:, o:o + 1]),
                   reads=[f"P{b}", "B2S"], writes=["BE"])
            for o in range(2):
                b3k = 6 if o == 0 else 0
                def e(t, o=o, b3k=b3k):
                    for ki in range(4):
                        ins = t.matmul(P[b3k].ap()[:, :],
                                       W3S.ap()[:, ki, 128 * o:128 * (o + 1)],
                                       h2v[:, ki, :], start=(ki == 0), stop=(ki == 3))
                    return ins
                OP("t", e, reads=["BE", "W3S"], writes=[f"P{b3k}"])
                OP("s", lambda s, o=o, b3k=b3k: s.activation(
                    h3v[:, o, :], P[b3k].ap()[:, :], AF.Gelu,
                    bias=B3S.ap()[:, o:o + 1]),
                   reads=[f"P{b3k}", "B3S"], writes=["BE"])
            def e(t):
                for ki in range(2):
                    ins = t.matmul(P[7].ap()[:, :], W4R.ap()[:, ki, :],
                                   h3v[:, ki, :], start=(ki == 0), stop=(ki == 1))
                return ins
            OP("t", e, reads=["BE", "W4R"], writes=["P7"])
            OP("s", lambda s: s.activation(crrep, P[7].ap()[:, :], AF.Tanh,
                                           scale=0.5, bias=B4HC.ap()[:, 0:1]),
               reads=["P7", "B4C"], writes=["BE"])
            OP("v", lambda v: v.tensor_scalar(crrep, crrep, A2C.ap()[:, 0:1],
                                              CB2C.ap()[:, 0:1],
                                              op0=ALU.mult, op1=ALU.add),
               reads=["BE", "A2c", "CB2c"], writes=["BE"])
            vgap()
            vgap()
            def em(c, curv=curv, nxtv=nxtv):
                OP("v", lambda v, c=c, curv=curv, nxtv=nxtv: v.tensor_tensor(
                    nxtv[:, c, :], crrep, curv[:, SIGMA[c], :], op=ALU.mult),
                   reads=[curres, "BE"], writes=[f"{nxtres}m{c}"])
            def ea(c, curv=curv, nxtv=nxtv):
                OP("v", lambda v, c=c, curv=curv, nxtv=nxtv: v.tensor_tensor(
                    nxtv[:, c, :], curv[:, c, :], nxtv[:, c, :],
                    op=(ALU.add if SIGN[c] > 0 else ALU.subtract)),
                   reads=[curres, f"{nxtres}m{c}"],
                   writes=[f"{nxtres}a{c}", nxtres])
            em(0)
            em(1)
            for c in range(2, 8):
                ea(c - 2)
                em(c)
            ea(6)
            ea(7)
            vgap()
            vgap()
            vgap("VSLK")
            cur, curres = nxt, nxtres
        curv = cur.ap() if cur is BB else cur.ap().rearrange("p (a d) -> p a d", a=8)
        for j in range(NSUB):
            for g2 in range(2):
                b = g2
                def e(t, j=j, g2=g2, b=b, curv=curv):
                    for q in range(4):
                        c = 4 * g2 + q
                        ins = t.matmul(P[b].ap()[:, 128 * q:128 * (q + 1)],
                                       curv[:, c, 128 * j:128 * (j + 1)], IDENT[:, :],
                                       is_transpose=True, start=(q == 0), stop=(q == 3))
                    return ins
                OP("t", e, reads=[curres, "IDENT", "VSLK"], writes=[f"P{b}"])
                OP("v", lambda v, j=j, g2=g2, b=b, xv4=xv4: v.tensor_tensor(
                    bd2[:, 1024 * (j % 2) + 512 * g2:1024 * (j % 2) + 512 * (g2 + 1)],
                    xv4[:, j, 512 * g2:512 * (g2 + 1)],
                    P[b].ap()[:, :], op=ALU.add),
                   reads=[f"P{b}", xres], writes=["BD"])
            def e(g, j=j, st=st):
                return g.dma_start(
                    out=y_h[st * TST + 128 * j:st * TST + 128 * (j + 1), :],
                    in_=bd2[:, 1024 * (j % 2):1024 * (j % 2 + 1)]).then_inc(d_g, 16)
            OP("g", e, reads=["BD"], writes=["yst"], ndma=1)

    def e(g):
        g.dma_start(out=ydbg_h[0:1, 0:1024], in_=ARENA.ap()[0:1, :]).then_inc(d_g, 16)
        g.dma_start(out=ydbg_h[0:1, 1024:1536], in_=ROWS.ap()[0:1, :]).then_inc(d_g, 16)
        g.dma_start(out=ydbg_h[0:1, 1536:2048],
                    in_=BE.ap()[0:1, 3072:3584]).then_inc(d_g, 16)
        g.dma_start(out=ydbg_h[0:1, 2048:2056],
                    in_=CSFM.ap()[0:1, 0:8]).then_inc(d_g, 16)
        g.dma_start(out=ydbg_h[0:1, 2056:2057],
                    in_=TGC.ap()[0:1, 0:1]).then_inc(d_g, 16)
        g.dma_start(out=ydbg_h[0:1, 2057:2058],
                    in_=A2C.ap()[0:1, 0:1]).then_inc(d_g, 16)
        g.dma_start(out=ydbg_h[0:1, 2058:2059],
                    in_=CB2C.ap()[0:1, 0:1]).then_inc(d_g, 16)
        g.dma_start(out=ydbg_h[0:1, 2059:2063],
                    in_=GH1.ap()[0:1, 0:4]).then_inc(d_g, 16)
        return g.dma_start(out=ydbg_h[0:1, 2063:2065],
                           in_=GH2.ap()[0:1, 0:2]).then_inc(d_g, 16)
    OP("g", e, reads=["GM", "TGc", "CB2c", "A2c", "BE", "CSFMw", "GH1w", "GH2w"],
       writes=["ydbg"], ndma=9)
    if DBG:
        def e(g):
            g.dma_start(out=ydbg2_h[:, 0:4], in_=GH1.ap()[:, :],
                        allow_slow_non_contiguous=True).then_inc(d_g, 16)
            g.dma_start(out=ydbg2_h[:, 4:6], in_=GH2.ap()[:, :],
                        allow_slow_non_contiguous=True).then_inc(d_g, 16)
            g.dma_start(out=ydbg2_h[:, 6:7], in_=TGC.ap()[:, :],
                        allow_slow_non_contiguous=True).then_inc(d_g, 16)
            g.dma_start(out=ydbg2_h[:, 7:8], in_=CB2C.ap()[:, :],
                        allow_slow_non_contiguous=True).then_inc(d_g, 16)
            g.dma_start(out=ydbg2_h[:, 8:9], in_=GW3R.ap()[:, 0, 0:1],
                        allow_slow_non_contiguous=True).then_inc(d_g, 16)
            g.dma_start(out=ydbg2_h[:, 9:10], in_=GW3R.ap()[:, 1, 0:1],
                        allow_slow_non_contiguous=True).then_inc(d_g, 16)
            g.dma_start(out=ydbg2_h[:, 10:11], in_=GW3R.ap()[:, 0, 127:128],
                        allow_slow_non_contiguous=True).then_inc(d_g, 16)
            g.dma_start(out=ydbg2_h[:, 11:12], in_=W4R.ap()[:, 0, 0:1],
                        allow_slow_non_contiguous=True).then_inc(d_g, 16)
            g.dma_start(out=ydbg2_h[:, 12:13], in_=W4R.ap()[:, 1, 0:1],
                        allow_slow_non_contiguous=True).then_inc(d_g, 16)
            g.dma_start(out=ydbg2_h[:, 13:14], in_=GW3.ap()[:, 0, 0:1],
                        allow_slow_non_contiguous=True).then_inc(d_g, 16)
            g.dma_start(out=ydbg2_h[:, 14:15], in_=GW3.ap()[:, 1, 0:1],
                        allow_slow_non_contiguous=True).then_inc(d_g, 16)
            return g.dma_start(out=ydbg2_h[:, 15:16], in_=W4S.ap()[:, 0, 0:1],
                               allow_slow_non_contiguous=True).then_inc(d_g, 16)
        OP("g", e, reads=["GM", "TGc", "CB2c", "A2c", "GH1w", "GH2w",
                          "W4R", "GW3R", "W4S", "GW3"],
           writes=["ydbg2"], ndma=12)

    def fin(y_eng):
        y_eng.wait_ge(d_g, bld.cnt["DG"])
        y_eng.wait_ge(d_y, bld.cnt["DY"])
    return nc, bld, SEM, es, fin


def emit(nc, bld, SEM, es, fin, nops=None):
    own = dict(t="T", v="V", s="S", g="G")
    ops = bld.ops if nops is None else bld.ops[:nops]
    if nops is not None:
        pc = dict(DY=0, DG=0)
        for (_e, _w2, _f, key, amt, _r, _w3) in ops:
            if key in pc:
                pc[key] += amt
        def fin(y_eng, _pc=pc):
            y_eng.wait_ge(SEM["DG"], _pc["DG"])
            y_eng.wait_ge(SEM["DY"], _pc["DY"])
    with es:
        with nc.Block() as block:
            def run(eng_name):
                def body(eng):
                    hw = {}
                    for (e, waits, emit_fn, key, amt, _r, _w) in ops:
                        if e != eng_name:
                            continue
                        for k, v in sorted(waits.items()):
                            if k == own.get(eng_name):
                                continue
                            if hw.get(k, 0) >= v:
                                continue
                            eng.wait_ge(SEM[k], v)
                            hw[k] = v
                        last = emit_fn(eng)
                        if key not in ("DY", "DG"):
                            last.then_inc(SEM[key], amt)
                    if eng_name == "y":
                        fin(eng)
                return body
            block.sync(run("y"))
            block.tensor(run("t"))
            block.vector(run("v"))
            block.scalar(run("s"))
            block.gpsimd(run("g"))
    return nc


NOPS = None
LAST_RESULTS = None

def kernel(**inputs):
    ncores = 8
    nc, bld, SEM, es, fin = build_nc(STAGE)
    emit(nc, bld, SEM, es, fin, nops=NOPS)

    f32 = np.float32
    ca = lambda a: np.ascontiguousarray(np.asarray(a, dtype=f32))
    x = ca(inputs["L0_attn_x"])
    shared = {
        "wq": ca(inputs["L0_attn_wq"]), "wk": ca(inputs["L0_attn_wk"]),
        "wv": ca(inputs["L0_attn_wv"]), "wo": ca(inputs["L0_attn_wo"]),
        "bo": ca(inputs["L0_attn_bo"]).reshape(1, D),
        "proj": ca(inputs["L0_attn_proj"]),
        "w1": ca(inputs["L0_ffn_cn_w1"]), "b1": ca(inputs["L0_ffn_cn_b1"]),
        "w2": ca(inputs["L0_ffn_cn_w2"]), "b2": ca(inputs["L0_ffn_cn_b2"]),
        "w3": ca(inputs["L0_ffn_cn_w3"]), "b3": ca(inputs["L0_ffn_cn_b3"]),
        "w4": ca(inputs["L0_ffn_cn_w4"]),
        "b4": ca(inputs["L0_ffn_cn_b4"]).reshape(1, 1),
        "gw1": ca(inputs["L0_ffn_gc_w1"]), "gb1": ca(inputs["L0_ffn_gc_b1"]),
        "gw2": ca(inputs["L0_ffn_gc_w2"]), "gb2": ca(inputs["L0_ffn_gc_b2"]),
        "gw3": ca(inputs["L0_ffn_gc_w3"]),
        "gb3": ca(inputs["L0_ffn_gc_b3"]).reshape(1, 1),
        "ar": ca(inputs["L0_ffn_adaptive_range"]).reshape(1, 1),
    }
    in_maps = [dict(shared, x=np.ascontiguousarray(x[i])) for i in range(ncores)]
    res = run_bass_kernel_spmd(nc, in_maps, core_ids=list(range(ncores)))
    global LAST_RESULTS
    LAST_RESULTS = res.results
    out = np.stack([np.asarray(res.results[i]["y"]) for i in range(ncores)], axis=0)
    return out.astype(np.float32)


# version 23
# speedup vs baseline: 1.0519x; 1.0381x over previous
import contextlib
import numpy as np

import concourse.bass as bass
import concourse.mybir as mybir
from concourse.bass_utils import run_bass_kernel_spmd

F32 = mybir.dt.float32
AF = mybir.ActivationFunctionType
ALU = mybir.AluOpType
AX = mybir.AxisListType

STAGE = "full"
DBG = False
NCYC = 3

NTOK, D, H, DH, M = 4096, 1024, 16, 64, 256
NST, TST, NSUB = 8, 512, 4
NRM = float(64.0 ** -0.25)
EPSK = 1e-4
LNEPS = 1e-5
LNEPSK = float(np.log(EPSK))
SIGN = [1, 1, -1, -1, -1, -1, 1, 1]
SIGMA = [4, 5, 6, 7, 0, 1, 2, 3]


class Builder:

    def __init__(self):
        self.ops = []
        self.cnt = dict(T=0, V=0, S=0, G=0, DY=0, DG=0, CC=0)
        self.res = {}
        self.gap_emit = None
        self.s_spacer = None
        self.prev_v_writes = set()
        self._in_gap = False

    def op(self, eng, emit, reads=(), writes=(), ndma=0, cc=False, dup=False):
        if dup:
            self.op(eng, emit, reads=reads, writes=writes, ndma=ndma, cc=cc)
            if eng == "s" and self.s_spacer is not None:
                self._raw_op("s", self.s_spacer, (), ("SSPC",), 0, False)
                self._raw_op("s", self.s_spacer, (), ("SSPC",), 0, False)
            elif eng == "v" and self.gap_emit is not None:
                self._raw_op("v", self.gap_emit, (), ("VDres",), 0, False)
                self._raw_op("v", self.gap_emit, (), ("VDres",), 0, False)
        if eng == "v" and not self._in_gap:
            touched = set(reads) | set(writes)
            if touched & self.prev_v_writes and self.gap_emit is not None:
                self._in_gap = True
                self.op("v", self.gap_emit, reads=(), writes=("VDres",))
                self._in_gap = False
            self.prev_v_writes = set(writes)
        elif eng == "v":
            self.prev_v_writes = set(writes)
        self._raw_op(eng, emit, reads, writes, ndma, cc)

    def _raw_op(self, eng, emit, reads=(), writes=(), ndma=0, cc=False):
        if ndma:
            key, amt = ("DY" if eng == "y" else "DG"), 16 * ndma
        elif cc:
            key, amt = "CC", 1
        else:
            key, amt = dict(t="T", v="V", s="S", g="G", y="DY")[eng], 1
        waits = {}

        def addw(m):
            k, v = m
            if v > waits.get(k, 0):
                waits[k] = v

        for r in reads:
            e = self.res.get(r)
            if e and e["w"] is not None:
                addw(e["w"])
        for r in writes:
            e = self.res.get(r)
            if e:
                if e["w"] is not None:
                    addw(e["w"])
                for m in e["r"]:
                    addw(m)
        self.cnt[key] += amt
        marker = (key, self.cnt[key])
        for r in writes:
            self.res[r] = {"w": marker, "r": []}
        for r in reads:
            self.res.setdefault(r, {"w": None, "r": []})["r"].append(marker)
        self.ops.append((eng, waits, emit, key, amt, tuple(reads), tuple(writes)))


def build_nc(stage):
    nc = bass.Bass("TRN2", target_bir_lowering=False)

    x_h = nc.dram_tensor("x", [NTOK, D], F32, kind="ExternalInput").ap()
    wq_h = nc.dram_tensor("wq", [D, D], F32, kind="ExternalInput").ap()
    wk_h = nc.dram_tensor("wk", [D, D], F32, kind="ExternalInput").ap()
    wv_h = nc.dram_tensor("wv", [D, D], F32, kind="ExternalInput").ap()
    wo_h = nc.dram_tensor("wo", [D, D], F32, kind="ExternalInput").ap()
    bo_h = nc.dram_tensor("bo", [1, D], F32, kind="ExternalInput").ap()
    pj_h = nc.dram_tensor("proj", [M, DH], F32, kind="ExternalInput").ap()
    w1_h = nc.dram_tensor("w1", [D, D], F32, kind="ExternalInput").ap()
    b1_h = nc.dram_tensor("b1", [D], F32, kind="ExternalInput").ap()
    w2_h = nc.dram_tensor("w2", [D, 512], F32, kind="ExternalInput").ap()
    b2_h = nc.dram_tensor("b2", [512], F32, kind="ExternalInput").ap()
    w3_h = nc.dram_tensor("w3", [512, 256], F32, kind="ExternalInput").ap()
    b3_h = nc.dram_tensor("b3", [256], F32, kind="ExternalInput").ap()
    w4_h = nc.dram_tensor("w4", [256, 1], F32, kind="ExternalInput").ap()
    b4_h = nc.dram_tensor("b4", [1, 1], F32, kind="ExternalInput").ap()
    g1_h = nc.dram_tensor("gw1", [D, 512], F32, kind="ExternalInput").ap()
    gb1_h = nc.dram_tensor("gb1", [512], F32, kind="ExternalInput").ap()
    g2_h = nc.dram_tensor("gw2", [512, 256], F32, kind="ExternalInput").ap()
    gb2_h = nc.dram_tensor("gb2", [256], F32, kind="ExternalInput").ap()
    g3_h = nc.dram_tensor("gw3", [256, 1], F32, kind="ExternalInput").ap()
    gb3_h = nc.dram_tensor("gb3", [1, 1], F32, kind="ExternalInput").ap()
    ar_h = nc.dram_tensor("ar", [1, 1], F32, kind="ExternalInput").ap()
    y_h = nc.dram_tensor("y", [NTOK, D], F32, kind="ExternalOutput").ap()
    ydbg_h = nc.dram_tensor("ydbg", [1, 4096], F32, kind="ExternalOutput").ap()
    ydbg2_h = nc.dram_tensor("ydbg2", [128, 16], F32, kind="ExternalOutput").ap()
    x2s_h = nc.dram_tensor("x2scr", [NTOK, D], F32).ap()
    ccin_h = nc.dram_tensor("cc_in", [1, 8], F32).ap()
    ccout_h = nc.dram_tensor("cc_out", [1, 8], F32).ap()

    es = contextlib.ExitStack()
    sb = lambda n, s: es.enter_context(nc.sbuf_tensor(n, s, F32))
    ps = lambda n: es.enter_context(nc.psum_tensor(n, [128, 512], F32))

    BA = sb("BA", [128, 4, 1024])
    BB = sb("BB", [128, 8, 512])
    BC = sb("BC", [128, 4096])
    BD = sb("BD", [128, 8, 512])
    BE = sb("BE", [128, 4096])
    BF = sb("BF", [128, 4096])
    BH = sb("BH", [128, 1536])
    WA = sb("WA", [128, 8, 1024])
    WB = sb("WB", [128, 8, 1024])
    W3S = sb("W3S", [128, 4, 256])
    W4S = sb("W4S", [128, 2, 1])
    GW3 = sb("GW3", [128, 2, 1])
    B1S = sb("B1S", [128, 8])
    B2S = sb("B2S", [128, 4])
    B3S = sb("B3S", [128, 2])
    GB1 = sb("GB1", [128, 4])
    GB2 = sb("GB2", [128, 2])
    IDENT = sb("IDENT", [128, 128])
    PRJCAT = sb("PRJCAT", [128, 512])
    BLK = sb("BLK", [128, 8, 16])
    CTX = sb("CTX", [128, 32 * 66])
    ARENA = sb("ARENA", [1, 1024])
    ROWS = sb("ROWS", [1, 512])
    LN1M = sb("LN1M", [128, 32])
    LN1I = sb("LN1I", [128, 32])
    LN2M = sb("LN2M", [128, 32])
    LN2I = sb("LN2I", [128, 32])
    UMAT = sb("UMAT", [128, 256])
    NSTAB = sb("NSTABm", [128, 16])
    BIASQ = sb("BIASQ", [128, 16])
    NDS = sb("NDS", [16, 512])
    NDT = sb("NDT", [128, 4, 16])
    SS4 = sb("SS4", [128, 4])
    TMP4 = sb("TMP4", [128, 4])
    TMP4B = sb("TMP4B", [128, 4])
    TMP32 = sb("TMP32", [128, 32])
    DEN = sb("DEN", [128, 16])
    DINV = sb("DINV", [128, 16])
    RBD = sb("RBD", [128, 16])
    BCAST = sb("BCAST", [128, 2])
    SVREP = sb("SVREP", [128, 1024])
    BOREP = sb("BOREP", [128, 1024])
    ONESC = sb("ONESC", [128, 8])
    ONES1 = sb("ONES1", [1, 128])
    CSFM = sb("CSFM", [128, 8])
    GH1 = sb("GH1", [128, 4])
    GH2 = sb("GH2", [128, 2])
    PLOAD = sb("PLOAD", [128, 2, 64])
    CONSTS = sb("CONSTS", [128, 4])
    VD = sb("VD", [128, 4])
    SD = sb("SD", [128, 8])
    W4R = sb("W4R", [128, 2, 128])
    GW3R = sb("GW3R", [128, 2, 128])
    ARCOL = sb("ARCOL", [128, 1])
    B4HC = sb("B4HC", [128, 1])
    B3HC = sb("B3HC", [128, 1])
    A2C = sb("A2C", [128, 1])
    CB2C = sb("CB2C", [128, 1])
    T1C = sb("T1C", [128, 1])
    TGC = sb("TGC", [128, 1])
    NEG32 = sb("NEG32", [32, 512])
    P = [ps(f"P{i}") for i in range(8)]
    nc.const_aps.register(F32, LNEPS, CONSTS.ap()[:, 0:1])
    nc.const_aps.register(F32, EPSK, CONSTS.ap()[:, 1:2])
    nc.const_aps.register(F32, LNEPSK, CONSTS.ap()[:, 2:3])
    nc.const_aps.register(F32, 0.1, CONSTS.ap()[:, 3:4])

    d_y = es.enter_context(nc.semaphore("d_y"))
    d_g = es.enter_context(nc.semaphore("d_g"))
    s_t = es.enter_context(nc.semaphore("s_t"))
    s_v = es.enter_context(nc.semaphore("s_v"))
    s_s = es.enter_context(nc.semaphore("s_s"))
    s_g = es.enter_context(nc.semaphore("s_g"))
    s_cc = es.enter_context(nc.semaphore("s_cc"))
    SEM = dict(DY=d_y, DG=d_g, T=s_t, V=s_v, S=s_s, G=s_g, CC=s_cc)

    A_CS, A_GM = 0, 0
    A_TROW, A_CROW = 0, 512
    R_B4, R_GB3, R_B4H, R_B3H = 0, 1, 2, 3
    R_SLOC, R_S2, R_GZ = 4, 8, 9
    R_S8, R_SG, R_BETA, R_BN, R_B2 = 16, 24, 32, 40, 48
    R_AROW, R_A2, R_CB2, R_TG, R_T1 = 56, 64, 72, 80, 88
    R_UR = 128

    bld = Builder()
    OP = bld.op

    bld.gap_emit = lambda v: v.tensor_copy(VD.ap()[:, 0:2], VD.ap()[:, 2:4])
    bld.s_spacer = lambda s: s.copy(SD.ap()[:, 0:2], SD.ap()[:, 2:4])

    def vgap(res="VDres"):
        OP("v", bld.gap_emit, reads=[], writes=[res])

    def sflush():
        OP("s", lambda s: s.activation(SD.ap()[:, 0:4], CONSTS.ap()[:, 0:4],
                                       AF.Copy, accum_out=SD.ap()[:, 4:5]),
           reads=["CONSTS"], writes=["SDres"])

    def ld(name, dst_ap, src_ap, reads=(), extra_writes=(), slow=False):
        def e(eng, _aps=(dst_ap, src_ap), _slow=slow):
            return eng.dma_start(out=_aps[0], in_=_aps[1],
                                 allow_slow_non_contiguous=_slow).then_inc(d_y, 16)
        OP("y", e, reads=list(reads), writes=[name] + list(extra_writes), ndma=1)

    ld("BA", BA.ap()[:, :, :],
       x_h[0:TST, :].rearrange("(j p) d -> p j d", p=128))
    ld("WA", WA.ap()[:, :, :], wk_h.rearrange("(a p) d -> p a d", p=128))
    ld("WB", WB.ap()[:, :, :], wv_h.rearrange("(a p) d -> p a d", p=128))
    ld("PLOAD", PLOAD.ap()[:, :, :], pj_h.rearrange("(a p) d -> p a d", p=128))
    ld("B1S", B1S.ap()[:, :], b1_h.rearrange("(a p) -> p a", p=128), slow=True)
    ld("B2S", B2S.ap()[:, :], b2_h.rearrange("(a p) -> p a", p=128), slow=True)
    ld("B3S", B3S.ap()[:, :], b3_h.rearrange("(a p) -> p a", p=128), slow=True)
    ld("GB1", GB1.ap()[:, :], gb1_h.rearrange("(a p) -> p a", p=128), slow=True)
    ld("GB2", GB2.ap()[:, :], gb2_h.rearrange("(a p) -> p a", p=128), slow=True)
    ld("W4S", W4S.ap()[:, :, :], w4_h.rearrange("(a p) o -> p a o", p=128), slow=True)
    ld("GW3", GW3.ap()[:, :, :], g3_h.rearrange("(a p) o -> p a o", p=128), slow=True)
    ld("GW2", W3S.ap()[:, :, :], g2_h.rearrange("(a p) o -> p a o", p=128))
    ld("BOREP", BOREP.ap()[:, :], bo_h.broadcast_to([128, D]))
    ld("ROWS", ROWS.ap()[0:1, R_B4:R_B4 + 1], b4_h[0:1, 0:1])
    ld("ROWS", ROWS.ap()[0:1, R_GB3:R_GB3 + 1], gb3_h[0:1, 0:1])
    ld("B4C", B4HC.ap()[:, :], b4_h.broadcast_to([128, 1]))
    ld("B3C", B3HC.ap()[:, :], gb3_h.broadcast_to([128, 1]))
    ld("rar", ARCOL.ap()[:, :], ar_h.broadcast_to([128, 1]))

    def init_consts(g):
        g.memset(IDENT[:, :], 0.0)
        g.affine_select(out=IDENT[:, :], in_=IDENT[:, :],
                        compare_op=ALU.not_equal, fill=1.0, base=0,
                        pattern=[[-1, 128]], channel_multiplier=1)
        g.memset(ONESC[:, :], 1.0)
        g.memset(ONES1[:, :], 1.0)
        g.memset(BLK.ap()[:, :, :], 0.0)
        for c in range(8):
            g.memset(BLK.ap()[0:64, c, 2 * c:2 * c + 1], 0.5 * NRM * NRM)
            g.memset(BLK.ap()[64:128, c, 2 * c + 1:2 * c + 2], 0.5 * NRM * NRM)
        g.memset(CTX[:, :], 0.0)
        ctx3i = CTX.ap().rearrange("p (b w) -> p b w", w=66)
        g.memset(ctx3i[:, :, 65:66], 1.0)
        g.memset(ARENA[:, :], 0.0)
        g.memset(CONSTS.ap()[:, 0:1], LNEPS)
        g.memset(CONSTS.ap()[:, 1:2], EPSK)
        g.memset(CONSTS.ap()[:, 2:3], LNEPSK)
        g.memset(CONSTS.ap()[:, 3:4], 0.1)
        g.memset(NEG32[:, :], -1.0)
        g.memset(CSFM[:, :], 0.0)
        return g.memset(PRJCAT[:, :], 0.0)
    OP("g", init_consts, reads=["ROWS", "rar"],
       writes=["IDENT", "ONES", "BLK", "CTX", "CSUM1", "PRJ", "CONSTS", "NEG32"])

    def e(s):
        s.activation(B4HC.ap()[:, :], B4HC.ap()[:, :], AF.Copy, scale=0.5)
        return s.activation(B3HC.ap()[:, :], B3HC.ap()[:, :], AF.Copy, scale=0.5)
    OP("s", e, reads=["B4C", "B3C"], writes=["halves", "B4C", "B3C"])

    def e(t):
        for q in range(2):
            ins = t.matmul(P[0].ap()[0:64, 128 * q:128 * (q + 1)],
                           PLOAD.ap()[:, q, :], IDENT[:, :],
                           is_transpose=True, start=(q == 0), stop=(q == 1))
        return ins
    OP("t", e, reads=["PLOAD", "IDENT"], writes=["P0"], dup=True)
    OP("s", lambda s: s.activation(PRJCAT[0:64, 0:256], P[0].ap()[0:64, 0:256],
                                   AF.Copy, scale=NRM),
       reads=["P0", "PRJ"], writes=["PRJLO_w"], dup=True)
    def e(g):
        return g.dma_start(out=PRJCAT[64:128, 256:512],
                           in_=PRJCAT[0:64, 0:256]).then_inc(d_g, 16)
    OP("g", e, reads=["PRJLO_w", "PRJ"], writes=["PRJHI_w"], ndma=1)
    PRJ_R = ["PRJLO_w", "PRJHI_w"]
    def e(s):
        return s.add_instruction(mybir.InstLoadActFuncSet(
            name=nc.get_next_instruction_name(), act_func_set_id=6, ins=[], outs=[]))
    OP("s", e)

    ST4 = lambda st: st * 4
    bc4 = BC.ap().rearrange("p (j d) -> p j d", j=4)
    ctx3 = CTX.ap().rearrange("p (b w) -> p b w", w=66)
    e32 = BE.ap().rearrange("p (c t) -> p c t", c=32)

    def emit_loadA(st):
        ld("BA", BA.ap()[:, :, :],
           x_h[st * TST:(st + 1) * TST, :].rearrange("(j p) d -> p j d", p=128))

    def emit_lnA(st):
        for j in range(NSUB):
            OP("v", lambda v, j=j, st=st: v.reduce_sum(
                LN1M.ap()[:, ST4(st) + j:ST4(st) + j + 1], BA.ap()[:, j, :], axis=AX.X),
               reads=["BA"], writes=["LN1M"])
        OP("s", lambda s, st=st: s.activation(
            LN1M.ap()[:, ST4(st):ST4(st) + 4], LN1M.ap()[:, ST4(st):ST4(st) + 4],
            AF.Copy, scale=1.0 / D), writes=["LN1M"])
        for j in range(NSUB):
            OP("v", lambda v, j=j, st=st: v.tensor_scalar(
                BA.ap()[:, j, :], BA.ap()[:, j, :],
                LN1M.ap()[:, ST4(st) + j:ST4(st) + j + 1], None, op0=ALU.subtract),
               reads=["LN1M"], writes=["BA"])
            OP("s", lambda s, j=j: s.activation(
                BH.ap()[:, 0:1024], BA.ap()[:, j, :], AF.Square),
               reads=["BA"], writes=["ksq0", "ksq1"])
            OP("v", lambda v, j=j: v.reduce_sum(
                SS4.ap()[:, j:j + 1], BH.ap()[:, 0:1024], axis=AX.X),
               reads=["ksq0"], writes=["SS4"])
        OP("s", lambda s: s.activation(TMP4[:, :], SS4[:, :], AF.Ln,
                                       bias=CONSTS.ap()[:, 0:1], scale=1.0 / D),
           reads=["SS4", "CONSTS"], writes=["TMP4"])
        OP("s", lambda s, st=st: s.activation(
            LN1I.ap()[:, ST4(st):ST4(st) + 4], TMP4[:, :], AF.Exp, scale=-0.5),
           reads=["TMP4"], writes=["LN1I"])
        for j in range(NSUB):
            OP("v", lambda v, j=j, st=st: v.tensor_scalar(
                BA.ap()[:, j, :], BA.ap()[:, j, :],
                LN1I.ap()[:, ST4(st) + j:ST4(st) + j + 1], None, op0=ALU.mult),
               reads=["LN1I"], writes=["BA"])

    emit_loadA(0)
    emit_lnA(0)
    for st in range(NST):
        for c in range(8):
            b = 6 + (c % 2)
            def e(t, c=c, b=b):
                for j in range(NSUB):
                    ins = t.matmul(P[b].ap()[:, 128 * j:128 * (j + 1)],
                                   BA.ap()[:, j, 128 * c:128 * (c + 1)], IDENT[:, :],
                                   is_transpose=True, start=(j == 0), stop=(j == NSUB - 1))
                return ins
            OP("t", e, reads=["BA", "IDENT"], writes=[f"P{b}"])
            OP("v", lambda v, c=c, b=b: v.reduce_sum(
                TMP32.ap()[:, c:c + 1], P[b].ap()[:, :], axis=AX.X),
               reads=[], writes=[f"P{b}", "SXT"])
            OP("s", lambda s, c=c, b=b: s.copy(BB.ap()[:, c, :], P[b].ap()[:, :]),
               reads=[f"P{b}"], writes=["BB"])
        vgap()
        OP("v", lambda v: v.tensor_tensor(CSFM[:, :], CSFM[:, :],
                                          TMP32.ap()[:, 0:8], op=ALU.add),
           reads=["SXT"], writes=["CSFMacc"])
        if st + 1 < NST:
            emit_loadA(st + 1)
            emit_lnA(st + 1)
        for j in range(NSUB):
            for h in range(2):
                b = 6 + h
                def e(t, j=j, h=h, b=b):
                    for ki in range(8):
                        ins = t.matmul(P[b].ap()[:, :],
                                       BB.ap()[:, ki, 128 * j:128 * (j + 1)],
                                       WB.ap()[:, ki, 512 * h:512 * (h + 1)],
                                       start=(ki == 0), stop=(ki == 7))
                    return ins
                OP("t", e, reads=["BB", "WB"], writes=[f"P{b}"])
                OP("s", lambda s, j=j, h=h, b=b: s.copy(
                    bc4[:, j, 512 * h:512 * (h + 1)], P[b].ap()[:, :]),
                   reads=[f"P{b}"], writes=["BC"])
        for o in range(8):
            b = 6 + (o % 2)
            def e(t, o=o, b=b):
                for ki in range(8):
                    ins = t.matmul(P[b].ap()[:, :],
                                   WA.ap()[:, ki, 128 * o:128 * (o + 1)],
                                   BB.ap()[:, ki, :],
                                   start=(ki == 0), stop=(ki == 7))
                return ins
            OP("t", e, reads=["BB", "WA"], writes=[f"P{b}"])
            OP("s", lambda s, o=o, b=b: s.copy(BD.ap()[:, o, :], P[b].ap()[:, :]),
               reads=[f"P{b}"], writes=["BD"])
        for c in range(8):
            kq = f"ksq{c % 2}"
            OP("v", lambda v, c=c: v.tensor_tensor(
                BH.ap()[:, 512 * (c % 2):512 * (c % 2 + 1)],
                BD.ap()[:, c, :], BD.ap()[:, c, :], op=ALU.mult),
               reads=["BD"], writes=[kq])
            OP("t", lambda t, c=c: t.matmul(
                P[5].ap()[0:16, :], BLK.ap()[:, c, :],
                BH.ap()[:, 512 * (c % 2):512 * (c % 2 + 1)],
                start=(c == 0), stop=(c == 7)),
               reads=[kq, "BLK"], writes=["P5"])
        OP("s", lambda s: s.activation(NDS[0:16, :], P[5].ap()[0:16, :],
                                       AF.Copy, scale=-1.0),
           reads=["P5"], writes=["NDS"], dup=True)
        def e(t):
            for j in range(NSUB):
                ins = t.matmul(P[5].ap()[:, 16 * j:16 * (j + 1)],
                               NDS.ap()[0:16, 128 * j:128 * (j + 1)],
                               IDENT[0:16, 0:16],
                               is_transpose=True, start=(j == 0), stop=(j == NSUB - 1))
            return ins
        OP("t", e, reads=["NDS", "IDENT"], writes=["P5"], dup=True)
        OP("s", lambda s: s.copy(NDT.ap()[:, :, :].rearrange("p a b -> p (a b)"),
                                 P[5].ap()[:, 0:64]),
           reads=["P5"], writes=["NDT"], dup=True)
        for j in range(NSUB):
            kbuf = BE if j % 2 == 0 else BF
            kres = f"kp{j % 2}"
            for p in range(8):
                b = 6 + (p % 2)
                def e(t, j=j, p=p, b=b):
                    return t.matmul(P[b].ap()[:, :],
                                    BD.ap()[:, p, 128 * j:128 * (j + 1)],
                                    PRJCAT.ap()[:, :], start=True, stop=True)
                OP("t", e, reads=["BD"] + PRJ_R, writes=[f"P{b}"])
                col = (ST4(st) + j) * 8 + p
                OP("v", lambda v, b=b, col=col: v.reduce_max(
                    UMAT.ap()[:, col:col + 1], P[b].ap()[:, :], axis=AX.X),
                   reads=[f"P{b}"], writes=["UMAT"])
                def e(s, j=j, p=p, b=b, kbuf=kbuf):
                    for i in range(2):
                        ins = s.activation(
                            kbuf.ap()[:, 512 * p + 256 * i:512 * p + 256 * (i + 1)],
                            P[b].ap()[:, 256 * i:256 * (i + 1)], AF.Exp,
                            bias=NDT.ap()[:, j, 2 * p + i:2 * p + i + 1])
                    return ins
                OP("s", e, reads=["NDT"], writes=[f"{kres}_{p}", f"P{b}"])
            for b in range(4):
                def e(t, j=j, b=b, kbuf=kbuf):
                    for cc in range(8):
                        c = 8 * b + cc
                        hd = c // 2
                        t.matmul(P[b].ap()[:, 64 * cc:64 * (cc + 1)],
                                 kbuf.ap()[:, 128 * c:128 * (c + 1)],
                                 bc4[:, j, 64 * hd:64 * (hd + 1)],
                                 start=(j == 0 and cc == 0),
                                 stop=(j == NSUB - 1 and cc == 7))
                        ins = t.matmul(P[4].ap()[:, c:c + 1],
                                       kbuf.ap()[:, 128 * c:128 * (c + 1)],
                                       ONESC.ap()[:, 0:1],
                                       start=(j == 0 and c == 0),
                                       stop=(j == NSUB - 1 and c == 31))
                    return ins
                OP("t", e,
                   reads=[f"{kres}_{2 * b}", f"{kres}_{2 * b + 1}", "BC", "ONES"],
                   writes=[f"P{b}", "P4"])
        for b in range(4):
            OP("v", lambda v, b=b: v.tensor_tensor(
                ctx3[:, 8 * b:8 * (b + 1), 0:64],
                ctx3[:, 8 * b:8 * (b + 1), 0:64],
                P[b].ap().rearrange("p (a d) -> p a d", a=8), op=ALU.add),
               reads=[f"P{b}"], writes=["CTX"])
        OP("v", lambda v: v.tensor_tensor(
            CTX.ap().rearrange("p (b w) -> p b w", w=66)[:, :, 64],
            CTX.ap().rearrange("p (b w) -> p b w", w=66)[:, :, 64],
            P[4].ap()[:, 0:32], op=ALU.add),
           reads=["P4"], writes=["CTX"])

    if stage == "A":
        def fin(y_eng):
            y_eng.wait_ge(d_g, bld.cnt["DG"])
            y_eng.wait_ge(d_y, bld.cnt["DY"])
        return nc, bld, SEM, es, fin

    OP("v", lambda v: v.reduce_max(TMP4.ap()[:, 0:1], UMAT[:, :], axis=AX.X),
       reads=["UMAT"], writes=["TMP4"])
    vgap()
    vgap()
    vgap("BSLK")
    OP("t", lambda t: t.matmul(P[0].ap()[0:1, 0:128], TMP4.ap()[:, 0:1],
                               IDENT[:, :], is_transpose=True, start=True, stop=True),
       reads=["TMP4", "IDENT", "BSLK"], writes=["P0"], dup=True)
    OP("s", lambda s: s.copy(ROWS.ap()[0:1, R_UR:R_UR + 128], P[0].ap()[0:1, 0:128]),
       reads=["P0"], writes=["UROW"], dup=True)
    OP("v", lambda v: v.reduce_max(ROWS.ap()[0:1, R_SLOC:R_SLOC + 1],
                                   ROWS.ap()[0:1, R_UR:R_UR + 128], axis=AX.X),
       reads=["UROW"], writes=["SLOC"], dup=True)
    vgap()
    vgap()
    vgap("BSLK2")
    OP("t", lambda t: t.matmul(P[1].ap()[0:1, 0:8], ROWS.ap()[0:1, R_SLOC:R_SLOC + 1],
                               ONESC.ap()[0:1, 0:8], start=True, stop=True),
       reads=["SLOC", "ONES", "BSLK2"], writes=["P1"], dup=True)
    OP("s", lambda s: s.copy(ROWS.ap()[0:1, R_S8:R_S8 + 8], P[1].ap()[0:1, 0:8]),
       reads=["P1"], writes=["S8"], dup=True)
    def e(g):
        return g.dma_start(out=ccin_h[0:1, 0:8],
                           in_=ROWS.ap()[0:1, R_S8:R_S8 + 8]).then_inc(d_g, 16)
    OP("g", e, reads=["S8"], writes=["ccin"], ndma=1)
    def e(g):
        return g.collective_compute(
            "AllReduce", ALU.max, replica_groups=[list(range(8))],
            ins=[ccin_h[0:1, 0:8]], outs=[ccout_h[0:1, 0:8]])
    OP("g", e, reads=["ccin"], writes=["ccout"], cc=True)
    def e(g):
        return g.dma_start(out=ROWS.ap()[0:1, R_SG:R_SG + 8],
                           in_=ccout_h[0:1, 0:8]).then_inc(d_g, 16)
    OP("g", e, reads=["ccout"], writes=["SG"], ndma=1)
    OP("s", lambda s: s.activation(ROWS.ap()[0:1, R_BETA:R_BETA + 8],
                                   ROWS.ap()[0:1, R_SG:R_SG + 8], AF.Exp,
                                   bias=CONSTS.ap()[0:1, 2:3]),
       reads=["SG", "CONSTS"], writes=["BETA"], dup=True)
    OP("s", lambda s: s.activation(ROWS.ap()[0:1, R_BN:R_BN + 8],
                                    ROWS.ap()[0:1, R_BETA:R_BETA + 8],
                                    AF.Copy, scale=float(NTOK)),
       reads=["BETA"], writes=["BN"], dup=True)
    OP("s", lambda s: s.copy(ROWS.ap()[0:1, R_B2:R_B2 + 1],
                             ROWS.ap()[0:1, R_BETA:R_BETA + 1]),
       reads=["BETA"], writes=["B2"], dup=True)
    OP("s", lambda s: s.copy(ROWS.ap()[0:1, R_B2 + 1:R_B2 + 2],
                             ROWS.ap()[0:1, R_BN:R_BN + 1]),
       reads=["BN"], writes=["B2"], dup=True)
    OP("t", lambda t: t.matmul(P[2].ap()[:, 0:2], ONES1.ap()[0:1, :],
                               ROWS.ap()[0:1, R_B2:R_B2 + 2], start=True, stop=True),
       reads=["B2", "ONES"], writes=["P2"], dup=True)
    OP("s", lambda s: s.copy(BCAST[:, :], P[2].ap()[:, 0:2]),
       reads=["P2"], writes=["BCASTw"], dup=True)
    vgap()
    vgap("ASLK")
    for h in range(2):
        def e(t, h=h):
            for ki in range(8):
                ins = t.matmul(P[4 + h].ap()[0:1, :], CSFM.ap()[:, ki:ki + 1],
                               WB.ap()[:, ki, 512 * h:512 * (h + 1)],
                               start=(ki == 0), stop=(ki == 7))
            return ins
        OP("t", e, reads=["CSFMacc", "WB", "ASLK"], writes=[f"P{4 + h}"], dup=True)
        OP("s", lambda s, h=h: s.copy(
            ARENA.ap()[0:1, A_CS + 512 * h:A_CS + 512 * (h + 1)],
            P[4 + h].ap()[0:1, :]),
           reads=[f"P{4 + h}"], writes=["SVROW", "CSUM1"], dup=True)
    for h in range(2):
        OP("t", lambda t, h=h: t.matmul(
            P[6 + h].ap()[:, :], ONES1.ap()[0:1, :],
            ARENA.ap()[0:1, A_CS + 512 * h:A_CS + 512 * (h + 1)],
            start=True, stop=True),
           reads=["SVROW", "ONES"], writes=[f"P{6 + h}"], dup=True)
        OP("s", lambda s, h=h: s.copy(SVREP.ap()[:, 512 * h:512 * (h + 1)],
                                      P[6 + h].ap()[:, :]),
           reads=[f"P{6 + h}"], writes=["SVREPw"], dup=True)

    if stage in ("AB", "dumpA"):
        if stage == "dumpA":
            dumps = [
                (y_h[0:128, :], CTX.ap()[:, 0:1024], ["CTX"]),
                (y_h[128:256, :], CTX.ap()[:, 1024:2048], ["CTX"]),
                (y_h[256:384, 0:64], CTX.ap()[:, 2048:2112], ["CTX"]),
                (y_h[384:512, 0:32], LN1M.ap()[:, :], ["LN1M"]),
                (y_h[384:512, 32:64], LN1I.ap()[:, :], ["LN1I"]),
                (y_h[512:640, :], SVREP.ap()[:, :], ["SVREPw"]),
                (y_h[640:768, 0:2], BCAST.ap()[:, :], ["BCASTw"]),
                (y_h[640:768, 4:8], SS4.ap()[:, :], ["SS4"]),
                (y_h[640:768, 8:12], TMP4.ap()[:, :], ["TMP4"]),
                (y_h[768:769, 0:512], ROWS.ap()[0:1, :], ["BETA", "BN", "SG", "S8"]),
                (y_h[769:770, 0:1024], ARENA.ap()[0:1, :], ["SVROW"]),
                (y_h[896:1024, 0:256], UMAT.ap()[:, :], ["UMAT"]),
            ]
            for dst, srcap, rds in dumps:
                def e(g, _a=(dst, srcap)):
                    return g.dma_start(out=_a[0], in_=_a[1]).then_inc(d_g, 16)
                OP("g", e, reads=rds, writes=["ydump"], ndma=1)
        def fin(y_eng):
            y_eng.wait_ge(d_g, bld.cnt["DG"])
            y_eng.wait_ge(d_y, bld.cnt["DY"])
        return nc, bld, SEM, es, fin

    ld("WA", WA.ap()[:, :, :], wq_h.rearrange("(a p) d -> p a d", p=128))
    ld("WB", WB.ap()[:, :, :], wo_h.rearrange("(a p) d -> p a d", p=128))

    x2v = BF.ap().rearrange("p (a d) -> p a d", a=4)
    atm = BF.ap()[:, 2048:3072]
    afm = BF.ap()[:, 3072:4096].rearrange("p (a c) -> p a c", a=8)

    def emit_chead(st):
        for j in range(NSUB):
            ld(f"BAc{j}", BA.ap()[:, j, :],
               x_h[st * TST + 128 * j:st * TST + 128 * (j + 1), :],
               reads=[f"BAg{j}"],
               extra_writes=(["BA"] if st == 0 else []))
        for j in range(NSUB):
            OP("v", lambda v, j=j, st=st: v.tensor_scalar(
                bc4[:, j, :], BA.ap()[:, j, :],
                LN1M.ap()[:, ST4(st) + j:ST4(st) + j + 1],
                LN1I.ap()[:, ST4(st) + j:ST4(st) + j + 1],
                op0=ALU.subtract, op1=ALU.mult),
               reads=[f"BAc{j}", "LN1M", "LN1I"], writes=["BC"])
        vgap()
        vgap("CHSLK")
        for c in range(8):
            b = 6 + (c % 2)
            def e(t, c=c, b=b):
                for j in range(NSUB):
                    ins = t.matmul(P[b].ap()[:, 128 * j:128 * (j + 1)],
                                   bc4[:, j, 128 * c:128 * (c + 1)], IDENT[:, :],
                                   is_transpose=True,
                                   start=(j == 0), stop=(j == NSUB - 1))
                return ins
            OP("t", e, reads=["BC", "IDENT", "CHSLK"], writes=[f"P{b}"])
            OP("s", lambda s, c=c, b=b: s.copy(BB.ap()[:, c, :], P[b].ap()[:, :]),
               reads=[f"P{b}"], writes=["BB"])
        for o in range(8):
            b = 6 + (o % 2)
            def e(t, o=o, b=b):
                for ki in range(8):
                    ins = t.matmul(P[b].ap()[:, :],
                                   WA.ap()[:, ki, 128 * o:128 * (o + 1)],
                                   BB.ap()[:, ki, :],
                                   start=(ki == 0), stop=(ki == 7))
                return ins
            OP("t", e, reads=["BB", "WA"], writes=[f"P{b}"])
            OP("s", lambda s, o=o, b=b: s.copy(BD.ap()[:, o, :], P[b].ap()[:, :]),
               reads=[f"P{b}"], writes=["BD"])
        for c in range(8):
            kq = f"qsq{c % 2}"
            OP("v", lambda v, c=c: v.tensor_tensor(
                BH.ap()[:, 512 * (c % 2):512 * (c % 2 + 1)],
                BD.ap()[:, c, :], BD.ap()[:, c, :], op=ALU.mult),
               reads=["BD"], writes=[kq, f"ksq{c % 2}"])
            OP("t", lambda t, c=c: t.matmul(
                P[0].ap()[0:16, :], BLK.ap()[:, c, :],
                BH.ap()[:, 512 * (c % 2):512 * (c % 2 + 1)],
                start=(c == 0), stop=(c == 7)),
               reads=[kq, "BLK"], writes=["P0"])
        OP("s", lambda s: s.activation(NDS[0:16, :], P[0].ap()[0:16, :],
                                       AF.Copy, scale=-1.0),
           reads=["P0"], writes=["NDS"], dup=True)
        def e(t):
            for j in range(NSUB):
                ins = t.matmul(P[0].ap()[:, 16 * j:16 * (j + 1)],
                               NDS.ap()[0:16, 128 * j:128 * (j + 1)],
                               IDENT[0:16, 0:16],
                               is_transpose=True,
                               start=(j == 0), stop=(j == NSUB - 1))
            return ins
        OP("t", e, reads=["NDS", "IDENT"], writes=["P0"], dup=True)
        OP("s", lambda s: s.copy(NDT.ap()[:, :, :].rearrange("p a b -> p (a b)"),
                                 P[0].ap()[:, 0:64]),
           reads=["P0"], writes=["NDT"], dup=True)

    def emit_qde(st, j):
        for p in range(8):
            b = 6 + (p % 2)
            def e(t, j=j, p=p, b=b):
                return t.matmul(P[b].ap()[:, :],
                                BD.ap()[:, p, 128 * j:128 * (j + 1)],
                                PRJCAT.ap()[:, :], start=True, stop=True)
            OP("t", e, reads=["BD"] + PRJ_R, writes=[f"P{b}"])
            OP("v", lambda v, p=p, b=b: v.reduce_max(
                NSTAB.ap()[:, 2 * p:2 * p + 2],
                P[b].ap().rearrange("p (a m) -> p a m", a=2),
                axis=AX.X, negate=True),
               reads=[f"P{b}"], writes=["NSTABr"])
            vgap()
            OP("v", lambda v, j=j, p=p: v.tensor_tensor(
                BIASQ.ap()[:, 2 * p:2 * p + 2], NSTAB.ap()[:, 2 * p:2 * p + 2],
                NDT.ap()[:, j, 2 * p:2 * p + 2], op=ALU.add),
               reads=["NSTABr", "NDT"], writes=["BIASQw"])
            def e(s, j=j, p=p, b=b):
                for i in range(2):
                    ins = s.activation(
                        BC.ap()[:, 512 * p + 256 * i:512 * p + 256 * (i + 1)],
                        P[b].ap()[:, 256 * i:256 * (i + 1)], AF.Exp,
                        bias=BIASQ.ap()[:, 2 * p + i:2 * p + i + 1])
                return ins
            OP("s", e, reads=["BIASQw"], writes=["BC", f"P{b}"])
        for g8 in range(8):
            b = g8 % 2
            def e(t, g8=g8, b=b):
                for q in range(4):
                    mc = 4 * g8 + q
                    ins = t.matmul(P[b].ap()[:, 128 * q:128 * (q + 1)],
                                   BC.ap()[:, 128 * mc:128 * (mc + 1)], IDENT[:, :],
                                   is_transpose=True, start=(q == 0), stop=(q == 3))
                return ins
            OP("t", e, reads=["BC", "IDENT"], writes=[f"P{b}"])
            ew = ["BE"]
            if st == 0 and j == 0 and g8 == 0:
                ew += [f"kp0_{p}" for p in range(8)]
            OP("s", lambda s, g8=g8, b=b: s.activation(
                BE.ap()[:, 512 * g8:512 * (g8 + 1)], P[b].ap()[:, :],
                AF.Copy, bias=EPSK),
               reads=[f"P{b}", "CONSTS"], writes=ew)

    def emit_odc(st, j):
        def e(t):
            for hh in range(16):
                for q in range(2):
                    kc = 2 * hh + q
                    ins = t.matmul(P[1].ap()[:, 2 * hh:2 * hh + 2],
                                   e32[:, kc, :], ctx3[:, kc, 64:66],
                                   start=(hh == 0 and q == 0),
                                   stop=(hh == 15 and q == 1))
            return ins
        OP("t", e, reads=["BE", "CTX"], writes=["P1"])
        for b2 in (2, 3):
            def e(t, b2=b2):
                for hh in range(8):
                    h = 8 * (b2 - 2) + hh
                    for q in range(2):
                        kc = 2 * h + q
                        ins = t.matmul(P[b2].ap()[:, 64 * hh:64 * (hh + 1)],
                                       e32[:, kc, :], ctx3[:, kc, 0:64],
                                       start=(hh == 0 and q == 0),
                                       stop=(hh == 7 and q == 1))
                return ins
            OP("t", e, reads=["BE", "CTX"], writes=[f"P{b2}"])
        OP("s", lambda s: s.copy(TMP32[:, :], P[1].ap()[:, 0:32]),
           reads=["P1"], writes=["P1c"])

    def emit_tail(st, j):
        p1s = TMP32.ap().rearrange("p (a b) -> p a b", a=16)
        for b2 in (2, 3):
            hb = 8 * (b2 - 2)
            OP("v", lambda v, hb=hb: v.scalar_tensor_tensor(
                DEN.ap()[:, hb:hb + 8],
                p1s[:, hb:hb + 8, 1], BCAST.ap()[:, 1:2],
                p1s[:, hb:hb + 8, 0], op0=ALU.mult, op1=ALU.add),
               reads=["P1c", "BCASTw"], writes=["DENw"])
            OP("v", lambda v, hb=hb: v.tensor_scalar(
                RBD.ap()[:, hb:hb + 8], p1s[:, hb:hb + 8, 1],
                BCAST.ap()[:, 0:1], None, op0=ALU.mult),
               reads=["P1c", "BCASTw"], writes=["RBDw"])
        for b2 in (2, 3):
            hb = 8 * (b2 - 2)
            OP("v", lambda v, hb=hb: v.reciprocal(
                DINV.ap()[:, hb:hb + 8], DEN.ap()[:, hb:hb + 8]),
               reads=["DENw"], writes=["DINVw"])
        for b2 in (2, 3):
            hb = 8 * (b2 - 2)
            OP("v", lambda v, hb=hb: v.tensor_tensor(
                RBD.ap()[:, hb:hb + 8], RBD.ap()[:, hb:hb + 8],
                DINV.ap()[:, hb:hb + 8], op=ALU.mult),
               reads=["DINVw"], writes=["RBDw"])
        for b2 in (2, 3):
            hb = 8 * (b2 - 2)
            ew = ["BF"]
            if st == 0 and j == 0 and b2 == 2:
                ew += [f"kp1_{p}" for p in range(8)]
            OP("v", lambda v, hb=hb, b2=b2: v.tensor_tensor(
                atm[:, 512 * (b2 - 2):512 * (b2 - 1)].rearrange(
                    "p (a d) -> p a d", a=8),
                P[b2].ap().rearrange("p (a d) -> p a d", a=8),
                DINV.ap()[:, hb:hb + 8].unsqueeze(2).broadcast_to([128, 8, 64]),
                op=ALU.mult),
               reads=[f"P{b2}", "DINVw", "BF"], writes=ew)
        for b2 in (2, 3):
            hb = 8 * (b2 - 2)
            t2off = 1024 if b2 == 2 else 512
            OP("v", lambda v, hb=hb, b2=b2, t2off=t2off: v.tensor_tensor(
                BH.ap()[:, t2off:t2off + 512].rearrange(
                    "p (a d) -> p a d", a=8),
                SVREP.ap()[:, 512 * (b2 - 2):512 * (b2 - 1)].rearrange(
                    "p (a d) -> p a d", a=8),
                RBD.ap()[:, hb:hb + 8].unsqueeze(2).broadcast_to([128, 8, 64]),
                op=ALU.mult),
               reads=["SVREPw", "RBDw"], writes=["T2w", "qsq1"])
        for b2 in (2, 3):
            t2off = 1024 if b2 == 2 else 512
            OP("v", lambda v, b2=b2, t2off=t2off: v.tensor_tensor(
                atm[:, 512 * (b2 - 2):512 * (b2 - 1)],
                atm[:, 512 * (b2 - 2):512 * (b2 - 1)],
                BH.ap()[:, t2off:t2off + 512],
                op=ALU.add),
               reads=["T2w", "BF"], writes=["BF"])
        vgap()
        vgap("CSLK")
        for g2 in range(2):
            b = g2
            def e(t, g2=g2, b=b):
                for q in range(4):
                    c = 4 * g2 + q
                    ins = t.matmul(P[b].ap()[:, 128 * q:128 * (q + 1)],
                                   atm[:, 128 * c:128 * (c + 1)], IDENT[:, :],
                                   is_transpose=True, start=(q == 0), stop=(q == 3))
                return ins
            OP("t", e, reads=["BF", "IDENT", "CSLK"], writes=[f"P{b}"])
            OP("s", lambda s, g2=g2, b=b: s.copy(
                afm[:, 4 * g2:4 * (g2 + 1), :].rearrange("p a c -> p (a c)"),
                P[b].ap()[:, :]),
               reads=[f"P{b}", "BF"], writes=["BF"])
        for h in range(2):
            b = 2 + h
            def e(t, h=h, b=b):
                for ki in range(8):
                    ins = t.matmul(P[b].ap()[:, :], afm[:, ki, :],
                                   WB.ap()[:, ki, 512 * h:512 * (h + 1)],
                                   start=(ki == 0), stop=(ki == 7))
                return ins
            OP("t", e, reads=["BF", "WB"], writes=[f"P{b}"])
        for h in range(2):
            b = 2 + h
            OP("v", lambda v, j=j, h=h, b=b: v.tensor_tensor(
                x2v[:, j % 2, 512 * h:512 * (h + 1)],
                x2v[:, (j + 1) % 2, 512 * h:512 * (h + 1)],
                P[b].ap()[:, :], op=ALU.add),
               reads=[f"P{b}", "BF", f"xbo{j}"], writes=["BF", f"x2v{j % 2}"])
        vgap()
        vgap("CSLK2")
        def e(g, j=j, st=st):
            return g.dma_start(
                out=(y_h if stage == "x2" else x2s_h)[
                    st * TST + 128 * j:st * TST + 128 * (j + 1), :],
                in_=x2v[:, j % 2, :]).then_inc(d_g, 16)
        OP("g", e, reads=["BF"], writes=["x2st"], ndma=1)
        idx = ST4(st) + j
        vgap()
        OP("v", lambda v, j=j, idx=idx: v.reduce_sum(
            LN2M.ap()[:, idx:idx + 1], x2v[:, j % 2, :], axis=AX.X),
           reads=["BF"], writes=["LN2M"])
        OP("s", lambda s, j=j: s.activation(
            BH.ap()[:, 0:1024], x2v[:, j % 2, :], AF.Square),
           reads=["BF"], writes=["qsq0", "qsq1", "ksq0", "ksq1"])
        OP("v", lambda v, j=j: v.reduce_sum(
            SS4.ap()[:, j:j + 1], BH.ap()[:, 0:1024], axis=AX.X),
           reads=["qsq0"], writes=["SS4"])
        OP("s", lambda s, idx=idx: s.activation(
            LN2M.ap()[:, idx:idx + 1], LN2M.ap()[:, idx:idx + 1], AF.Copy,
            scale=1.0 / D),
           reads=[], writes=["LN2M"])
        OP("s", lambda s, idx=idx: s.activation(
            TMP4B.ap()[:, 0:1], LN2M.ap()[:, idx:idx + 1], AF.Square),
           reads=["LN2M"], writes=["TMP4B"])
        OP("s", lambda s: s.activation(
            TMP4B.ap()[:, 1:2], TMP4B.ap()[:, 0:1], AF.Copy,
            scale=-1.0, bias=LNEPS),
           reads=["CONSTS"], writes=["TMP4B"])
        OP("s", lambda s, j=j: s.activation(
            TMP4B.ap()[:, 2:3], SS4.ap()[:, j:j + 1], AF.Ln, scale=1.0 / D,
            bias=TMP4B.ap()[:, 1:2]),
           reads=["SS4"], writes=["TMP4B"])
        OP("s", lambda s, idx=idx: s.activation(
            LN2I.ap()[:, idx:idx + 1], TMP4B.ap()[:, 2:3], AF.Exp, scale=-0.5),
           reads=["TMP4B"], writes=[f"LN2I{j}", "LN2I"])

    def emit_xbo(st, j):
        OP("g", lambda g, j=j: g.tensor_tensor(
            x2v[:, (j + 1) % 2, :], BA.ap()[:, j, :], BOREP[:, :], op=ALU.add),
           reads=[f"BAc{j}", "BOREP", "BF"],
           writes=["BF", f"x2v{(j + 1) % 2}", f"xbo{j}", f"BAg{j}"])

    def emit_mm1(st, j):
        idx = ST4(st) + j
        for h in range(2):
            OP("t", lambda t, st=st, j=j, h=h, idx=idx: t.matmul(
                P[4 + h].ap()[0:1, :], LN2I.ap()[:, idx:idx + 1],
                x2v[:, j % 2, 512 * h:512 * (h + 1)],
                start=(st == 0 and j == 0), stop=False),
               reads=[f"LN2I{j}", f"x2v{j % 2}", "CSLK2"], writes=[f"P{4 + h}"])

    emit_chead(0)
    emit_qde(0, 0)
    for st in range(NST):
        for j in range(NSUB):
            emit_odc(st, j)
            if j > 0:
                emit_mm1(st, j - 1)
            emit_xbo(st, j)
            if j < NSUB - 1:
                emit_qde(st, j + 1)
            elif st + 1 < NST:
                emit_chead(st + 1)
            emit_tail(st, j)
        if st + 1 < NST:
            emit_qde(st + 1, 0)
        emit_mm1(st, NSUB - 1)

    if stage == "x2":
        def fin(y_eng):
            y_eng.wait_ge(d_g, bld.cnt["DG"])
            y_eng.wait_ge(d_y, bld.cnt["DY"])
        return nc, bld, SEM, es, fin

    def e(v):
        for ki in range(2):
            v.tensor_copy(W4R.ap()[:, ki, :],
                          W4S.ap()[:, ki, 0:1].broadcast_to([128, 128]))
            ins = v.tensor_copy(GW3R.ap()[:, ki, :],
                                GW3.ap()[:, ki, 0:1].broadcast_to([128, 128]))
        return ins
    OP("v", e, reads=["W4S", "GW3"], writes=["W4R", "GW3R"], dup=True)

    def e(t):
        for j in range(32):
            ins = t.matmul(P[0].ap()[0:1, 0:1], LN2M.ap()[:, j:j + 1],
                           LN2I.ap()[:, j:j + 1], start=(j == 0), stop=(j == 31))
        return ins
    OP("t", e, reads=["LN2M", "LN2I"], writes=["P0"], dup=True)
    OP("s", lambda s: s.copy(ROWS.ap()[0:1, R_S2:R_S2 + 1], P[0].ap()[0:1, 0:1]),
       reads=["P0"], writes=["S2col"], dup=True)
    for h in range(2):
        OP("t", lambda t, h=h: t.matmul(
            P[4 + h].ap()[0:1, :], ROWS.ap()[0:1, R_S2:R_S2 + 1],
            NEG32.ap()[0:1, :], start=False, stop=True),
           reads=["S2col", "NEG32"], writes=[f"P{4 + h}"])
    for h in range(2):
        OP("s", lambda s, h=h: s.activation(
            ARENA.ap()[0:1, 512 * h:512 * (h + 1)], P[4 + h].ap()[0:1, :],
            AF.Copy, scale=1.0 / NTOK),
           reads=[f"P{4 + h}"], writes=["GM", "SVROW", "CSUM1"], dup=True)
    def e(t):
        for q in range(8):
            ins = t.matmul(P[1].ap()[:, q:q + 1],
                           ARENA.ap()[0:1, 128 * q:128 * (q + 1)],
                           IDENT[0:1, 0:1], is_transpose=True,
                           start=(q == 0), stop=(q == 7))
        return ins
    OP("t", e, reads=["GM", "IDENT"], writes=["P1"], dup=True)
    OP("s", lambda s: s.copy(CSFM[:, :], P[1].ap()[:, 0:8]),
       reads=["P1"], writes=["CSFMw", "CSFMacc"], dup=True)

    ld("WA", WA.ap()[:, :, :], w1_h.rearrange("(a p) d -> p a d", p=128))
    ld("WB2", WB.ap()[:, :, 0:512], w2_h.rearrange("(a p) d -> p a d", p=128),
       extra_writes=["WB"])
    ld("WBg", WB.ap()[:, :, 512:1024], g1_h.rearrange("(a p) d -> p a d", p=128),
       extra_writes=["WB"])

    def e(s):
        return s.add_instruction(mybir.InstLoadActFuncSet(
            name=nc.get_next_instruction_name(), act_func_set_id=10, ins=[], outs=[]))
    OP("s", e)

    def e(t):
        for o in range(4):
            for ki in range(8):
                ins = t.matmul(P[2].ap()[:, o:o + 1],
                               WB.ap()[:, ki, 512 + 128 * o:512 + 128 * (o + 1)],
                               CSFM.ap()[:, ki:ki + 1],
                               start=(o == 0 and ki == 0), stop=(o == 3 and ki == 7))
        return ins
    OP("t", e, reads=["CSFMw", "WBg"], writes=["P2"], dup=True)
    for o in range(4):
        OP("s", lambda s, o=o: s.activation(GH1.ap()[:, o:o + 1], P[2].ap()[:, o:o + 1],
                                            AF.Gelu, bias=GB1.ap()[:, o:o + 1]),
           reads=["P2", "GB1"], writes=["GH1w"], dup=True)
    def e(t):
        for o in range(2):
            for ki in range(4):
                ins = t.matmul(P[3].ap()[:, o:o + 1],
                               W3S.ap()[:, ki, 128 * o:128 * (o + 1)],
                               GH1.ap()[:, ki:ki + 1],
                               start=(o == 0 and ki == 0), stop=(o == 1 and ki == 3))
        return ins
    OP("t", e, reads=["GH1w", "GW2"], writes=["P3"], dup=True)
    for o in range(2):
        OP("s", lambda s, o=o: s.activation(GH2.ap()[:, o:o + 1], P[3].ap()[:, o:o + 1],
                                            AF.Gelu, bias=GB2.ap()[:, o:o + 1]),
           reads=["P3", "GB2"], writes=["GH2w"], dup=True)
    def e(t):
        for ki in range(2):
            ins = t.matmul(P[4].ap()[:, 0:1], GW3R.ap()[:, ki, :],
                           GH2.ap()[:, ki:ki + 1], start=(ki == 0), stop=(ki == 1))
        return ins
    OP("t", e, reads=["GH2w", "GW3R"], writes=["P4"], dup=True)
    OP("s", lambda s: s.activation(TGC.ap()[:, :], P[4].ap()[:, 0:1], AF.Tanh,
                                   scale=0.5, bias=B3HC.ap()[:, 0:1]),
       reads=["P4", "B3C"], writes=["TGc"], dup=True)
    OP("t", lambda t: t.matmul(P[5].ap()[0:1, 0:128], ARCOL.ap()[0:1, 0:1],
                               ONES1.ap()[0:1, :], start=True, stop=True),
       reads=["rar", "ONES"], writes=["P5"], dup=True)
    OP("s", lambda s: s.copy(ROWS.ap()[0:1, R_UR:R_UR + 128],
                             P[5].ap()[0:1, 0:128]),
       reads=["P5"], writes=["UROW"], dup=True)
    OP("t", lambda t: t.matmul(P[5].ap()[:, 0:1], ROWS.ap()[0:1, R_UR:R_UR + 128],
                               TGC.ap()[0:1, 0:1], start=True, stop=True),
       reads=["UROW", "TGc"], writes=["P5"], dup=True)
    OP("s", lambda s: s.activation(CB2C.ap()[:, :], P[5].ap()[:, 0:1], AF.Copy,
                                   scale=0.7, bias=0.1),
       reads=["P5", "CONSTS"], writes=["CB2c"], dup=True)
    OP("s", lambda s: s.activation(A2C.ap()[:, :], ARCOL.ap()[:, :], AF.Copy,
                                   scale=0.3),
       reads=["rar"], writes=["A2c"], dup=True)

    ld("W3S", W3S.ap()[:, :, :], w3_h.rearrange("(a p) o -> p a o", p=128),
       extra_writes=["GW2"])

    be4 = BE.ap().rearrange("p (j d) -> p j d", j=4)
    h2v = BE.ap()[:, 0:2048].rearrange("p (a d) -> p a d", a=4)
    h3v = BE.ap()[:, 2048:3072].rearrange("p (a d) -> p a d", a=2)
    crrep = BE.ap()[:, 3072:3584]
    bd2 = BD.ap().rearrange("p a d -> p (a d)")

    for st in range(NST):
        xres = "BA" if st % 2 == 0 else "BF"
        xv4 = (BA.ap()[:, :, :] if st % 2 == 0
               else BF.ap().rearrange("p (j d) -> p j d", j=4))
        ld(xres, xv4[:, :, :],
           x2s_h[st * TST:(st + 1) * TST, :].rearrange("(j p) d -> p j d", p=128),
           reads=["x2st"],
           extra_writes=(["BAc0", "BAc1", "BAc2", "BAc3"] if st == 0 else []))
        for j in range(NSUB):
            OP("v", lambda v, j=j, st=st, xv4=xv4: v.tensor_scalar(
                be4[:, j, :], xv4[:, j, :],
                LN2M.ap()[:, ST4(st) + j:ST4(st) + j + 1],
                LN2I.ap()[:, ST4(st) + j:ST4(st) + j + 1],
                op0=ALU.subtract, op1=ALU.mult),
               reads=[xres, "LN2M", "LN2I"], writes=["BE"])
        vgap()
        vgap("VSLK")
        for c in range(8):
            b = c % 2
            def e(t, c=c, b=b):
                for j in range(NSUB):
                    ins = t.matmul(P[b].ap()[:, 128 * j:128 * (j + 1)],
                                   be4[:, j, 128 * c:128 * (c + 1)], IDENT[:, :],
                                   is_transpose=True, start=(j == 0), stop=(j == NSUB - 1))
                return ins
            OP("t", e, reads=["BE", "IDENT", "VSLK"], writes=[f"P{b}"])
            OP("s", lambda s, c=c, b=b: s.copy(BB.ap()[:, c, :], P[b].ap()[:, :]),
               reads=[f"P{b}"], writes=["BB"])
        cur, curres = BB, "BB"
        for cyc in range(NCYC):
            nxt = BC if cur is BB else BB
            nxtres = "BC" if curres == "BB" else "BB"
            curv = cur.ap() if cur is BB else cur.ap().rearrange("p (a d) -> p a d", a=8)
            nxtv = nxt.ap() if nxt is BB else nxt.ap().rearrange("p (a d) -> p a d", a=8)
            for o in range(8):
                b = 2 + (o % 2)
                if cyc == 0:
                    def e(t, o=o, b=b, curv=curv):
                        for ki in range(8):
                            ins = t.matmul(P[b].ap()[:, :],
                                           WA.ap()[:, ki, 128 * o:128 * (o + 1)],
                                           curv[:, ki, :],
                                           start=(ki == 0), stop=(ki == 7))
                        return ins
                    OP("t", e, reads=[curres, "WA", "VSLK"], writes=[f"P{b}"])
                else:
                    for ki in range(8):
                        slk = (f"{curres}a{ki + 2}" if ki <= 5 else "VSLK")
                        OP("t", lambda t, o=o, ki=ki, b=b, curv=curv: t.matmul(
                            P[b].ap()[:, :],
                            WA.ap()[:, ki, 128 * o:128 * (o + 1)],
                            curv[:, ki, :], start=(ki == 0), stop=(ki == 7)),
                           reads=[slk, "WA"], writes=[f"P{b}"])
                OP("s", lambda s, o=o, b=b: s.activation(
                    BD.ap()[:, o, :], P[b].ap()[:, :], AF.Gelu,
                    bias=B1S.ap()[:, o:o + 1]),
                   reads=[f"P{b}", "B1S"], writes=["BD"])
            for o in range(4):
                b = 4 + (o % 2)
                def e(t, o=o, b=b):
                    for ki in range(8):
                        ins = t.matmul(P[b].ap()[:, :],
                                       WB.ap()[:, ki, 128 * o:128 * (o + 1)],
                                       BD.ap()[:, ki, :], start=(ki == 0), stop=(ki == 7))
                    return ins
                OP("t", e, reads=["BD", "WB2"], writes=[f"P{b}"])
                OP("s", lambda s, o=o, b=b: s.activation(
                    h2v[:, o, :], P[b].ap()[:, :], AF.Gelu,
                    bias=B2S.ap()[:, o:o + 1]),
                   reads=[f"P{b}", "B2S"], writes=["BE"])
            for o in range(2):
                b3k = 6 if o == 0 else 0
                def e(t, o=o, b3k=b3k):
                    for ki in range(4):
                        ins = t.matmul(P[b3k].ap()[:, :],
                                       W3S.ap()[:, ki, 128 * o:128 * (o + 1)],
                                       h2v[:, ki, :], start=(ki == 0), stop=(ki == 3))
                    return ins
                OP("t", e, reads=["BE", "W3S"], writes=[f"P{b3k}"])
                OP("s", lambda s, o=o, b3k=b3k: s.activation(
                    h3v[:, o, :], P[b3k].ap()[:, :], AF.Gelu,
                    bias=B3S.ap()[:, o:o + 1]),
                   reads=[f"P{b3k}", "B3S"], writes=["BE"])
            def e(t):
                for ki in range(2):
                    ins = t.matmul(P[7].ap()[:, :], W4R.ap()[:, ki, :],
                                   h3v[:, ki, :], start=(ki == 0), stop=(ki == 1))
                return ins
            OP("t", e, reads=["BE", "W4R"], writes=["P7"])
            OP("s", lambda s: s.activation(crrep, P[7].ap()[:, :], AF.Tanh,
                                           scale=0.5, bias=B4HC.ap()[:, 0:1]),
               reads=["P7", "B4C"], writes=["BE"])
            OP("v", lambda v: v.tensor_scalar(crrep, crrep, A2C.ap()[:, 0:1],
                                              CB2C.ap()[:, 0:1],
                                              op0=ALU.mult, op1=ALU.add),
               reads=["BE", "A2c", "CB2c"], writes=["BE"])
            vgap()
            vgap()
            def em(c, curv=curv, nxtv=nxtv):
                OP("v", lambda v, c=c, curv=curv, nxtv=nxtv: v.tensor_tensor(
                    nxtv[:, c, :], crrep, curv[:, SIGMA[c], :], op=ALU.mult),
                   reads=[curres, "BE"], writes=[f"{nxtres}m{c}"])
            def ea(c, curv=curv, nxtv=nxtv):
                OP("v", lambda v, c=c, curv=curv, nxtv=nxtv: v.tensor_tensor(
                    nxtv[:, c, :], curv[:, c, :], nxtv[:, c, :],
                    op=(ALU.add if SIGN[c] > 0 else ALU.subtract)),
                   reads=[curres, f"{nxtres}m{c}"],
                   writes=[f"{nxtres}a{c}", nxtres])
            em(0)
            em(1)
            for c in range(2, 8):
                ea(c - 2)
                em(c)
            ea(6)
            ea(7)
            vgap()
            vgap()
            vgap("VSLK")
            cur, curres = nxt, nxtres
        curv = cur.ap() if cur is BB else cur.ap().rearrange("p (a d) -> p a d", a=8)
        for j in range(NSUB):
            for g2 in range(2):
                b = g2
                def e(t, j=j, g2=g2, b=b, curv=curv):
                    for q in range(4):
                        c = 4 * g2 + q
                        ins = t.matmul(P[b].ap()[:, 128 * q:128 * (q + 1)],
                                       curv[:, c, 128 * j:128 * (j + 1)], IDENT[:, :],
                                       is_transpose=True, start=(q == 0), stop=(q == 3))
                    return ins
                OP("t", e, reads=[curres, "IDENT", "VSLK"], writes=[f"P{b}"])
                OP("v", lambda v, j=j, g2=g2, b=b, xv4=xv4: v.tensor_tensor(
                    bd2[:, 1024 * (j % 2) + 512 * g2:1024 * (j % 2) + 512 * (g2 + 1)],
                    xv4[:, j, 512 * g2:512 * (g2 + 1)],
                    P[b].ap()[:, :], op=ALU.add),
                   reads=[f"P{b}", xres], writes=["BD"])
            def e(g, j=j, st=st):
                return g.dma_start(
                    out=y_h[st * TST + 128 * j:st * TST + 128 * (j + 1), :],
                    in_=bd2[:, 1024 * (j % 2):1024 * (j % 2 + 1)]).then_inc(d_g, 16)
            OP("g", e, reads=["BD"], writes=["yst"], ndma=1)

    def e(g):
        g.dma_start(out=ydbg_h[0:1, 0:1024], in_=ARENA.ap()[0:1, :]).then_inc(d_g, 16)
        g.dma_start(out=ydbg_h[0:1, 1024:1536], in_=ROWS.ap()[0:1, :]).then_inc(d_g, 16)
        g.dma_start(out=ydbg_h[0:1, 1536:2048],
                    in_=BE.ap()[0:1, 3072:3584]).then_inc(d_g, 16)
        g.dma_start(out=ydbg_h[0:1, 2048:2056],
                    in_=CSFM.ap()[0:1, 0:8]).then_inc(d_g, 16)
        g.dma_start(out=ydbg_h[0:1, 2056:2057],
                    in_=TGC.ap()[0:1, 0:1]).then_inc(d_g, 16)
        g.dma_start(out=ydbg_h[0:1, 2057:2058],
                    in_=A2C.ap()[0:1, 0:1]).then_inc(d_g, 16)
        g.dma_start(out=ydbg_h[0:1, 2058:2059],
                    in_=CB2C.ap()[0:1, 0:1]).then_inc(d_g, 16)
        g.dma_start(out=ydbg_h[0:1, 2059:2063],
                    in_=GH1.ap()[0:1, 0:4]).then_inc(d_g, 16)
        return g.dma_start(out=ydbg_h[0:1, 2063:2065],
                           in_=GH2.ap()[0:1, 0:2]).then_inc(d_g, 16)
    OP("g", e, reads=["GM", "TGc", "CB2c", "A2c", "BE", "CSFMw", "GH1w", "GH2w"],
       writes=["ydbg"], ndma=9)
    if DBG:
        def e(g):
            g.dma_start(out=ydbg2_h[:, 0:4], in_=GH1.ap()[:, :],
                        allow_slow_non_contiguous=True).then_inc(d_g, 16)
            g.dma_start(out=ydbg2_h[:, 4:6], in_=GH2.ap()[:, :],
                        allow_slow_non_contiguous=True).then_inc(d_g, 16)
            g.dma_start(out=ydbg2_h[:, 6:7], in_=TGC.ap()[:, :],
                        allow_slow_non_contiguous=True).then_inc(d_g, 16)
            g.dma_start(out=ydbg2_h[:, 7:8], in_=CB2C.ap()[:, :],
                        allow_slow_non_contiguous=True).then_inc(d_g, 16)
            g.dma_start(out=ydbg2_h[:, 8:9], in_=GW3R.ap()[:, 0, 0:1],
                        allow_slow_non_contiguous=True).then_inc(d_g, 16)
            g.dma_start(out=ydbg2_h[:, 9:10], in_=GW3R.ap()[:, 1, 0:1],
                        allow_slow_non_contiguous=True).then_inc(d_g, 16)
            g.dma_start(out=ydbg2_h[:, 10:11], in_=GW3R.ap()[:, 0, 127:128],
                        allow_slow_non_contiguous=True).then_inc(d_g, 16)
            g.dma_start(out=ydbg2_h[:, 11:12], in_=W4R.ap()[:, 0, 0:1],
                        allow_slow_non_contiguous=True).then_inc(d_g, 16)
            g.dma_start(out=ydbg2_h[:, 12:13], in_=W4R.ap()[:, 1, 0:1],
                        allow_slow_non_contiguous=True).then_inc(d_g, 16)
            g.dma_start(out=ydbg2_h[:, 13:14], in_=GW3.ap()[:, 0, 0:1],
                        allow_slow_non_contiguous=True).then_inc(d_g, 16)
            g.dma_start(out=ydbg2_h[:, 14:15], in_=GW3.ap()[:, 1, 0:1],
                        allow_slow_non_contiguous=True).then_inc(d_g, 16)
            return g.dma_start(out=ydbg2_h[:, 15:16], in_=W4S.ap()[:, 0, 0:1],
                               allow_slow_non_contiguous=True).then_inc(d_g, 16)
        OP("g", e, reads=["GM", "TGc", "CB2c", "A2c", "GH1w", "GH2w",
                          "W4R", "GW3R", "W4S", "GW3"],
           writes=["ydbg2"], ndma=12)

    def fin(y_eng):
        y_eng.wait_ge(d_g, bld.cnt["DG"])
        y_eng.wait_ge(d_y, bld.cnt["DY"])
    return nc, bld, SEM, es, fin


def emit(nc, bld, SEM, es, fin, nops=None):
    own = dict(t="T", v="V", s="S", g="G")
    ops = bld.ops if nops is None else bld.ops[:nops]
    if nops is not None:
        pc = dict(DY=0, DG=0)
        for (_e, _w2, _f, key, amt, _r, _w3) in ops:
            if key in pc:
                pc[key] += amt
        def fin(y_eng, _pc=pc):
            y_eng.wait_ge(SEM["DG"], _pc["DG"])
            y_eng.wait_ge(SEM["DY"], _pc["DY"])
    with es:
        with nc.Block() as block:
            def run(eng_name):
                def body(eng):
                    hw = {}
                    for (e, waits, emit_fn, key, amt, _r, _w) in ops:
                        if e != eng_name:
                            continue
                        for k, v in sorted(waits.items()):
                            if k == own.get(eng_name):
                                continue
                            if hw.get(k, 0) >= v:
                                continue
                            eng.wait_ge(SEM[k], v)
                            hw[k] = v
                        last = emit_fn(eng)
                        if key not in ("DY", "DG"):
                            last.then_inc(SEM[key], amt)
                    if eng_name == "y":
                        fin(eng)
                return body
            block.sync(run("y"))
            block.tensor(run("t"))
            block.vector(run("v"))
            block.scalar(run("s"))
            block.gpsimd(run("g"))
    return nc


NOPS = None
LAST_RESULTS = None

def kernel(**inputs):
    ncores = 8
    nc, bld, SEM, es, fin = build_nc(STAGE)
    emit(nc, bld, SEM, es, fin, nops=NOPS)

    f32 = np.float32
    ca = lambda a: np.ascontiguousarray(np.asarray(a, dtype=f32))
    x = ca(inputs["L0_attn_x"])
    shared = {
        "wq": ca(inputs["L0_attn_wq"]), "wk": ca(inputs["L0_attn_wk"]),
        "wv": ca(inputs["L0_attn_wv"]), "wo": ca(inputs["L0_attn_wo"]),
        "bo": ca(inputs["L0_attn_bo"]).reshape(1, D),
        "proj": ca(inputs["L0_attn_proj"]),
        "w1": ca(inputs["L0_ffn_cn_w1"]), "b1": ca(inputs["L0_ffn_cn_b1"]),
        "w2": ca(inputs["L0_ffn_cn_w2"]), "b2": ca(inputs["L0_ffn_cn_b2"]),
        "w3": ca(inputs["L0_ffn_cn_w3"]), "b3": ca(inputs["L0_ffn_cn_b3"]),
        "w4": ca(inputs["L0_ffn_cn_w4"]),
        "b4": ca(inputs["L0_ffn_cn_b4"]).reshape(1, 1),
        "gw1": ca(inputs["L0_ffn_gc_w1"]), "gb1": ca(inputs["L0_ffn_gc_b1"]),
        "gw2": ca(inputs["L0_ffn_gc_w2"]), "gb2": ca(inputs["L0_ffn_gc_b2"]),
        "gw3": ca(inputs["L0_ffn_gc_w3"]),
        "gb3": ca(inputs["L0_ffn_gc_b3"]).reshape(1, 1),
        "ar": ca(inputs["L0_ffn_adaptive_range"]).reshape(1, 1),
    }
    in_maps = [dict(shared, x=np.ascontiguousarray(x[i])) for i in range(ncores)]
    res = run_bass_kernel_spmd(nc, in_maps, core_ids=list(range(ncores)))
    global LAST_RESULTS
    LAST_RESULTS = res.results
    out = np.stack([np.asarray(res.results[i]["y"]) for i in range(ncores)], axis=0)
    return out.astype(np.float32)


# version 24
# speedup vs baseline: 1.0874x; 1.0337x over previous
import contextlib
import numpy as np

import concourse.bass as bass
import concourse.mybir as mybir
from concourse.bass_utils import run_bass_kernel_spmd

F32 = mybir.dt.float32
AF = mybir.ActivationFunctionType
ALU = mybir.AluOpType
AX = mybir.AxisListType

STAGE = "full"
DBG = False
NCYC = 3

NTOK, D, H, DH, M = 4096, 1024, 16, 64, 256
NST, TST, NSUB = 8, 512, 4
NRM = float(64.0 ** -0.25)
EPSK = 1e-4
LNEPS = 1e-5
LNEPSK = float(np.log(EPSK))
SIGN = [1, 1, -1, -1, -1, -1, 1, 1]
SIGMA = [4, 5, 6, 7, 0, 1, 2, 3]


class Builder:

    def __init__(self):
        self.ops = []
        self.cnt = dict(T=0, V=0, S=0, G=0, DY=0, DG=0, CC=0)
        self.res = {}
        self.gap_emit = None
        self.s_spacer = None
        self.prev_v_writes = set()
        self._in_gap = False

    def op(self, eng, emit, reads=(), writes=(), ndma=0, cc=False, dup=False):
        if dup:
            self.op(eng, emit, reads=reads, writes=writes, ndma=ndma, cc=cc)
            if eng == "s" and self.s_spacer is not None:
                self._raw_op("s", self.s_spacer, (), ("SSPC",), 0, False)
                self._raw_op("s", self.s_spacer, (), ("SSPC",), 0, False)
            elif eng == "v" and self.gap_emit is not None:
                self._raw_op("v", self.gap_emit, (), ("VDres",), 0, False)
                self._raw_op("v", self.gap_emit, (), ("VDres",), 0, False)
        if eng == "v" and not self._in_gap:
            touched = set(reads) | set(writes)
            if touched & self.prev_v_writes and self.gap_emit is not None:
                self._in_gap = True
                self.op("v", self.gap_emit, reads=(), writes=("VDres",))
                self._in_gap = False
            self.prev_v_writes = set(writes)
        elif eng == "v":
            self.prev_v_writes = set(writes)
        self._raw_op(eng, emit, reads, writes, ndma, cc)

    def _raw_op(self, eng, emit, reads=(), writes=(), ndma=0, cc=False):
        if ndma:
            key, amt = ("DY" if eng == "y" else "DG"), 16 * ndma
        elif cc:
            key, amt = "CC", 1
        else:
            key, amt = dict(t="T", v="V", s="S", g="G", y="DY")[eng], 1
        waits = {}

        def addw(m):
            k, v = m
            if v > waits.get(k, 0):
                waits[k] = v

        for r in reads:
            e = self.res.get(r)
            if e and e["w"] is not None:
                addw(e["w"])
        for r in writes:
            e = self.res.get(r)
            if e:
                if e["w"] is not None:
                    addw(e["w"])
                for m in e["r"]:
                    addw(m)
        self.cnt[key] += amt
        marker = (key, self.cnt[key])
        for r in writes:
            self.res[r] = {"w": marker, "r": []}
        for r in reads:
            self.res.setdefault(r, {"w": None, "r": []})["r"].append(marker)
        self.ops.append((eng, waits, emit, key, amt, tuple(reads), tuple(writes)))


def build_nc(stage):
    nc = bass.Bass("TRN2", target_bir_lowering=False)

    x_h = nc.dram_tensor("x", [NTOK, D], F32, kind="ExternalInput").ap()
    wq_h = nc.dram_tensor("wq", [D, D], F32, kind="ExternalInput").ap()
    wk_h = nc.dram_tensor("wk", [D, D], F32, kind="ExternalInput").ap()
    wv_h = nc.dram_tensor("wv", [D, D], F32, kind="ExternalInput").ap()
    wo_h = nc.dram_tensor("wo", [D, D], F32, kind="ExternalInput").ap()
    bo_h = nc.dram_tensor("bo", [1, D], F32, kind="ExternalInput").ap()
    pj_h = nc.dram_tensor("proj", [M, DH], F32, kind="ExternalInput").ap()
    w1_h = nc.dram_tensor("w1", [D, D], F32, kind="ExternalInput").ap()
    b1_h = nc.dram_tensor("b1", [D], F32, kind="ExternalInput").ap()
    w2_h = nc.dram_tensor("w2", [D, 512], F32, kind="ExternalInput").ap()
    b2_h = nc.dram_tensor("b2", [512], F32, kind="ExternalInput").ap()
    w3_h = nc.dram_tensor("w3", [512, 256], F32, kind="ExternalInput").ap()
    b3_h = nc.dram_tensor("b3", [256], F32, kind="ExternalInput").ap()
    w4_h = nc.dram_tensor("w4", [256, 1], F32, kind="ExternalInput").ap()
    b4_h = nc.dram_tensor("b4", [1, 1], F32, kind="ExternalInput").ap()
    g1_h = nc.dram_tensor("gw1", [D, 512], F32, kind="ExternalInput").ap()
    gb1_h = nc.dram_tensor("gb1", [512], F32, kind="ExternalInput").ap()
    g2_h = nc.dram_tensor("gw2", [512, 256], F32, kind="ExternalInput").ap()
    gb2_h = nc.dram_tensor("gb2", [256], F32, kind="ExternalInput").ap()
    g3_h = nc.dram_tensor("gw3", [256, 1], F32, kind="ExternalInput").ap()
    gb3_h = nc.dram_tensor("gb3", [1, 1], F32, kind="ExternalInput").ap()
    ar_h = nc.dram_tensor("ar", [1, 1], F32, kind="ExternalInput").ap()
    y_h = nc.dram_tensor("y", [NTOK, D], F32, kind="ExternalOutput").ap()
    ydbg_h = nc.dram_tensor("ydbg", [1, 4096], F32, kind="ExternalOutput").ap()
    ydbg2_h = nc.dram_tensor("ydbg2", [128, 16], F32, kind="ExternalOutput").ap()
    x2s_h = nc.dram_tensor("x2scr", [NTOK, D], F32).ap()
    ccin_h = nc.dram_tensor("cc_in", [1, 8], F32).ap()
    ccout_h = nc.dram_tensor("cc_out", [1, 8], F32).ap()

    es = contextlib.ExitStack()
    sb = lambda n, s: es.enter_context(nc.sbuf_tensor(n, s, F32))
    ps = lambda n: es.enter_context(nc.psum_tensor(n, [128, 512], F32))

    BA = sb("BA", [128, 4, 1024])
    BB = sb("BB", [128, 8, 512])
    BC = sb("BC", [128, 4096])
    BD = sb("BD", [128, 8, 512])
    BE = sb("BE", [128, 4096])
    BF = sb("BF", [128, 4096])
    BH = sb("BH", [128, 1536])
    WA = sb("WA", [128, 8, 1024])
    WB = sb("WB", [128, 8, 1024])
    W3S = sb("W3S", [128, 4, 256])
    W4S = sb("W4S", [128, 2, 1])
    GW3 = sb("GW3", [128, 2, 1])
    B1S = sb("B1S", [128, 8])
    B2S = sb("B2S", [128, 4])
    B3S = sb("B3S", [128, 2])
    GB1 = sb("GB1", [128, 4])
    GB2 = sb("GB2", [128, 2])
    IDENT = sb("IDENT", [128, 128])
    PRJCAT = sb("PRJCAT", [128, 512])
    BLK = sb("BLK", [128, 8, 16])
    CTX = sb("CTX", [128, 32 * 66])
    ARENA = sb("ARENA", [1, 1024])
    ROWS = sb("ROWS", [1, 512])
    LN1M = sb("LN1M", [128, 32])
    LN1I = sb("LN1I", [128, 32])
    LN2M = sb("LN2M", [128, 32])
    LN2I = sb("LN2I", [128, 32])
    UMAT = sb("UMAT", [128, 256])
    NSTAB = sb("NSTABm", [128, 16])
    BIASQ = sb("BIASQ", [128, 16])
    NDS = sb("NDS", [16, 512])
    NDT = sb("NDT", [128, 4, 16])
    SS4 = sb("SS4", [128, 4])
    TMP4 = sb("TMP4", [128, 4])
    TMP4B = sb("TMP4B", [128, 4])
    TMP32 = sb("TMP32", [128, 32])
    DEN = sb("DEN", [128, 16])
    DINV = sb("DINV", [128, 16])
    RBD = sb("RBD", [128, 16])
    BCAST = sb("BCAST", [128, 2])
    SVREP = sb("SVREP", [128, 1024])
    BOREP = sb("BOREP", [128, 1024])
    ONESC = sb("ONESC", [128, 8])
    ONES1 = sb("ONES1", [1, 128])
    CSFM = sb("CSFM", [128, 8])
    GH1 = sb("GH1", [128, 4])
    GH2 = sb("GH2", [128, 2])
    PLOAD = sb("PLOAD", [128, 2, 64])
    CONSTS = sb("CONSTS", [128, 4])
    VD = sb("VD", [128, 4])
    SD = sb("SD", [128, 8])
    W4R = sb("W4R", [128, 2, 128])
    GW3R = sb("GW3R", [128, 2, 128])
    ARCOL = sb("ARCOL", [128, 1])
    B4HC = sb("B4HC", [128, 1])
    B3HC = sb("B3HC", [128, 1])
    A2C = sb("A2C", [128, 1])
    CB2C = sb("CB2C", [128, 1])
    T1C = sb("T1C", [128, 1])
    TGC = sb("TGC", [128, 1])
    NEG32 = sb("NEG32", [32, 512])
    P = [ps(f"P{i}") for i in range(8)]
    nc.const_aps.register(F32, LNEPS, CONSTS.ap()[:, 0:1])
    nc.const_aps.register(F32, EPSK, CONSTS.ap()[:, 1:2])
    nc.const_aps.register(F32, LNEPSK, CONSTS.ap()[:, 2:3])
    nc.const_aps.register(F32, 0.1, CONSTS.ap()[:, 3:4])

    d_y = es.enter_context(nc.semaphore("d_y"))
    d_g = es.enter_context(nc.semaphore("d_g"))
    s_t = es.enter_context(nc.semaphore("s_t"))
    s_v = es.enter_context(nc.semaphore("s_v"))
    s_s = es.enter_context(nc.semaphore("s_s"))
    s_g = es.enter_context(nc.semaphore("s_g"))
    s_cc = es.enter_context(nc.semaphore("s_cc"))
    SEM = dict(DY=d_y, DG=d_g, T=s_t, V=s_v, S=s_s, G=s_g, CC=s_cc)

    A_CS, A_GM = 0, 0
    A_TROW, A_CROW = 0, 512
    R_B4, R_GB3, R_B4H, R_B3H = 0, 1, 2, 3
    R_SLOC, R_S2, R_GZ = 4, 8, 9
    R_S8, R_SG, R_BETA, R_BN, R_B2 = 16, 24, 32, 40, 48
    R_AROW, R_A2, R_CB2, R_TG, R_T1 = 56, 64, 72, 80, 88
    R_UR = 128

    bld = Builder()
    OP = bld.op

    bld.gap_emit = lambda v: v.tensor_copy(VD.ap()[:, 0:2], VD.ap()[:, 2:4])
    bld.s_spacer = lambda s: s.copy(SD.ap()[:, 0:2], SD.ap()[:, 2:4])

    def vgap(res="VDres"):
        OP("v", bld.gap_emit, reads=[], writes=[res])

    def sflush():
        OP("s", lambda s: s.activation(SD.ap()[:, 0:4], CONSTS.ap()[:, 0:4],
                                       AF.Copy, accum_out=SD.ap()[:, 4:5]),
           reads=["CONSTS"], writes=["SDres"])

    def ld(name, dst_ap, src_ap, reads=(), extra_writes=(), slow=False):
        def e(eng, _aps=(dst_ap, src_ap), _slow=slow):
            return eng.dma_start(out=_aps[0], in_=_aps[1],
                                 allow_slow_non_contiguous=_slow).then_inc(d_y, 16)
        OP("y", e, reads=list(reads), writes=[name] + list(extra_writes), ndma=1)

    ld("BA", BA.ap()[:, :, :],
       x_h[0:TST, :].rearrange("(j p) d -> p j d", p=128))
    ld("WA", WA.ap()[:, :, :], wk_h.rearrange("(a p) d -> p a d", p=128))
    ld("WB", WB.ap()[:, :, :], wv_h.rearrange("(a p) d -> p a d", p=128))
    ld("PLOAD", PLOAD.ap()[:, :, :], pj_h.rearrange("(a p) d -> p a d", p=128))
    ld("B1S", B1S.ap()[:, :], b1_h.rearrange("(a p) -> p a", p=128), slow=True)
    ld("B2S", B2S.ap()[:, :], b2_h.rearrange("(a p) -> p a", p=128), slow=True)
    ld("B3S", B3S.ap()[:, :], b3_h.rearrange("(a p) -> p a", p=128), slow=True)
    ld("GB1", GB1.ap()[:, :], gb1_h.rearrange("(a p) -> p a", p=128), slow=True)
    ld("GB2", GB2.ap()[:, :], gb2_h.rearrange("(a p) -> p a", p=128), slow=True)
    ld("W4S", W4S.ap()[:, :, :], w4_h.rearrange("(a p) o -> p a o", p=128), slow=True)
    ld("GW3", GW3.ap()[:, :, :], g3_h.rearrange("(a p) o -> p a o", p=128), slow=True)
    ld("GW2", W3S.ap()[:, :, :], g2_h.rearrange("(a p) o -> p a o", p=128))
    ld("BOREP", BOREP.ap()[:, :], bo_h.broadcast_to([128, D]))
    ld("ROWS", ROWS.ap()[0:1, R_B4:R_B4 + 1], b4_h[0:1, 0:1])
    ld("ROWS", ROWS.ap()[0:1, R_GB3:R_GB3 + 1], gb3_h[0:1, 0:1])
    ld("B4C", B4HC.ap()[:, :], b4_h.broadcast_to([128, 1]))
    ld("B3C", B3HC.ap()[:, :], gb3_h.broadcast_to([128, 1]))
    ld("rar", ARCOL.ap()[:, :], ar_h.broadcast_to([128, 1]))

    def init_consts(g):
        g.memset(IDENT[:, :], 0.0)
        g.affine_select(out=IDENT[:, :], in_=IDENT[:, :],
                        compare_op=ALU.not_equal, fill=1.0, base=0,
                        pattern=[[-1, 128]], channel_multiplier=1)
        g.memset(ONESC[:, :], 1.0)
        g.memset(ONES1[:, :], 1.0)
        g.memset(BLK.ap()[:, :, :], 0.0)
        for c in range(8):
            g.memset(BLK.ap()[0:64, c, 2 * c:2 * c + 1], 0.5 * NRM * NRM)
            g.memset(BLK.ap()[64:128, c, 2 * c + 1:2 * c + 2], 0.5 * NRM * NRM)
        g.memset(CTX[:, :], 0.0)
        ctx3i = CTX.ap().rearrange("p (b w) -> p b w", w=66)
        g.memset(ctx3i[:, :, 65:66], 1.0)
        g.memset(ARENA[:, :], 0.0)
        g.memset(CONSTS.ap()[:, 0:1], LNEPS)
        g.memset(CONSTS.ap()[:, 1:2], EPSK)
        g.memset(CONSTS.ap()[:, 2:3], LNEPSK)
        g.memset(CONSTS.ap()[:, 3:4], 0.1)
        g.memset(NEG32[:, :], -1.0)
        g.memset(CSFM[:, :], 0.0)
        return g.memset(PRJCAT[:, :], 0.0)
    OP("g", init_consts, reads=["ROWS", "rar"],
       writes=["IDENT", "ONES", "BLK", "CTX", "CSUM1", "PRJ", "CONSTS", "NEG32"])

    def e(s):
        s.activation(B4HC.ap()[:, :], B4HC.ap()[:, :], AF.Copy, scale=0.5)
        return s.activation(B3HC.ap()[:, :], B3HC.ap()[:, :], AF.Copy, scale=0.5)
    OP("s", e, reads=["B4C", "B3C"], writes=["halves", "B4C", "B3C"])

    def e(t):
        for q in range(2):
            ins = t.matmul(P[0].ap()[0:64, 128 * q:128 * (q + 1)],
                           PLOAD.ap()[:, q, :], IDENT[:, :],
                           is_transpose=True, start=(q == 0), stop=(q == 1))
        return ins
    OP("t", e, reads=["PLOAD", "IDENT"], writes=["P0"], dup=True)
    OP("s", lambda s: s.activation(PRJCAT[0:64, 0:256], P[0].ap()[0:64, 0:256],
                                   AF.Copy, scale=NRM),
       reads=["P0", "PRJ"], writes=["PRJLO_w"], dup=True)
    def e(g):
        return g.dma_start(out=PRJCAT[64:128, 256:512],
                           in_=PRJCAT[0:64, 0:256]).then_inc(d_g, 16)
    OP("g", e, reads=["PRJLO_w", "PRJ"], writes=["PRJHI_w"], ndma=1)
    PRJ_R = ["PRJLO_w", "PRJHI_w"]
    def e(s):
        return s.add_instruction(mybir.InstLoadActFuncSet(
            name=nc.get_next_instruction_name(), act_func_set_id=6, ins=[], outs=[]))
    OP("s", e)

    ST4 = lambda st: st * 4
    bc4 = BC.ap().rearrange("p (j d) -> p j d", j=4)
    ctx3 = CTX.ap().rearrange("p (b w) -> p b w", w=66)
    e32 = BE.ap().rearrange("p (c t) -> p c t", c=32)

    def emit_loadA(st):
        ld("BA", BA.ap()[:, :, :],
           x_h[st * TST:(st + 1) * TST, :].rearrange("(j p) d -> p j d", p=128))

    def emit_lnA(st):
        for j in range(NSUB):
            OP("v", lambda v, j=j, st=st: v.reduce_sum(
                LN1M.ap()[:, ST4(st) + j:ST4(st) + j + 1], BA.ap()[:, j, :], axis=AX.X),
               reads=["BA"], writes=["LN1M"])
        OP("s", lambda s, st=st: s.activation(
            LN1M.ap()[:, ST4(st):ST4(st) + 4], LN1M.ap()[:, ST4(st):ST4(st) + 4],
            AF.Copy, scale=1.0 / D), writes=["LN1M"])
        for j in range(NSUB):
            OP("v", lambda v, j=j, st=st: v.tensor_scalar(
                BA.ap()[:, j, :], BA.ap()[:, j, :],
                LN1M.ap()[:, ST4(st) + j:ST4(st) + j + 1], None, op0=ALU.subtract),
               reads=["LN1M"], writes=["BA"])
            OP("s", lambda s, j=j: s.activation(
                BH.ap()[:, 0:1024], BA.ap()[:, j, :], AF.Square),
               reads=["BA"], writes=["ksq0", "ksq1"])
            OP("v", lambda v, j=j: v.reduce_sum(
                SS4.ap()[:, j:j + 1], BH.ap()[:, 0:1024], axis=AX.X),
               reads=["ksq0"], writes=["SS4"])
        OP("s", lambda s: s.activation(TMP4[:, :], SS4[:, :], AF.Ln,
                                       bias=CONSTS.ap()[:, 0:1], scale=1.0 / D),
           reads=["SS4", "CONSTS"], writes=["TMP4"])
        OP("s", lambda s, st=st: s.activation(
            LN1I.ap()[:, ST4(st):ST4(st) + 4], TMP4[:, :], AF.Exp, scale=-0.5),
           reads=["TMP4"], writes=["LN1I"])
        for j in range(NSUB):
            OP("v", lambda v, j=j, st=st: v.tensor_scalar(
                BA.ap()[:, j, :], BA.ap()[:, j, :],
                LN1I.ap()[:, ST4(st) + j:ST4(st) + j + 1], None, op0=ALU.mult),
               reads=["LN1I"], writes=["BA"])

    emit_loadA(0)
    emit_lnA(0)
    for st in range(NST):
        for c in range(8):
            b = 6 + (c % 2)
            def e(t, c=c, b=b):
                for j in range(NSUB):
                    ins = t.matmul(P[b].ap()[:, 128 * j:128 * (j + 1)],
                                   BA.ap()[:, j, 128 * c:128 * (c + 1)], IDENT[:, :],
                                   is_transpose=True, start=(j == 0), stop=(j == NSUB - 1))
                return ins
            OP("t", e, reads=["BA", "IDENT"], writes=[f"P{b}"])
            OP("v", lambda v, c=c, b=b: v.reduce_sum(
                TMP32.ap()[:, c:c + 1], P[b].ap()[:, :], axis=AX.X),
               reads=[], writes=[f"P{b}", "SXT"])
            OP("s", lambda s, c=c, b=b: s.copy(BB.ap()[:, c, :], P[b].ap()[:, :]),
               reads=[f"P{b}"], writes=["BB"])
        vgap()
        OP("v", lambda v: v.tensor_tensor(CSFM[:, :], CSFM[:, :],
                                          TMP32.ap()[:, 0:8], op=ALU.add),
           reads=["SXT"], writes=["CSFMacc"])
        if st + 1 < NST:
            emit_loadA(st + 1)
            emit_lnA(st + 1)
        for j in range(NSUB):
            for h in range(2):
                b = 6 + h
                def e(t, j=j, h=h, b=b):
                    for ki in range(8):
                        ins = t.matmul(P[b].ap()[:, :],
                                       BB.ap()[:, ki, 128 * j:128 * (j + 1)],
                                       WB.ap()[:, ki, 512 * h:512 * (h + 1)],
                                       start=(ki == 0), stop=(ki == 7))
                    return ins
                OP("t", e, reads=["BB", "WB"], writes=[f"P{b}"])
                OP("s", lambda s, j=j, h=h, b=b: s.copy(
                    bc4[:, j, 512 * h:512 * (h + 1)], P[b].ap()[:, :]),
                   reads=[f"P{b}"], writes=["BC"])
        for o in range(8):
            b = 6 + (o % 2)
            def e(t, o=o, b=b):
                for ki in range(8):
                    ins = t.matmul(P[b].ap()[:, :],
                                   WA.ap()[:, ki, 128 * o:128 * (o + 1)],
                                   BB.ap()[:, ki, :],
                                   start=(ki == 0), stop=(ki == 7))
                return ins
            OP("t", e, reads=["BB", "WA"], writes=[f"P{b}"])
            OP("s", lambda s, o=o, b=b: s.copy(BD.ap()[:, o, :], P[b].ap()[:, :]),
               reads=[f"P{b}"], writes=["BD"])
        for c in range(8):
            kq = f"ksq{c % 2}"
            OP("v", lambda v, c=c: v.tensor_tensor(
                BH.ap()[:, 512 * (c % 2):512 * (c % 2 + 1)],
                BD.ap()[:, c, :], BD.ap()[:, c, :], op=ALU.mult),
               reads=["BD"], writes=[kq])
            OP("t", lambda t, c=c: t.matmul(
                P[5].ap()[0:16, :], BLK.ap()[:, c, :],
                BH.ap()[:, 512 * (c % 2):512 * (c % 2 + 1)],
                start=(c == 0), stop=(c == 7)),
               reads=[kq, "BLK"], writes=["P5"])
        OP("s", lambda s: s.activation(NDS[0:16, :], P[5].ap()[0:16, :],
                                       AF.Copy, scale=-1.0),
           reads=["P5"], writes=["NDS"], dup=True)
        def e(t):
            for j in range(NSUB):
                ins = t.matmul(P[5].ap()[:, 16 * j:16 * (j + 1)],
                               NDS.ap()[0:16, 128 * j:128 * (j + 1)],
                               IDENT[0:16, 0:16],
                               is_transpose=True, start=(j == 0), stop=(j == NSUB - 1))
            return ins
        OP("t", e, reads=["NDS", "IDENT"], writes=["P5"], dup=True)
        OP("s", lambda s: s.copy(NDT.ap()[:, :, :].rearrange("p a b -> p (a b)"),
                                 P[5].ap()[:, 0:64]),
           reads=["P5"], writes=["NDT"], dup=True)
        for j in range(NSUB):
            kbuf = BE if j % 2 == 0 else BF
            kres = f"kp{j % 2}"
            for p in range(8):
                b = 6 + (p % 2)
                def e(t, j=j, p=p, b=b):
                    return t.matmul(P[b].ap()[:, :],
                                    BD.ap()[:, p, 128 * j:128 * (j + 1)],
                                    PRJCAT.ap()[:, :], start=True, stop=True)
                OP("t", e, reads=["BD"] + PRJ_R, writes=[f"P{b}"])
                col = (ST4(st) + j) * 8 + p
                OP("v", lambda v, b=b, col=col: v.reduce_max(
                    UMAT.ap()[:, col:col + 1], P[b].ap()[:, :], axis=AX.X),
                   reads=[f"P{b}"], writes=["UMAT"])
                def e(s, j=j, p=p, b=b, kbuf=kbuf):
                    for i in range(2):
                        ins = s.activation(
                            kbuf.ap()[:, 512 * p + 256 * i:512 * p + 256 * (i + 1)],
                            P[b].ap()[:, 256 * i:256 * (i + 1)], AF.Exp,
                            bias=NDT.ap()[:, j, 2 * p + i:2 * p + i + 1])
                    return ins
                OP("s", e, reads=["NDT"], writes=[f"{kres}_{p}", f"P{b}"])
            for b in range(4):
                def e(t, j=j, b=b, kbuf=kbuf):
                    for cc in range(8):
                        c = 8 * b + cc
                        hd = c // 2
                        t.matmul(P[b].ap()[:, 64 * cc:64 * (cc + 1)],
                                 kbuf.ap()[:, 128 * c:128 * (c + 1)],
                                 bc4[:, j, 64 * hd:64 * (hd + 1)],
                                 start=(j == 0 and cc == 0),
                                 stop=(j == NSUB - 1 and cc == 7))
                        ins = t.matmul(P[4].ap()[:, c:c + 1],
                                       kbuf.ap()[:, 128 * c:128 * (c + 1)],
                                       ONESC.ap()[:, 0:1],
                                       start=(j == 0 and c == 0),
                                       stop=(j == NSUB - 1 and c == 31))
                    return ins
                OP("t", e,
                   reads=[f"{kres}_{2 * b}", f"{kres}_{2 * b + 1}", "BC", "ONES"],
                   writes=[f"P{b}", "P4"])
        for b in range(4):
            OP("v", lambda v, b=b: v.tensor_tensor(
                ctx3[:, 8 * b:8 * (b + 1), 0:64],
                ctx3[:, 8 * b:8 * (b + 1), 0:64],
                P[b].ap().rearrange("p (a d) -> p a d", a=8), op=ALU.add),
               reads=[f"P{b}"], writes=["CTX"])
        OP("v", lambda v: v.tensor_tensor(
            CTX.ap().rearrange("p (b w) -> p b w", w=66)[:, :, 64],
            CTX.ap().rearrange("p (b w) -> p b w", w=66)[:, :, 64],
            P[4].ap()[:, 0:32], op=ALU.add),
           reads=["P4"], writes=["CTX"])

    if stage == "A":
        def fin(y_eng):
            y_eng.wait_ge(d_g, bld.cnt["DG"])
            y_eng.wait_ge(d_y, bld.cnt["DY"])
        return nc, bld, SEM, es, fin

    OP("v", lambda v: v.reduce_max(TMP4.ap()[:, 0:1], UMAT[:, :], axis=AX.X),
       reads=["UMAT"], writes=["TMP4"])
    vgap()
    vgap()
    vgap("BSLK")
    OP("t", lambda t: t.matmul(P[0].ap()[0:1, 0:128], TMP4.ap()[:, 0:1],
                               IDENT[:, :], is_transpose=True, start=True, stop=True),
       reads=["TMP4", "IDENT", "BSLK"], writes=["P0"], dup=True)
    OP("s", lambda s: s.copy(ROWS.ap()[0:1, R_UR:R_UR + 128], P[0].ap()[0:1, 0:128]),
       reads=["P0"], writes=["UROW"], dup=True)
    OP("v", lambda v: v.reduce_max(ROWS.ap()[0:1, R_SLOC:R_SLOC + 1],
                                   ROWS.ap()[0:1, R_UR:R_UR + 128], axis=AX.X),
       reads=["UROW"], writes=["SLOC"], dup=True)
    vgap()
    vgap()
    vgap("BSLK2")
    OP("t", lambda t: t.matmul(P[1].ap()[0:1, 0:8], ROWS.ap()[0:1, R_SLOC:R_SLOC + 1],
                               ONESC.ap()[0:1, 0:8], start=True, stop=True),
       reads=["SLOC", "ONES", "BSLK2"], writes=["P1"], dup=True)
    OP("s", lambda s: s.copy(ROWS.ap()[0:1, R_S8:R_S8 + 8], P[1].ap()[0:1, 0:8]),
       reads=["P1"], writes=["S8"], dup=True)
    def e(g):
        return g.dma_start(out=ccin_h[0:1, 0:8],
                           in_=ROWS.ap()[0:1, R_S8:R_S8 + 8]).then_inc(d_g, 16)
    OP("g", e, reads=["S8"], writes=["ccin"], ndma=1)
    def e(g):
        return g.collective_compute(
            "AllReduce", ALU.max, replica_groups=[list(range(8))],
            ins=[ccin_h[0:1, 0:8]], outs=[ccout_h[0:1, 0:8]])
    OP("g", e, reads=["ccin"], writes=["ccout"], cc=True)
    def e(g):
        return g.dma_start(out=ROWS.ap()[0:1, R_SG:R_SG + 8],
                           in_=ccout_h[0:1, 0:8]).then_inc(d_g, 16)
    OP("g", e, reads=["ccout"], writes=["SG"], ndma=1)
    OP("s", lambda s: s.activation(ROWS.ap()[0:1, R_BETA:R_BETA + 8],
                                   ROWS.ap()[0:1, R_SG:R_SG + 8], AF.Exp,
                                   bias=CONSTS.ap()[0:1, 2:3]),
       reads=["SG", "CONSTS"], writes=["BETA"], dup=True)
    OP("s", lambda s: s.activation(ROWS.ap()[0:1, R_BN:R_BN + 8],
                                    ROWS.ap()[0:1, R_BETA:R_BETA + 8],
                                    AF.Copy, scale=float(NTOK)),
       reads=["BETA"], writes=["BN"], dup=True)
    OP("s", lambda s: s.copy(ROWS.ap()[0:1, R_B2:R_B2 + 1],
                             ROWS.ap()[0:1, R_BETA:R_BETA + 1]),
       reads=["BETA"], writes=["B2"], dup=True)
    OP("s", lambda s: s.copy(ROWS.ap()[0:1, R_B2 + 1:R_B2 + 2],
                             ROWS.ap()[0:1, R_BN:R_BN + 1]),
       reads=["BN"], writes=["B2"], dup=True)
    def emit_bcast():
        OP("t", lambda t: t.matmul(P[2].ap()[:, 0:2], ONES1.ap()[0:1, :],
                                   ROWS.ap()[0:1, R_B2:R_B2 + 2],
                                   start=True, stop=True),
           reads=["B2", "ONES"], writes=["P2"], dup=True)
        OP("s", lambda s: s.copy(BCAST[:, :], P[2].ap()[:, 0:2]),
           reads=["P2"], writes=["BCASTw"], dup=True)
    if stage in ("AB", "dumpA"):
        emit_bcast()
    vgap()
    vgap("ASLK")
    for h in range(2):
        def e(t, h=h):
            for ki in range(8):
                ins = t.matmul(P[4 + h].ap()[0:1, :], CSFM.ap()[:, ki:ki + 1],
                               WB.ap()[:, ki, 512 * h:512 * (h + 1)],
                               start=(ki == 0), stop=(ki == 7))
            return ins
        OP("t", e, reads=["CSFMacc", "WB", "ASLK"], writes=[f"P{4 + h}"], dup=True)
        OP("s", lambda s, h=h: s.copy(
            ARENA.ap()[0:1, A_CS + 512 * h:A_CS + 512 * (h + 1)],
            P[4 + h].ap()[0:1, :]),
           reads=[f"P{4 + h}"], writes=["SVROW", "CSUM1"], dup=True)
    for h in range(2):
        OP("t", lambda t, h=h: t.matmul(
            P[6 + h].ap()[:, :], ONES1.ap()[0:1, :],
            ARENA.ap()[0:1, A_CS + 512 * h:A_CS + 512 * (h + 1)],
            start=True, stop=True),
           reads=["SVROW", "ONES"], writes=[f"P{6 + h}"], dup=True)
        OP("s", lambda s, h=h: s.copy(SVREP.ap()[:, 512 * h:512 * (h + 1)],
                                      P[6 + h].ap()[:, :]),
           reads=[f"P{6 + h}"], writes=["SVREPw"], dup=True)

    if stage in ("AB", "dumpA"):
        if stage == "dumpA":
            dumps = [
                (y_h[0:128, :], CTX.ap()[:, 0:1024], ["CTX"]),
                (y_h[128:256, :], CTX.ap()[:, 1024:2048], ["CTX"]),
                (y_h[256:384, 0:64], CTX.ap()[:, 2048:2112], ["CTX"]),
                (y_h[384:512, 0:32], LN1M.ap()[:, :], ["LN1M"]),
                (y_h[384:512, 32:64], LN1I.ap()[:, :], ["LN1I"]),
                (y_h[512:640, :], SVREP.ap()[:, :], ["SVREPw"]),
                (y_h[640:768, 0:2], BCAST.ap()[:, :], ["BCASTw"]),
                (y_h[640:768, 4:8], SS4.ap()[:, :], ["SS4"]),
                (y_h[640:768, 8:12], TMP4.ap()[:, :], ["TMP4"]),
                (y_h[768:769, 0:512], ROWS.ap()[0:1, :], ["BETA", "BN", "SG", "S8"]),
                (y_h[769:770, 0:1024], ARENA.ap()[0:1, :], ["SVROW"]),
                (y_h[896:1024, 0:256], UMAT.ap()[:, :], ["UMAT"]),
            ]
            for dst, srcap, rds in dumps:
                def e(g, _a=(dst, srcap)):
                    return g.dma_start(out=_a[0], in_=_a[1]).then_inc(d_g, 16)
                OP("g", e, reads=rds, writes=["ydump"], ndma=1)
        def fin(y_eng):
            y_eng.wait_ge(d_g, bld.cnt["DG"])
            y_eng.wait_ge(d_y, bld.cnt["DY"])
        return nc, bld, SEM, es, fin

    x2v = BF.ap().rearrange("p (a d) -> p a d", a=4)
    atm = BF.ap()[:, 2048:3072]
    afm = BF.ap()[:, 3072:4096].rearrange("p (a c) -> p a c", a=8)

    def emit_chead_pre(st):
        for j in range(NSUB):
            ld(f"BAc{j}", BA.ap()[:, j, :],
               x_h[st * TST + 128 * j:st * TST + 128 * (j + 1), :],
               reads=[f"BAg{j}"],
               extra_writes=(["BA"] if st == 0 else []))
        for j in range(NSUB):
            OP("v", lambda v, j=j, st=st: v.tensor_scalar(
                bc4[:, j, :], BA.ap()[:, j, :],
                LN1M.ap()[:, ST4(st) + j:ST4(st) + j + 1],
                LN1I.ap()[:, ST4(st) + j:ST4(st) + j + 1],
                op0=ALU.subtract, op1=ALU.mult),
               reads=[f"BAc{j}", "LN1M", "LN1I"], writes=["BC"])
        vgap()
        vgap("CHSLK")
        for c in range(8):
            b = 6 + (c % 2)
            def e(t, c=c, b=b):
                for j in range(NSUB):
                    ins = t.matmul(P[b].ap()[:, 128 * j:128 * (j + 1)],
                                   bc4[:, j, 128 * c:128 * (c + 1)], IDENT[:, :],
                                   is_transpose=True,
                                   start=(j == 0), stop=(j == NSUB - 1))
                return ins
            OP("t", e, reads=["BC", "IDENT", "CHSLK"], writes=[f"P{b}"])
            OP("s", lambda s, c=c, b=b: s.copy(BB.ap()[:, c, :], P[b].ap()[:, :]),
               reads=[f"P{b}"], writes=["BB"])

    def emit_chead_post(st):
        for o in range(8):
            b = 6 + (o % 2)
            def e(t, o=o, b=b):
                for ki in range(8):
                    ins = t.matmul(P[b].ap()[:, :],
                                   WA.ap()[:, ki, 128 * o:128 * (o + 1)],
                                   BB.ap()[:, ki, :],
                                   start=(ki == 0), stop=(ki == 7))
                return ins
            OP("t", e, reads=["BB", "WA"], writes=[f"P{b}"])
            OP("s", lambda s, o=o, b=b: s.copy(BD.ap()[:, o, :], P[b].ap()[:, :]),
               reads=[f"P{b}"], writes=["BD"])
        for c in range(8):
            kq = f"qsq{c % 2}"
            OP("v", lambda v, c=c: v.tensor_tensor(
                BH.ap()[:, 512 * (c % 2):512 * (c % 2 + 1)],
                BD.ap()[:, c, :], BD.ap()[:, c, :], op=ALU.mult),
               reads=["BD"], writes=[kq, f"ksq{c % 2}"])
            OP("t", lambda t, c=c: t.matmul(
                P[0].ap()[0:16, :], BLK.ap()[:, c, :],
                BH.ap()[:, 512 * (c % 2):512 * (c % 2 + 1)],
                start=(c == 0), stop=(c == 7)),
               reads=[kq, "BLK"], writes=["P0"])
        OP("s", lambda s: s.activation(NDS[0:16, :], P[0].ap()[0:16, :],
                                       AF.Copy, scale=-1.0),
           reads=["P0"], writes=["NDS"], dup=True)
        def e(t):
            for j in range(NSUB):
                ins = t.matmul(P[0].ap()[:, 16 * j:16 * (j + 1)],
                               NDS.ap()[0:16, 128 * j:128 * (j + 1)],
                               IDENT[0:16, 0:16],
                               is_transpose=True,
                               start=(j == 0), stop=(j == NSUB - 1))
            return ins
        OP("t", e, reads=["NDS", "IDENT"], writes=["P0"], dup=True)
        OP("s", lambda s: s.copy(NDT.ap()[:, :, :].rearrange("p a b -> p (a b)"),
                                 P[0].ap()[:, 0:64]),
           reads=["P0"], writes=["NDT"], dup=True)

    def emit_qde(st, j):
        for p in range(8):
            b = 6 + (p % 2)
            def e(t, j=j, p=p, b=b):
                return t.matmul(P[b].ap()[:, :],
                                BD.ap()[:, p, 128 * j:128 * (j + 1)],
                                PRJCAT.ap()[:, :], start=True, stop=True)
            OP("t", e, reads=["BD"] + PRJ_R, writes=[f"P{b}"])
            OP("v", lambda v, p=p, b=b: v.reduce_max(
                NSTAB.ap()[:, 2 * p:2 * p + 2],
                P[b].ap().rearrange("p (a m) -> p a m", a=2),
                axis=AX.X, negate=True),
               reads=[f"P{b}"], writes=["NSTABr"])
            vgap()
            OP("v", lambda v, j=j, p=p: v.tensor_tensor(
                BIASQ.ap()[:, 2 * p:2 * p + 2], NSTAB.ap()[:, 2 * p:2 * p + 2],
                NDT.ap()[:, j, 2 * p:2 * p + 2], op=ALU.add),
               reads=["NSTABr", "NDT"], writes=["BIASQw"])
            def e(s, j=j, p=p, b=b):
                for i in range(2):
                    ins = s.activation(
                        BC.ap()[:, 512 * p + 256 * i:512 * p + 256 * (i + 1)],
                        P[b].ap()[:, 256 * i:256 * (i + 1)], AF.Exp,
                        bias=BIASQ.ap()[:, 2 * p + i:2 * p + i + 1])
                return ins
            OP("s", e, reads=["BIASQw"], writes=["BC", f"P{b}"])
        for g8 in range(8):
            b = g8 % 2
            def e(t, g8=g8, b=b):
                for q in range(4):
                    mc = 4 * g8 + q
                    ins = t.matmul(P[b].ap()[:, 128 * q:128 * (q + 1)],
                                   BC.ap()[:, 128 * mc:128 * (mc + 1)], IDENT[:, :],
                                   is_transpose=True, start=(q == 0), stop=(q == 3))
                return ins
            OP("t", e, reads=["BC", "IDENT"], writes=[f"P{b}"])
            ew = ["BE"]
            if st == 0 and j == 0 and g8 == 0:
                ew += [f"kp0_{p}" for p in range(8)]
            OP("s", lambda s, g8=g8, b=b: s.activation(
                BE.ap()[:, 512 * g8:512 * (g8 + 1)], P[b].ap()[:, :],
                AF.Copy, bias=EPSK),
               reads=[f"P{b}", "CONSTS"], writes=ew)

    def emit_odc(st, j):
        def e(t):
            for hh in range(16):
                for q in range(2):
                    kc = 2 * hh + q
                    ins = t.matmul(P[1].ap()[:, 2 * hh:2 * hh + 2],
                                   e32[:, kc, :], ctx3[:, kc, 64:66],
                                   start=(hh == 0 and q == 0),
                                   stop=(hh == 15 and q == 1))
            return ins
        OP("t", e, reads=["BE", "CTX"], writes=["P1"])
        for b2 in (2, 3):
            def e(t, b2=b2):
                for hh in range(8):
                    h = 8 * (b2 - 2) + hh
                    for q in range(2):
                        kc = 2 * h + q
                        ins = t.matmul(P[b2].ap()[:, 64 * hh:64 * (hh + 1)],
                                       e32[:, kc, :], ctx3[:, kc, 0:64],
                                       start=(hh == 0 and q == 0),
                                       stop=(hh == 7 and q == 1))
                return ins
            OP("t", e, reads=["BE", "CTX"], writes=[f"P{b2}"])
        OP("s", lambda s: s.copy(TMP32[:, :], P[1].ap()[:, 0:32]),
           reads=["P1"], writes=["P1c"])

    def emit_tail(st, j):
        p1s = TMP32.ap().rearrange("p (a b) -> p a b", a=16)
        for b2 in (2, 3):
            hb = 8 * (b2 - 2)
            OP("v", lambda v, hb=hb: v.scalar_tensor_tensor(
                DEN.ap()[:, hb:hb + 8],
                p1s[:, hb:hb + 8, 1], BCAST.ap()[:, 1:2],
                p1s[:, hb:hb + 8, 0], op0=ALU.mult, op1=ALU.add),
               reads=["P1c", "BCASTw"], writes=["DENw"])
            OP("v", lambda v, hb=hb: v.tensor_scalar(
                RBD.ap()[:, hb:hb + 8], p1s[:, hb:hb + 8, 1],
                BCAST.ap()[:, 0:1], None, op0=ALU.mult),
               reads=["P1c", "BCASTw"], writes=["RBDw"])
        for b2 in (2, 3):
            hb = 8 * (b2 - 2)
            OP("v", lambda v, hb=hb: v.reciprocal(
                DINV.ap()[:, hb:hb + 8], DEN.ap()[:, hb:hb + 8]),
               reads=["DENw"], writes=["DINVw"])
        for b2 in (2, 3):
            hb = 8 * (b2 - 2)
            OP("v", lambda v, hb=hb: v.tensor_tensor(
                RBD.ap()[:, hb:hb + 8], RBD.ap()[:, hb:hb + 8],
                DINV.ap()[:, hb:hb + 8], op=ALU.mult),
               reads=["DINVw"], writes=["RBDw"])
        for b2 in (2, 3):
            hb = 8 * (b2 - 2)
            ew = ["BF"]
            if st == 0 and j == 0 and b2 == 2:
                ew += [f"kp1_{p}" for p in range(8)]
            OP("v", lambda v, hb=hb, b2=b2: v.tensor_tensor(
                atm[:, 512 * (b2 - 2):512 * (b2 - 1)].rearrange(
                    "p (a d) -> p a d", a=8),
                P[b2].ap().rearrange("p (a d) -> p a d", a=8),
                DINV.ap()[:, hb:hb + 8].unsqueeze(2).broadcast_to([128, 8, 64]),
                op=ALU.mult),
               reads=[f"P{b2}", "DINVw", "BF"], writes=ew)
        for b2 in (2, 3):
            hb = 8 * (b2 - 2)
            t2off = 1024 if b2 == 2 else 512
            OP("v", lambda v, hb=hb, b2=b2, t2off=t2off: v.tensor_tensor(
                BH.ap()[:, t2off:t2off + 512].rearrange(
                    "p (a d) -> p a d", a=8),
                SVREP.ap()[:, 512 * (b2 - 2):512 * (b2 - 1)].rearrange(
                    "p (a d) -> p a d", a=8),
                RBD.ap()[:, hb:hb + 8].unsqueeze(2).broadcast_to([128, 8, 64]),
                op=ALU.mult),
               reads=["SVREPw", "RBDw"], writes=["T2w", "qsq1"])
        for b2 in (2, 3):
            t2off = 1024 if b2 == 2 else 512
            OP("v", lambda v, b2=b2, t2off=t2off: v.tensor_tensor(
                atm[:, 512 * (b2 - 2):512 * (b2 - 1)],
                atm[:, 512 * (b2 - 2):512 * (b2 - 1)],
                BH.ap()[:, t2off:t2off + 512],
                op=ALU.add),
               reads=["T2w", "BF"], writes=["BF"])
        vgap()
        vgap("CSLK")
        for g2 in range(2):
            b = g2
            def e(t, g2=g2, b=b):
                for q in range(4):
                    c = 4 * g2 + q
                    ins = t.matmul(P[b].ap()[:, 128 * q:128 * (q + 1)],
                                   atm[:, 128 * c:128 * (c + 1)], IDENT[:, :],
                                   is_transpose=True, start=(q == 0), stop=(q == 3))
                return ins
            OP("t", e, reads=["BF", "IDENT", "CSLK"], writes=[f"P{b}"])
            OP("s", lambda s, g2=g2, b=b: s.copy(
                afm[:, 4 * g2:4 * (g2 + 1), :].rearrange("p a c -> p (a c)"),
                P[b].ap()[:, :]),
               reads=[f"P{b}", "BF"], writes=["BF"])
        for h in range(2):
            b = 2 + h
            def e(t, h=h, b=b):
                for ki in range(8):
                    ins = t.matmul(P[b].ap()[:, :], afm[:, ki, :],
                                   WB.ap()[:, ki, 512 * h:512 * (h + 1)],
                                   start=(ki == 0), stop=(ki == 7))
                return ins
            OP("t", e, reads=["BF", "WB"], writes=[f"P{b}"])
        for h in range(2):
            b = 2 + h
            OP("v", lambda v, j=j, h=h, b=b: v.tensor_tensor(
                x2v[:, j % 2, 512 * h:512 * (h + 1)],
                x2v[:, (j + 1) % 2, 512 * h:512 * (h + 1)],
                P[b].ap()[:, :], op=ALU.add),
               reads=[f"P{b}", "BF", f"xbo{j}"], writes=["BF", f"x2v{j % 2}"])
        vgap()
        vgap("CSLK2")
        def e(g, j=j, st=st):
            return g.dma_start(
                out=(y_h if stage == "x2" else x2s_h)[
                    st * TST + 128 * j:st * TST + 128 * (j + 1), :],
                in_=x2v[:, j % 2, :]).then_inc(d_g, 16)
        OP("g", e, reads=["BF"], writes=["x2st"], ndma=1)
        idx = ST4(st) + j
        vgap()
        OP("v", lambda v, j=j, idx=idx: v.reduce_sum(
            LN2M.ap()[:, idx:idx + 1], x2v[:, j % 2, :], axis=AX.X),
           reads=["BF"], writes=["LN2M"])
        OP("s", lambda s, j=j: s.activation(
            BH.ap()[:, 0:1024], x2v[:, j % 2, :], AF.Square),
           reads=["BF"], writes=["qsq0", "qsq1", "ksq0", "ksq1"])
        OP("v", lambda v, j=j: v.reduce_sum(
            SS4.ap()[:, j:j + 1], BH.ap()[:, 0:1024], axis=AX.X),
           reads=["qsq0"], writes=["SS4"])
        OP("s", lambda s, idx=idx: s.activation(
            LN2M.ap()[:, idx:idx + 1], LN2M.ap()[:, idx:idx + 1], AF.Copy,
            scale=1.0 / D),
           reads=[], writes=["LN2M"])
        OP("s", lambda s, idx=idx: s.activation(
            TMP4B.ap()[:, 0:1], LN2M.ap()[:, idx:idx + 1], AF.Square),
           reads=["LN2M"], writes=["TMP4B"])
        OP("s", lambda s: s.activation(
            TMP4B.ap()[:, 1:2], TMP4B.ap()[:, 0:1], AF.Copy,
            scale=-1.0, bias=LNEPS),
           reads=["CONSTS"], writes=["TMP4B"])
        OP("s", lambda s, j=j: s.activation(
            TMP4B.ap()[:, 2:3], SS4.ap()[:, j:j + 1], AF.Ln, scale=1.0 / D,
            bias=TMP4B.ap()[:, 1:2]),
           reads=["SS4"], writes=["TMP4B"])
        OP("s", lambda s, idx=idx: s.activation(
            LN2I.ap()[:, idx:idx + 1], TMP4B.ap()[:, 2:3], AF.Exp, scale=-0.5),
           reads=["TMP4B"], writes=[f"LN2I{j}", "LN2I"])

    def emit_xbo(st, j):
        OP("g", lambda g, j=j: g.tensor_tensor(
            x2v[:, (j + 1) % 2, :], BA.ap()[:, j, :], BOREP[:, :], op=ALU.add),
           reads=[f"BAc{j}", "BOREP", "BF"],
           writes=["BF", f"x2v{(j + 1) % 2}", f"xbo{j}", f"BAg{j}"])

    def emit_mm1(st, j):
        idx = ST4(st) + j
        for h in range(2):
            OP("t", lambda t, st=st, j=j, h=h, idx=idx: t.matmul(
                P[4 + h].ap()[0:1, :], LN2I.ap()[:, idx:idx + 1],
                x2v[:, j % 2, 512 * h:512 * (h + 1)],
                start=(st == 0 and j == 0), stop=False),
               reads=[f"LN2I{j}", f"x2v{j % 2}", "CSLK2"], writes=[f"P{4 + h}"])

    def emit_chead(st):
        emit_chead_pre(st)
        emit_chead_post(st)

    emit_chead_pre(0)
    ld("WA", WA.ap()[:, :, :], wq_h.rearrange("(a p) d -> p a d", p=128))
    ld("WB", WB.ap()[:, :, :], wo_h.rearrange("(a p) d -> p a d", p=128))
    emit_bcast()
    emit_chead_post(0)
    emit_qde(0, 0)
    for st in range(NST):
        for j in range(NSUB):
            emit_odc(st, j)
            if j > 0:
                emit_mm1(st, j - 1)
            emit_xbo(st, j)
            if j < NSUB - 1:
                emit_qde(st, j + 1)
            elif st + 1 < NST:
                emit_chead(st + 1)
                emit_qde(st + 1, 0)
            emit_tail(st, j)
        emit_mm1(st, NSUB - 1)

    if stage == "x2":
        def fin(y_eng):
            y_eng.wait_ge(d_g, bld.cnt["DG"])
            y_eng.wait_ge(d_y, bld.cnt["DY"])
        return nc, bld, SEM, es, fin

    def e(v):
        for ki in range(2):
            v.tensor_copy(W4R.ap()[:, ki, :],
                          W4S.ap()[:, ki, 0:1].broadcast_to([128, 128]))
            ins = v.tensor_copy(GW3R.ap()[:, ki, :],
                                GW3.ap()[:, ki, 0:1].broadcast_to([128, 128]))
        return ins
    OP("v", e, reads=["W4S", "GW3"], writes=["W4R", "GW3R"], dup=True)

    def e(t):
        for j in range(32):
            ins = t.matmul(P[0].ap()[0:1, 0:1], LN2M.ap()[:, j:j + 1],
                           LN2I.ap()[:, j:j + 1], start=(j == 0), stop=(j == 31))
        return ins
    OP("t", e, reads=["LN2M", "LN2I"], writes=["P0"], dup=True)
    OP("s", lambda s: s.copy(ROWS.ap()[0:1, R_S2:R_S2 + 1], P[0].ap()[0:1, 0:1]),
       reads=["P0"], writes=["S2col"], dup=True)
    for h in range(2):
        OP("t", lambda t, h=h: t.matmul(
            P[4 + h].ap()[0:1, :], ROWS.ap()[0:1, R_S2:R_S2 + 1],
            NEG32.ap()[0:1, :], start=False, stop=True),
           reads=["S2col", "NEG32"], writes=[f"P{4 + h}"])
    for h in range(2):
        OP("s", lambda s, h=h: s.activation(
            ARENA.ap()[0:1, 512 * h:512 * (h + 1)], P[4 + h].ap()[0:1, :],
            AF.Copy, scale=1.0 / NTOK),
           reads=[f"P{4 + h}"], writes=["GM", "SVROW", "CSUM1"], dup=True)
    def e(t):
        for q in range(8):
            ins = t.matmul(P[1].ap()[:, q:q + 1],
                           ARENA.ap()[0:1, 128 * q:128 * (q + 1)],
                           IDENT[0:1, 0:1], is_transpose=True,
                           start=(q == 0), stop=(q == 7))
        return ins
    OP("t", e, reads=["GM", "IDENT"], writes=["P1"], dup=True)
    OP("s", lambda s: s.copy(CSFM[:, :], P[1].ap()[:, 0:8]),
       reads=["P1"], writes=["CSFMw", "CSFMacc"], dup=True)

    ld("WA", WA.ap()[:, :, :], w1_h.rearrange("(a p) d -> p a d", p=128))
    ld("WB2", WB.ap()[:, :, 0:512], w2_h.rearrange("(a p) d -> p a d", p=128),
       extra_writes=["WB"])
    ld("WBg", WB.ap()[:, :, 512:1024], g1_h.rearrange("(a p) d -> p a d", p=128),
       extra_writes=["WB"])

    def e(s):
        return s.add_instruction(mybir.InstLoadActFuncSet(
            name=nc.get_next_instruction_name(), act_func_set_id=10, ins=[], outs=[]))
    OP("s", e)

    def e(t):
        for o in range(4):
            for ki in range(8):
                ins = t.matmul(P[2].ap()[:, o:o + 1],
                               WB.ap()[:, ki, 512 + 128 * o:512 + 128 * (o + 1)],
                               CSFM.ap()[:, ki:ki + 1],
                               start=(o == 0 and ki == 0), stop=(o == 3 and ki == 7))
        return ins
    OP("t", e, reads=["CSFMw", "WBg"], writes=["P2"], dup=True)
    for o in range(4):
        OP("s", lambda s, o=o: s.activation(GH1.ap()[:, o:o + 1], P[2].ap()[:, o:o + 1],
                                            AF.Gelu, bias=GB1.ap()[:, o:o + 1]),
           reads=["P2", "GB1"], writes=["GH1w"], dup=True)
    def e(t):
        for o in range(2):
            for ki in range(4):
                ins = t.matmul(P[3].ap()[:, o:o + 1],
                               W3S.ap()[:, ki, 128 * o:128 * (o + 1)],
                               GH1.ap()[:, ki:ki + 1],
                               start=(o == 0 and ki == 0), stop=(o == 1 and ki == 3))
        return ins
    OP("t", e, reads=["GH1w", "GW2"], writes=["P3"], dup=True)
    for o in range(2):
        OP("s", lambda s, o=o: s.activation(GH2.ap()[:, o:o + 1], P[3].ap()[:, o:o + 1],
                                            AF.Gelu, bias=GB2.ap()[:, o:o + 1]),
           reads=["P3", "GB2"], writes=["GH2w"], dup=True)
    def e(t):
        for ki in range(2):
            ins = t.matmul(P[4].ap()[:, 0:1], GW3R.ap()[:, ki, :],
                           GH2.ap()[:, ki:ki + 1], start=(ki == 0), stop=(ki == 1))
        return ins
    OP("t", e, reads=["GH2w", "GW3R"], writes=["P4"], dup=True)
    OP("s", lambda s: s.activation(TGC.ap()[:, :], P[4].ap()[:, 0:1], AF.Tanh,
                                   scale=0.5, bias=B3HC.ap()[:, 0:1]),
       reads=["P4", "B3C"], writes=["TGc"], dup=True)
    OP("t", lambda t: t.matmul(P[5].ap()[0:1, 0:128], ARCOL.ap()[0:1, 0:1],
                               ONES1.ap()[0:1, :], start=True, stop=True),
       reads=["rar", "ONES"], writes=["P5"], dup=True)
    OP("s", lambda s: s.copy(ROWS.ap()[0:1, R_UR:R_UR + 128],
                             P[5].ap()[0:1, 0:128]),
       reads=["P5"], writes=["UROW"], dup=True)
    OP("t", lambda t: t.matmul(P[5].ap()[:, 0:1], ROWS.ap()[0:1, R_UR:R_UR + 128],
                               TGC.ap()[0:1, 0:1], start=True, stop=True),
       reads=["UROW", "TGc"], writes=["P5"], dup=True)
    OP("s", lambda s: s.activation(CB2C.ap()[:, :], P[5].ap()[:, 0:1], AF.Copy,
                                   scale=0.7, bias=0.1),
       reads=["P5", "CONSTS"], writes=["CB2c"], dup=True)
    OP("s", lambda s: s.activation(A2C.ap()[:, :], ARCOL.ap()[:, :], AF.Copy,
                                   scale=0.3),
       reads=["rar"], writes=["A2c"], dup=True)

    ld("W3S", W3S.ap()[:, :, :], w3_h.rearrange("(a p) o -> p a o", p=128),
       extra_writes=["GW2"])

    be4 = BE.ap().rearrange("p (j d) -> p j d", j=4)
    h2v = BE.ap()[:, 0:2048].rearrange("p (a d) -> p a d", a=4)
    h3v = BE.ap()[:, 2048:3072].rearrange("p (a d) -> p a d", a=2)
    crrep = BE.ap()[:, 3072:3584]
    bd2 = BD.ap().rearrange("p a d -> p (a d)")

    for st in range(NST):
        xres = "BA" if st % 2 == 0 else "BF"
        xv4 = (BA.ap()[:, :, :] if st % 2 == 0
               else BF.ap().rearrange("p (j d) -> p j d", j=4))
        ld(xres, xv4[:, :, :],
           x2s_h[st * TST:(st + 1) * TST, :].rearrange("(j p) d -> p j d", p=128),
           reads=["x2st"],
           extra_writes=(["BAc0", "BAc1", "BAc2", "BAc3"] if st == 0 else []))
        for j in range(NSUB):
            OP("v", lambda v, j=j, st=st, xv4=xv4: v.tensor_scalar(
                be4[:, j, :], xv4[:, j, :],
                LN2M.ap()[:, ST4(st) + j:ST4(st) + j + 1],
                LN2I.ap()[:, ST4(st) + j:ST4(st) + j + 1],
                op0=ALU.subtract, op1=ALU.mult),
               reads=[xres, "LN2M", "LN2I"], writes=["BE"])
        vgap()
        vgap("VSLK")
        for c in range(8):
            b = c % 2
            def e(t, c=c, b=b):
                for j in range(NSUB):
                    ins = t.matmul(P[b].ap()[:, 128 * j:128 * (j + 1)],
                                   be4[:, j, 128 * c:128 * (c + 1)], IDENT[:, :],
                                   is_transpose=True, start=(j == 0), stop=(j == NSUB - 1))
                return ins
            OP("t", e, reads=["BE", "IDENT", "VSLK"], writes=[f"P{b}"])
            OP("s", lambda s, c=c, b=b: s.copy(BB.ap()[:, c, :], P[b].ap()[:, :]),
               reads=[f"P{b}"], writes=["BB"])
        cur, curres = BB, "BB"
        for cyc in range(NCYC):
            nxt = BC if cur is BB else BB
            nxtres = "BC" if curres == "BB" else "BB"
            curv = cur.ap() if cur is BB else cur.ap().rearrange("p (a d) -> p a d", a=8)
            nxtv = nxt.ap() if nxt is BB else nxt.ap().rearrange("p (a d) -> p a d", a=8)
            for o in range(8):
                b = 2 + (o % 2)
                if cyc == 0:
                    def e(t, o=o, b=b, curv=curv):
                        for ki in range(8):
                            ins = t.matmul(P[b].ap()[:, :],
                                           WA.ap()[:, ki, 128 * o:128 * (o + 1)],
                                           curv[:, ki, :],
                                           start=(ki == 0), stop=(ki == 7))
                        return ins
                    OP("t", e, reads=[curres, "WA", "VSLK"], writes=[f"P{b}"])
                else:
                    for ki in range(8):
                        slk = (f"{curres}a{ki + 2}" if ki <= 5 else "VSLK")
                        OP("t", lambda t, o=o, ki=ki, b=b, curv=curv: t.matmul(
                            P[b].ap()[:, :],
                            WA.ap()[:, ki, 128 * o:128 * (o + 1)],
                            curv[:, ki, :], start=(ki == 0), stop=(ki == 7)),
                           reads=[slk, "WA"], writes=[f"P{b}"])
                OP("s", lambda s, o=o, b=b: s.activation(
                    BD.ap()[:, o, :], P[b].ap()[:, :], AF.Gelu,
                    bias=B1S.ap()[:, o:o + 1]),
                   reads=[f"P{b}", "B1S"], writes=["BD"])
            for o in range(4):
                b = 4 + (o % 2)
                def e(t, o=o, b=b):
                    for ki in range(8):
                        ins = t.matmul(P[b].ap()[:, :],
                                       WB.ap()[:, ki, 128 * o:128 * (o + 1)],
                                       BD.ap()[:, ki, :], start=(ki == 0), stop=(ki == 7))
                    return ins
                OP("t", e, reads=["BD", "WB2"], writes=[f"P{b}"])
                OP("s", lambda s, o=o, b=b: s.activation(
                    h2v[:, o, :], P[b].ap()[:, :], AF.Gelu,
                    bias=B2S.ap()[:, o:o + 1]),
                   reads=[f"P{b}", "B2S"], writes=["BE"])
            for o in range(2):
                b3k = 6 if o == 0 else 0
                def e(t, o=o, b3k=b3k):
                    for ki in range(4):
                        ins = t.matmul(P[b3k].ap()[:, :],
                                       W3S.ap()[:, ki, 128 * o:128 * (o + 1)],
                                       h2v[:, ki, :], start=(ki == 0), stop=(ki == 3))
                    return ins
                OP("t", e, reads=["BE", "W3S"], writes=[f"P{b3k}"])
                OP("s", lambda s, o=o, b3k=b3k: s.activation(
                    h3v[:, o, :], P[b3k].ap()[:, :], AF.Gelu,
                    bias=B3S.ap()[:, o:o + 1]),
                   reads=[f"P{b3k}", "B3S"], writes=["BE"])
            def e(t):
                for ki in range(2):
                    ins = t.matmul(P[7].ap()[:, :], W4R.ap()[:, ki, :],
                                   h3v[:, ki, :], start=(ki == 0), stop=(ki == 1))
                return ins
            OP("t", e, reads=["BE", "W4R"], writes=["P7"])
            OP("s", lambda s: s.activation(crrep, P[7].ap()[:, :], AF.Tanh,
                                           scale=0.5, bias=B4HC.ap()[:, 0:1]),
               reads=["P7", "B4C"], writes=["BE"])
            OP("v", lambda v: v.tensor_scalar(crrep, crrep, A2C.ap()[:, 0:1],
                                              CB2C.ap()[:, 0:1],
                                              op0=ALU.mult, op1=ALU.add),
               reads=["BE", "A2c", "CB2c"], writes=["BE"])
            vgap()
            vgap()
            def em(c, curv=curv, nxtv=nxtv):
                OP("v", lambda v, c=c, curv=curv, nxtv=nxtv: v.tensor_tensor(
                    nxtv[:, c, :], crrep, curv[:, SIGMA[c], :], op=ALU.mult),
                   reads=[curres, "BE"], writes=[f"{nxtres}m{c}"])
            def ea(c, curv=curv, nxtv=nxtv):
                OP("v", lambda v, c=c, curv=curv, nxtv=nxtv: v.tensor_tensor(
                    nxtv[:, c, :], curv[:, c, :], nxtv[:, c, :],
                    op=(ALU.add if SIGN[c] > 0 else ALU.subtract)),
                   reads=[curres, f"{nxtres}m{c}"],
                   writes=[f"{nxtres}a{c}", nxtres])
            em(0)
            em(1)
            for c in range(2, 8):
                ea(c - 2)
                em(c)
            ea(6)
            ea(7)
            vgap()
            vgap()
            vgap("VSLK")
            cur, curres = nxt, nxtres
        curv = cur.ap() if cur is BB else cur.ap().rearrange("p (a d) -> p a d", a=8)
        for j in range(NSUB):
            for g2 in range(2):
                b = g2
                def e(t, j=j, g2=g2, b=b, curv=curv):
                    for q in range(4):
                        c = 4 * g2 + q
                        ins = t.matmul(P[b].ap()[:, 128 * q:128 * (q + 1)],
                                       curv[:, c, 128 * j:128 * (j + 1)], IDENT[:, :],
                                       is_transpose=True, start=(q == 0), stop=(q == 3))
                    return ins
                OP("t", e, reads=[curres, "IDENT", "VSLK"], writes=[f"P{b}"])
                OP("v", lambda v, j=j, g2=g2, b=b, xv4=xv4: v.tensor_tensor(
                    bd2[:, 1024 * (j % 2) + 512 * g2:1024 * (j % 2) + 512 * (g2 + 1)],
                    xv4[:, j, 512 * g2:512 * (g2 + 1)],
                    P[b].ap()[:, :], op=ALU.add),
                   reads=[f"P{b}", xres], writes=["BD"])
            def e(g, j=j, st=st):
                return g.dma_start(
                    out=y_h[st * TST + 128 * j:st * TST + 128 * (j + 1), :],
                    in_=bd2[:, 1024 * (j % 2):1024 * (j % 2 + 1)]).then_inc(d_g, 16)
            OP("g", e, reads=["BD"], writes=["yst"], ndma=1)

    def e(g):
        g.dma_start(out=ydbg_h[0:1, 0:1024], in_=ARENA.ap()[0:1, :]).then_inc(d_g, 16)
        g.dma_start(out=ydbg_h[0:1, 1024:1536], in_=ROWS.ap()[0:1, :]).then_inc(d_g, 16)
        g.dma_start(out=ydbg_h[0:1, 1536:2048],
                    in_=BE.ap()[0:1, 3072:3584]).then_inc(d_g, 16)
        g.dma_start(out=ydbg_h[0:1, 2048:2056],
                    in_=CSFM.ap()[0:1, 0:8]).then_inc(d_g, 16)
        g.dma_start(out=ydbg_h[0:1, 2056:2057],
                    in_=TGC.ap()[0:1, 0:1]).then_inc(d_g, 16)
        g.dma_start(out=ydbg_h[0:1, 2057:2058],
                    in_=A2C.ap()[0:1, 0:1]).then_inc(d_g, 16)
        g.dma_start(out=ydbg_h[0:1, 2058:2059],
                    in_=CB2C.ap()[0:1, 0:1]).then_inc(d_g, 16)
        g.dma_start(out=ydbg_h[0:1, 2059:2063],
                    in_=GH1.ap()[0:1, 0:4]).then_inc(d_g, 16)
        return g.dma_start(out=ydbg_h[0:1, 2063:2065],
                           in_=GH2.ap()[0:1, 0:2]).then_inc(d_g, 16)
    OP("g", e, reads=["GM", "TGc", "CB2c", "A2c", "BE", "CSFMw", "GH1w", "GH2w"],
       writes=["ydbg"], ndma=9)
    if DBG:
        def e(g):
            g.dma_start(out=ydbg2_h[:, 0:4], in_=GH1.ap()[:, :],
                        allow_slow_non_contiguous=True).then_inc(d_g, 16)
            g.dma_start(out=ydbg2_h[:, 4:6], in_=GH2.ap()[:, :],
                        allow_slow_non_contiguous=True).then_inc(d_g, 16)
            g.dma_start(out=ydbg2_h[:, 6:7], in_=TGC.ap()[:, :],
                        allow_slow_non_contiguous=True).then_inc(d_g, 16)
            g.dma_start(out=ydbg2_h[:, 7:8], in_=CB2C.ap()[:, :],
                        allow_slow_non_contiguous=True).then_inc(d_g, 16)
            g.dma_start(out=ydbg2_h[:, 8:9], in_=GW3R.ap()[:, 0, 0:1],
                        allow_slow_non_contiguous=True).then_inc(d_g, 16)
            g.dma_start(out=ydbg2_h[:, 9:10], in_=GW3R.ap()[:, 1, 0:1],
                        allow_slow_non_contiguous=True).then_inc(d_g, 16)
            g.dma_start(out=ydbg2_h[:, 10:11], in_=GW3R.ap()[:, 0, 127:128],
                        allow_slow_non_contiguous=True).then_inc(d_g, 16)
            g.dma_start(out=ydbg2_h[:, 11:12], in_=W4R.ap()[:, 0, 0:1],
                        allow_slow_non_contiguous=True).then_inc(d_g, 16)
            g.dma_start(out=ydbg2_h[:, 12:13], in_=W4R.ap()[:, 1, 0:1],
                        allow_slow_non_contiguous=True).then_inc(d_g, 16)
            g.dma_start(out=ydbg2_h[:, 13:14], in_=GW3.ap()[:, 0, 0:1],
                        allow_slow_non_contiguous=True).then_inc(d_g, 16)
            g.dma_start(out=ydbg2_h[:, 14:15], in_=GW3.ap()[:, 1, 0:1],
                        allow_slow_non_contiguous=True).then_inc(d_g, 16)
            return g.dma_start(out=ydbg2_h[:, 15:16], in_=W4S.ap()[:, 0, 0:1],
                               allow_slow_non_contiguous=True).then_inc(d_g, 16)
        OP("g", e, reads=["GM", "TGc", "CB2c", "A2c", "GH1w", "GH2w",
                          "W4R", "GW3R", "W4S", "GW3"],
           writes=["ydbg2"], ndma=12)

    def fin(y_eng):
        y_eng.wait_ge(d_g, bld.cnt["DG"])
        y_eng.wait_ge(d_y, bld.cnt["DY"])
    return nc, bld, SEM, es, fin


def emit(nc, bld, SEM, es, fin, nops=None):
    own = dict(t="T", v="V", s="S", g="G")
    ops = bld.ops if nops is None else bld.ops[:nops]
    if nops is not None:
        pc = dict(DY=0, DG=0)
        for (_e, _w2, _f, key, amt, _r, _w3) in ops:
            if key in pc:
                pc[key] += amt
        def fin(y_eng, _pc=pc):
            y_eng.wait_ge(SEM["DG"], _pc["DG"])
            y_eng.wait_ge(SEM["DY"], _pc["DY"])
    with es:
        with nc.Block() as block:
            def run(eng_name):
                def body(eng):
                    hw = {}
                    for (e, waits, emit_fn, key, amt, _r, _w) in ops:
                        if e != eng_name:
                            continue
                        for k, v in sorted(waits.items()):
                            if k == own.get(eng_name):
                                continue
                            if hw.get(k, 0) >= v:
                                continue
                            eng.wait_ge(SEM[k], v)
                            hw[k] = v
                        last = emit_fn(eng)
                        if key not in ("DY", "DG"):
                            last.then_inc(SEM[key], amt)
                    if eng_name == "y":
                        fin(eng)
                return body
            block.sync(run("y"))
            block.tensor(run("t"))
            block.vector(run("v"))
            block.scalar(run("s"))
            block.gpsimd(run("g"))
    return nc


NOPS = None
LAST_RESULTS = None

def kernel(**inputs):
    ncores = 8
    nc, bld, SEM, es, fin = build_nc(STAGE)
    emit(nc, bld, SEM, es, fin, nops=NOPS)

    f32 = np.float32
    ca = lambda a: np.ascontiguousarray(np.asarray(a, dtype=f32))
    x = ca(inputs["L0_attn_x"])
    shared = {
        "wq": ca(inputs["L0_attn_wq"]), "wk": ca(inputs["L0_attn_wk"]),
        "wv": ca(inputs["L0_attn_wv"]), "wo": ca(inputs["L0_attn_wo"]),
        "bo": ca(inputs["L0_attn_bo"]).reshape(1, D),
        "proj": ca(inputs["L0_attn_proj"]),
        "w1": ca(inputs["L0_ffn_cn_w1"]), "b1": ca(inputs["L0_ffn_cn_b1"]),
        "w2": ca(inputs["L0_ffn_cn_w2"]), "b2": ca(inputs["L0_ffn_cn_b2"]),
        "w3": ca(inputs["L0_ffn_cn_w3"]), "b3": ca(inputs["L0_ffn_cn_b3"]),
        "w4": ca(inputs["L0_ffn_cn_w4"]),
        "b4": ca(inputs["L0_ffn_cn_b4"]).reshape(1, 1),
        "gw1": ca(inputs["L0_ffn_gc_w1"]), "gb1": ca(inputs["L0_ffn_gc_b1"]),
        "gw2": ca(inputs["L0_ffn_gc_w2"]), "gb2": ca(inputs["L0_ffn_gc_b2"]),
        "gw3": ca(inputs["L0_ffn_gc_w3"]),
        "gb3": ca(inputs["L0_ffn_gc_b3"]).reshape(1, 1),
        "ar": ca(inputs["L0_ffn_adaptive_range"]).reshape(1, 1),
    }
    in_maps = [dict(shared, x=np.ascontiguousarray(x[i])) for i in range(ncores)]
    res = run_bass_kernel_spmd(nc, in_maps, core_ids=list(range(ncores)))
    global LAST_RESULTS
    LAST_RESULTS = res.results
    out = np.stack([np.asarray(res.results[i]["y"]) for i in range(ncores)], axis=0)
    return out.astype(np.float32)
